# version 12
# baseline (speedup 1.0000x reference)
import numpy as np
import ml_dtypes
import concourse.bass as bass
import concourse.mybir as mybir
from concourse.bass_utils import run_bass_kernel_spmd
from contextlib import ExitStack

F32 = mybir.dt.float32
BF16 = mybir.dt.bfloat16
AF = mybir.ActivationFunctionType
ALU = mybir.AluOpType
AX = mybir.AxisListType

NCORES = 8
R = 2048
D = 1024
NIC = D // 128
G = 512
NG = R // G
NT = R // 128
DBG = {"gu": True, "down": True, "evac": True}


def f32c(a):
    return np.ascontiguousarray(np.asarray(a, dtype=np.float32))


class Plan:
    ENGS = ("pe", "act", "dve", "gp", "sy")

    def __init__(self):
        self.stages = {e: [] for e in self.ENGS}

    def __init2__(self):
        pass

    def add(self, eng, fn, deps=(), inc=1):
        cnt = getattr(self, "_cnt", None)
        if cnt is None:
            self._cnt = {e: 0 for e in self.ENGS}
            cnt = self._cnt
        self.stages[eng].append((fn, [d for d in deps if d is not None], inc))
        self._pending = getattr(self, "_pending", {})
        raise RuntimeError("use add_n")

    def add_n(self, eng, fn, deps=(), inc_total=1, chan=None):
        if chan is None:
            chan = eng
        if not hasattr(self, "_cnt"):
            self._cnt = {}
        self._cnt.setdefault(chan, 0)
        self.stages[eng].append((fn, [d for d in deps if d is not None], inc_total, chan))
        self._cnt[chan] += inc_total
        return (chan, self._cnt[chan])

    def emit(self, nc, c, tag):
        chans = {ch for e in self.ENGS for (_, _, _, ch) in self.stages[e]}
        for e in self.ENGS:
            for (_, deps, _, _) in self.stages[e]:
                for (ch, _) in deps:
                    chans.add(ch)
        sems = {ch: c.enter_context(nc.semaphore(f"{tag}_{ch}")) for ch in chans}
        attr = {"pe": "tensor", "act": "scalar", "dve": "vector",
                "gp": "gpsimd", "sy": "sync"}
        with nc.Block() as blk:
            for e in self.ENGS:
                if not self.stages[e]:
                    continue
                def mk(e):
                    def body(eng):
                        for fn, deps, inc_total, ch in self.stages[e]:
                            for (pc_, pn) in deps:
                                eng.wait_ge(sems[pc_], pn)
                            if inc_total == 1:
                                ins = fn(eng)
                                ins.then_inc(sems[ch], 1)
                            else:
                                fn(eng, sems[ch])
                    return body
                getattr(blk, attr[e])(mk(e))


class VSync:
    def __init__(self, c, nc, name="vsync"):
        self.h = c.enter_context(nc.semaphore(name))
        self.n = 0

    def __call__(self, eng, ins):
        self.n += 1
        ins.then_inc(self.h, 1)
        eng.wait_ge(self.h, self.n)
        return ins



def emit_ln_rows(nc, semc, vsn, x_sb, ones_col, rows_m, rows_s, rows_c, kind,
                 eps1_col=None, eps2_col=None, nic=NIC, xcols=R, name="",
                 rows_s1=None, rows_c1=None, width=None, xoff=0, shared=None):
    with ExitStack() as c:
        p_sx = None if kind == 'rms' else c.enter_context(nc.psum_tensor("p_sx" + name, [128, G], F32))
        p_sq = c.enter_context(nc.psum_tensor("p_sq" + name, [128, G], F32))
        sqs = [c.enter_context(nc.sbuf_tensor(f"lnsq{i}" + name, [128, G], F32)) for i in range(2)]
        rt1 = c.enter_context(nc.sbuf_tensor("lnrt1" + name, [1, G], F32))
        rt2 = c.enter_context(nc.sbuf_tensor("lnrt2" + name, [1, G], F32))
        width = float(nic * 128) if width is None else float(width)
        if shared is None:
            sqd = semc.enter_context(nc.semaphore("lnsqd" + name))
            mmd = semc.enter_context(nc.semaphore("lnmmd" + name))
            b_sq = b_mm = 0
        else:
            sqd, mmd = shared["sqd"], shared["mmd"]
            b_sq, b_mm = shared["n_sq"], shared["n_mm"]
        for g in range(NG):
            gs = slice(g * G, (g + 1) * G)
            if True:
                b0 = b_sq + g * nic
                b1 = b_mm + g * nic
                with nc.Block() as blk:
                    @blk.scalar
                    def _(s):
                        for ic in range(nic):
                            s.activation(sqs[ic % 2][:, :],
                                         x_sb[:, xoff + ic * xcols + g * G: xoff + ic * xcols + (g + 1) * G],
                                         AF.Square).then_inc(sqd, 1)
                            if ic >= 1:
                                s.wait_ge(mmd, b1 + ic)
                    @blk.tensor
                    def _(t):
                        if kind != 'rms':
                            for ic in range(nic):
                                t.matmul(p_sx[0:1, :], ones_col[:, :],
                                         x_sb[:, xoff + ic * xcols + g * G: xoff + ic * xcols + (g + 1) * G],
                                         start=(ic == 0), stop=(ic == nic - 1))
                        for ic in range(nic):
                            t.wait_ge(sqd, b0 + ic + 1)
                            t.matmul(p_sq[0:1, :], ones_col[:, :], sqs[ic % 2][:, :],
                                     start=(ic == 0), stop=(ic == nic - 1)).then_inc(mmd, 1)
            with nc.Block() as blk:
                @blk.vector
                def _(v):
                    if kind == 'rms':
                        v.tensor_scalar(rt1[:, :], p_sq[0:1, :], 1.0 / width, None, ALU.mult)
                    else:
                        v.tensor_scalar(rows_m[:, gs], p_sx[0:1, :], 1.0 / width, None, ALU.mult)
                        v.tensor_tensor(rt2[:, :], rows_m[:, gs], rows_m[:, gs], ALU.mult)
                        v.scalar_tensor_tensor(rt1[:, :], p_sq[0:1, :], 1.0 / width, rt2[:, :],
                                               ALU.mult, ALU.subtract)
            with nc.Block() as blk:
                @blk.scalar
                def _(s):
                    s.activation(rt2[:, :], rt1[:, :], AF.Sqrt, bias=eps1_col[0:1, 0:1])
            with nc.Block() as blk:
                @blk.vector
                def _(v):
                    vsn(v, v.reciprocal(rows_s[:, gs], rt2[:, :]))
                    if kind == 'lnrms':
                        if rows_s1 is not None:
                            v.tensor_copy(rows_s1[:, gs], rows_s[:, gs])
                            v.scalar_tensor_tensor(rows_c1[:, gs], rows_m[:, gs], -1.0,
                                                   rows_s[:, gs], ALU.mult, ALU.mult)
                        v.tensor_tensor(rt2[:, :], rows_s[:, gs], rows_s[:, gs], ALU.mult)
                        v.tensor_tensor(rt2[:, :], rt1[:, :], rt2[:, :], ALU.mult)
            if kind == 'lnrms':
                with nc.Block() as blk:
                    @blk.scalar
                    def _(s):
                        s.activation(rt2[:, :], rt2[:, :], AF.Sqrt, bias=eps2_col[0:1, 0:1])
                with nc.Block() as blk:
                    @blk.vector
                    def _(v):
                        v.reciprocal(rt1[:, :], rt2[:, :])
                        v.tensor_tensor(rows_s[:, gs], rows_s[:, gs], rt1[:, :], ALU.mult)
            if kind != 'rms':
                with nc.Block() as blk:
                    @blk.vector
                    def _(v):
                        v.scalar_tensor_tensor(rows_c[:, gs], rows_m[:, gs], -1.0, rows_s[:, gs],
                                               ALU.mult, ALU.mult)
        if shared is not None:
            shared["n_sq"] = b_sq + NG * nic
            shared["n_mm"] = b_mm + NG * nic


def emit_ln_zn(nc, c, vsn, x_sb, zn_sb, ones_col, ones_row, kind,
               eps1_col=None, eps2_col=None, nic=NIC, xcols=R, zcols=R,
               gain_rowtile=None, gain_coltile=None, rms_gain=False,
               rows_s1=None, rows_c1=None, width=None, name="", ones_col_bf=None):
    width = float(nic * 128) if width is None else float(width)
    with ExitStack() as sbx:
        p_sx = None if kind == 'rms' else sbx.enter_context(nc.psum_tensor("lz_sx" + name, [128, G], F32))
        p_sq = sbx.enter_context(nc.psum_tensor("lz_sq" + name, [128, G], F32))
        pR = sbx.enter_context(nc.psum_tensor("lz_pR" + name, [128, G], F32))
        pC = None
        if not rms_gain:
            pC = [sbx.enter_context(nc.psum_tensor(f"lz_pC{i}" + name, [128, G], F32)) for i in range(2)]
        sqb = [sbx.enter_context(nc.sbuf_tensor(f"lz_sqb{i}" + name, [128, nic * G], BF16)) for i in range(2)]
        xbb = [sbx.enter_context(nc.sbuf_tensor(f"lz_xbb{i}" + name, [128, nic * G], BF16)) for i in range(2)]
        rt1 = sbx.enter_context(nc.sbuf_tensor("lz_rt1" + name, [1, NG * G], F32))
        rt2 = sbx.enter_context(nc.sbuf_tensor("lz_rt2" + name, [1, NG * G], F32))
        rS = sbx.enter_context(nc.sbuf_tensor("lz_rS" + name, [1, NG * G], F32))
        rC = sbx.enter_context(nc.sbuf_tensor("lz_rC" + name, [1, NG * G], F32))
        scr = sbx.enter_context(nc.sbuf_tensor("lz_scr" + name, [128, G], F32))
        plan = Plan()
        prev_summ = [None, None]
        prev_apply_last = None
        aplist = [None, None]
        for g in range(NG):
            gs = slice(g * G, (g + 1) * G)
            def mk_sqall(g=g):
                def f(s):
                    for ic in range(nic):
                        ins = s.activation(sqb[g % 2][:, ic * G:(ic + 1) * G],
                                           x_sb[:, ic * xcols + g * G: ic * xcols + (g + 1) * G],
                                           AF.Square)
                        if kind != 'rms':
                            ins = s.activation(xbb[g % 2][:, ic * G:(ic + 1) * G],
                                               x_sb[:, ic * xcols + g * G: ic * xcols + (g + 1) * G],
                                               AF.Copy)
                    return ins
                return f
            sq = plan.add_n("act", mk_sqall(), [prev_summ[g % 2]])
            def mk_summ(g=g):
                def f(t):
                    if kind != 'rms':
                        for ic in range(nic):
                            t.matmul(p_sx[0:1, :], ones_col_bf[:, :],
                                     xbb[g % 2][:, ic * G:(ic + 1) * G],
                                     start=(ic == 0), stop=(ic == nic - 1))
                    for ic in range(nic):
                        ins = t.matmul(p_sq[0:1, :], ones_col_bf[:, :], sqb[g % 2][:, ic * G:(ic + 1) * G],
                                       start=(ic == 0), stop=(ic == nic - 1))
                    return ins
                return f
            sdeps = [sq]
            if g >= 1:
                sdeps.append(rows1_ref)
            summ = plan.add_n("pe", mk_summ(), sdeps)
            prev_summ[g % 2] = summ
            def mk_rows1(g=g, gs=gs):
                def f(v):
                    if kind == 'rms':
                        return v.tensor_scalar(rt1[0:1, gs], p_sq[0:1, :], 1.0 / width, None, ALU.mult)
                    v.tensor_scalar(rt2[0:1, gs], p_sx[0:1, :], 1.0 / width, None, ALU.mult)
                    v.tensor_tensor(scr[0:1, 0:G], rt2[0:1, gs], rt2[0:1, gs], ALU.mult)
                    return v.scalar_tensor_tensor(rt1[0:1, gs], p_sq[0:1, :], 1.0 / width,
                                                  scr[0:1, 0:G], ALU.mult, ALU.subtract)
                return f
            rows1_ref = plan.add_n("dve", mk_rows1(), [summ])
            def mk_sqrt1(g=g, gs=gs):
                def f(s):
                    return s.activation(rS[0:1, gs], rt1[0:1, gs], AF.Sqrt, bias=eps1_col[0:1, 0:1])
                return f
            sq1 = plan.add_n("act", mk_sqrt1(), [rows1_ref])
            def mk_rows2(g=g, gs=gs):
                def f(v):
                    vsn(v, v.reciprocal(rS[0:1, gs], rS[0:1, gs]))
                    ins = None
                    if kind == 'lnrms':
                        if rows_s1 is not None:
                            v.tensor_copy(rows_s1[0:1, gs], rS[0:1, gs])
                            v.scalar_tensor_tensor(rows_c1[0:1, gs], rt2[0:1, gs], -1.0,
                                                   rS[0:1, gs], ALU.mult, ALU.mult)
                        v.tensor_tensor(scr[0:1, 0:G], rS[0:1, gs], rS[0:1, gs], ALU.mult)
                        ins = v.tensor_tensor(scr[0:1, 0:G], rt1[0:1, gs], scr[0:1, 0:G], ALU.mult)
                    if ins is None:
                        ins = v.tensor_copy(scr[0:1, 0:G], rS[0:1, gs])
                    return ins
                return f
            rows2_ref = plan.add_n("dve", mk_rows2(), [sq1])
            if kind == 'lnrms':
                def mk_sqrt2(g=g, gs=gs):
                    def f(s):
                        return s.activation(scr[0:1, 0:G], scr[0:1, 0:G], AF.Sqrt,
                                            bias=eps2_col[0:1, 0:1])
                    return f
                sq2 = plan.add_n("act", mk_sqrt2(), [rows2_ref])
                def mk_rows3(g=g, gs=gs):
                    def f(v):
                        vsn(v, v.reciprocal(scr[0:1, 0:G], scr[0:1, 0:G]))
                        ins = v.tensor_tensor(rS[0:1, gs], rS[0:1, gs], scr[0:1, 0:G], ALU.mult)
                        if True:
                            vsn(v, ins)
                            ins = v.scalar_tensor_tensor(rC[0:1, gs], rt2[0:1, gs], -1.0,
                                                         rS[0:1, gs], ALU.mult, ALU.mult)
                        return ins
                    return f
                rows_final = plan.add_n("dve", mk_rows3(), [sq2])
            elif kind == 'ln':
                def mk_rows3(g=g, gs=gs):
                    def f(v):
                        return v.scalar_tensor_tensor(rC[0:1, gs], rt2[0:1, gs], -1.0,
                                                      rS[0:1, gs], ALU.mult, ALU.mult)
                    return f
                rows_final = plan.add_n("dve", mk_rows3(), [rows2_ref])
            else:
                rows_final = rows2_ref
            def mk_bR(g=g, gs=gs):
                def f(t):
                    return t.matmul(pR[:, :], ones_row[:, :], rS[0:1, gs], start=True, stop=True)
                return f
            bR = plan.add_n("pe", mk_bR(), [rows_final, prev_apply_last])
            if rms_gain:
                for ic in range(nic):
                    def mk_ap(g=g, ic=ic):
                        def f(v):
                            v.tensor_tensor(scr[:, :], x_sb[:, ic * xcols + g * G: ic * xcols + (g + 1) * G],
                                            pR[:, :], ALU.mult)
                            return v.tensor_scalar(
                                zn_sb[:, ic * zcols + g * G: ic * zcols + (g + 1) * G],
                                scr[:, :], gain_coltile[:, ic:ic + 1], None, ALU.mult)
                        return f
                    prev_apply_last = plan.add_n("dve", mk_ap(), [bR])
            elif gain_rowtile is not None:
                def mk_bC0g(g=g, gs=gs):
                    def f(t):
                        return t.matmul(pC[0][:, :], ones_row[:, :], rC[0:1, gs],
                                        start=True, stop=True)
                    return f
                bC0 = plan.add_n("pe", mk_bC0g(), [rows_final, prev_apply_last])
                def mk_apg(g=g):
                    def f(v):
                        for ic in range(nic):
                            v.tensor_tensor(scr[:, :],
                                            x_sb[:, ic * xcols + g * G: ic * xcols + (g + 1) * G],
                                            pR[:, :], ALU.mult)
                            v.tensor_tensor(scr[:, :], scr[:, :], pC[0][:, :], ALU.add)
                            ins = v.tensor_scalar(
                                zn_sb[:, ic * zcols + g * G: ic * zcols + (g + 1) * G],
                                scr[:, :], gain_coltile[:, ic:ic + 1], None, ALU.mult)
                        return ins
                    return f
                prev_apply_last = plan.add_n("dve", mk_apg(), [bC0, bR])
            else:
                def mk_bC0(g=g, gs=gs):
                    def f(t):
                        return t.matmul(pC[0][:, :], ones_row[:, :], rC[0:1, gs],
                                        start=True, stop=True)
                    return f
                bC0 = plan.add_n("pe", mk_bC0(), [rows_final, prev_apply_last])
                def mk_ap(g=g):
                    def f(v):
                        for ic in range(nic):
                            v.tensor_tensor(scr[:, :],
                                            x_sb[:, ic * xcols + g * G: ic * xcols + (g + 1) * G],
                                            pR[:, :], ALU.mult)
                            ins = v.tensor_tensor(
                                zn_sb[:, ic * zcols + g * G: ic * zcols + (g + 1) * G],
                                scr[:, :], pC[0][:, :], ALU.add)
                        return ins
                    return f
                prev_apply_last = plan.add_n("dve", mk_ap(), [bC0, bR])
        plan.emit(nc, c, "lz" + name)


def emit_zn(nc, x_sb, zn_sb, rows_s, rows_c, ones_row, gain_rowtile=None, gain_coltile=None,
            nic=NIC, xin_cols=R, zout_cols=R, name="", rms_gain=False, zoff=0, xoff=0):
    with ExitStack() as c:
        pR = c.enter_context(nc.psum_tensor("pR" + name, [128, G], F32))
        pC = None if rms_gain else c.enter_context(nc.psum_tensor("pC" + name, [128, G], F32))
        scr = c.enter_context(nc.sbuf_tensor("znscr" + name, [128, G], F32))
        for g in range(NG):
            if rms_gain:
                with nc.Block() as blk:
                    @blk.tensor
                    def _(t):
                        t.matmul(pR[:, :], ones_row[:, :], rows_s[0:1, g * G:(g + 1) * G],
                                 start=True, stop=True)
                with nc.Block() as blk:
                    @blk.vector
                    def _(v):
                        for ic in range(nic):
                            xs = x_sb[:, xoff + ic * xin_cols + g * G: xoff + ic * xin_cols + (g + 1) * G]
                            zs = zn_sb[:, zoff + ic * zout_cols + g * G: zoff + ic * zout_cols + (g + 1) * G]
                            v.tensor_tensor(scr[:, :], xs, pR[:, :], ALU.mult)
                            v.tensor_scalar(zs, scr[:, :], gain_coltile[:, ic:ic + 1], None, ALU.mult)
                continue
            if gain_rowtile is None:
                with nc.Block() as blk:
                    @blk.tensor
                    def _(t):
                        t.matmul(pR[:, :], ones_row[:, :], rows_s[0:1, g * G:(g + 1) * G],
                                 start=True, stop=True)
                        t.matmul(pC[:, :], ones_row[:, :], rows_c[0:1, g * G:(g + 1) * G],
                                 start=True, stop=True)
                with nc.Block() as blk:
                    @blk.vector
                    def _(v):
                        for ic in range(nic):
                            xs = x_sb[:, ic * xin_cols + g * G: ic * xin_cols + (g + 1) * G]
                            zs = zn_sb[:, ic * zout_cols + g * G: ic * zout_cols + (g + 1) * G]
                            v.tensor_tensor(scr[:, :], xs, pR[:, :], ALU.mult)
                            v.tensor_tensor(zs, scr[:, :], pC[:, :], ALU.add)
            else:
                for ic in range(nic):
                    with nc.Block() as blk:
                        @blk.tensor
                        def _(t):
                            if ic == 0:
                                t.matmul(pR[:, :], ones_row[:, :], rows_s[0:1, g * G:(g + 1) * G],
                                         start=True, stop=True)
                            t.matmul(pC[:, :], gain_rowtile[0:1, ic * 128:(ic + 1) * 128],
                                     rows_c[0:1, g * G:(g + 1) * G], start=True, stop=True)
                    with nc.Block() as blk:
                        @blk.vector
                        def _(v):
                            xs = x_sb[:, ic * xin_cols + g * G: ic * xin_cols + (g + 1) * G]
                            zs = zn_sb[:, ic * zout_cols + g * G: ic * zout_cols + (g + 1) * G]
                            v.tensor_tensor(scr[:, :], xs, pR[:, :], ALU.mult)
                            v.scalar_tensor_tensor(zs, scr[:, :], gain_coltile[:, ic:ic + 1],
                                                   pC[:, :], ALU.mult, ALU.add)


def declare_consts(nc):
    ios = {}
    ios["ones_col"] = nc.dram_tensor("ones_col", [128, 1], F32, kind="ExternalInput").ap()
    ios["ones_row"] = nc.dram_tensor("ones_row", [1, 128], F32, kind="ExternalInput").ap()
    ios["ident_f32"] = nc.dram_tensor("ident_f32", [128, 128], F32, kind="ExternalInput").ap()
    ios["eps_cols"] = nc.dram_tensor("eps_cols", [128, 4], F32, kind="ExternalInput").ap()
    ios["ident_bf"] = nc.dram_tensor("ident_bf", [128, 128], BF16, kind="ExternalInput").ap()
    ios["ones_col_bf"] = nc.dram_tensor("ones_col_bf", [128, 1], BF16, kind="ExternalInput").ap()
    ios["triu"] = nc.dram_tensor("triu", [128, 128], F32, kind="ExternalInput").ap()
    return ios


def const_arrays():
    return {
        "ones_col": np.ones((128, 1), np.float32),
        "ones_row": np.ones((1, 128), np.float32),
        "ident_f32": np.eye(128, dtype=np.float32),
        "eps_cols": np.tile(np.array([[1e-5, 1e-6, 0.0, 0.0]], np.float32), (128, 1)),
        "ident_bf": np.eye(128, dtype=ml_dtypes.bfloat16),
        "ones_col_bf": np.ones((128, 1), ml_dtypes.bfloat16),
        "triu": np.triu(np.ones((128, 128), np.float32)),
    }


def load_consts(nc, c, io):
    ones_col = c.enter_context(nc.sbuf_tensor("c_onescol", [128, 1], F32))
    ones_row = c.enter_context(nc.sbuf_tensor("c_onesrow", [1, 128], F32))
    ident = c.enter_context(nc.sbuf_tensor("c_ident", [128, 128], F32))
    eps_cols = c.enter_context(nc.sbuf_tensor("c_eps", [128, 4], F32))
    ident_bf = c.enter_context(nc.sbuf_tensor("c_identbf", [128, 128], BF16))
    ones_col_bf = c.enter_context(nc.sbuf_tensor("c_onescolbf", [128, 1], BF16))
    triu = c.enter_context(nc.sbuf_tensor("c_triu", [128, 128], F32))
    ld = c.enter_context(nc.semaphore("c_ld"))
    with nc.Block() as blk:
        @blk.sync
        def _(sy):
            sy.dma_start(out=ones_col[:, :], in_=io["ones_col"]).then_inc(ld, 16)
            sy.dma_start(out=ones_row[:, :], in_=io["ones_row"]).then_inc(ld, 16)
            sy.dma_start(out=ident[:, :], in_=io["ident_f32"]).then_inc(ld, 16)
            sy.dma_start(out=eps_cols[:, :], in_=io["eps_cols"]).then_inc(ld, 16)
            sy.dma_start(out=ident_bf[:, :], in_=io["ident_bf"]).then_inc(ld, 16)
            sy.dma_start(out=ones_col_bf[:, :], in_=io["ones_col_bf"]).then_inc(ld, 16)
            sy.dma_start(out=triu[:, :], in_=io["triu"]).then_inc(ld, 16)
            sy.wait_ge(ld, 112)
    return {"ones_col": ones_col, "ones_row": ones_row, "ident": ident,
            "eps_cols": eps_cols, "ident_bf": ident_bf, "ones_col_bf": ones_col_bf,
            "triu": triu}



def emit_blockB(nc, c, io, x_sb, zn_sb, ones_col, ones_row, ident, eps_cols, stage="full", ones_col_bf=None):
    sb = ExitStack()
    NE, F = 8, 2048
    if stage.startswith("exp"):
        NE = int(stage[3:])
    NFC = F // 128
    NDC = 8
    vsn = VSync(c, nc, "b_vs")
    if True:
        emit_ln_zn(nc, c, vsn, x_sb, zn_sb, ones_col, ones_row, 'ln',
                   eps1_col=eps_cols[:, 0:1], name="B", ones_col_bf=ones_col_bf)
    if stage == "ln":
        return

    fw_tm = sb.enter_context(nc.sbuf_tensor("b_fwtm", [128, NT * 8], F32))
    with ExitStack() as cc:
        wrs = cc.enter_context(nc.sbuf_tensor("b_wr", [128, NIC * 8], BF16))
        rld = c.enter_context(nc.semaphore("b_rld"))
        with nc.Block() as blk:
            @blk.gpsimd
            def _(g):
                g.dma_start(out=wrs[:, :].rearrange("p (ic e) -> p ic e", ic=NIC),
                            in_=io["wr"].rearrange("(ic p) e -> p ic e", p=128)).then_inc(rld, 16)
                g.wait_ge(rld, 16)
        pl = cc.enter_context(nc.psum_tensor("b_pl", [128, 2 * G], F32))
        ltm = cc.enter_context(nc.sbuf_tensor("b_ltm", [128, NT * 8], F32))
        prb = cc.enter_context(nc.sbuf_tensor("b_prb", [128, 8], F32))
        col1 = cc.enter_context(nc.sbuf_tensor("b_col1", [128, 1], F32))
        col2 = cc.enter_context(nc.sbuf_tensor("b_col2", [128, 1], F32))
        col3 = cc.enter_context(nc.sbuf_tensor("b_col3", [128, 1], F32))
        msk = cc.enter_context(nc.sbuf_tensor("b_msk", [128, 8], F32))
        rp = Plan()
        d3_prev = None
        e1_prev = None
        d1list = [None] * NT
        for tt in range(NT):
            ts8 = slice((tt % 2) * G + (tt // 2) * 8, (tt % 2) * G + (tt // 2) * 8 + 8)
            def mk_lmm(tt=tt, ts8=ts8):
                def f(t):
                    for ic in range(NIC):
                        ins = t.matmul(pl[:, ts8],
                                       zn_sb[:, ic * R + tt * 128:ic * R + (tt + 1) * 128],
                                       wrs[:, ic * 8:(ic + 1) * 8],
                                       start=(tt < 2 and ic == 0), stop=(ic == NIC - 1),
                                       skip_group_check=True)
                    return ins
                return f
            lmm = rp.add_n("pe", mk_lmm(), [d1list[tt - 2] if tt >= 2 else None])
            def mk_d1(tt=tt, ts8=ts8):
                def f(v):
                    return v.tensor_reduce(col1[:, :], pl[:, ts8], AX.X, ALU.max, negate=True)
                return f
            d1 = rp.add_n("dve", mk_d1(), [lmm, e1_prev])
            d1list[tt] = d1
            def mk_e1(tt=tt, ts8=ts8):
                def f(s):
                    return s.activation(prb[:, :], pl[:, ts8], AF.Exp, bias=col1[:, :])
                return f
            e1 = rp.add_n("act", mk_e1(), [d1, d3_prev])
            d1list[tt] = e1
            ls8 = slice(tt * 8, (tt + 1) * 8)
            def mk_d2(tt=tt, ls8=ls8):
                def f(v):
                    vsn(v, v.tensor_reduce(col2[:, :], prb[:, :], AX.X, ALU.add))
                    vsn(v, v.reciprocal(col2[:, :], col2[:, :]))
                    vsn(v, v.tensor_scalar(prb[:, :], prb[:, :], col2[:, :], None, ALU.mult))
                    vsn(v, v.tensor_reduce(col1[:, :], prb[:, :], AX.X, ALU.max))
                    vsn(v, v.tensor_scalar(msk[:, :], prb[:, :], col1[:, :], None, ALU.is_ge))
                    vsn(v, v.scalar_tensor_tensor(ltm[:, ls8], msk[:, :], -3.0, prb[:, :],
                                                  ALU.mult, ALU.add))
                    vsn(v, v.tensor_reduce(col3[:, :], ltm[:, ls8], AX.X, ALU.max))
                    return v.tensor_scalar(msk[:, :], prb[:, :], col3[:, :], None, ALU.is_ge)
                return f
            d2 = rp.add_n("dve", mk_d2(), [e1])
            def mk_e2(tt=tt):
                def f(s):
                    return s.activation(prb[:, :], prb[:, :], AF.Exp)
                return f
            e2 = rp.add_n("act", mk_e2(), [d2])
            def mk_d3(tt=tt, ls8=ls8):
                def f(v):
                    vsn(v, v.tensor_tensor(prb[:, :], prb[:, :], msk[:, :], ALU.mult))
                    vsn(v, v.tensor_reduce(col2[:, :], prb[:, :], AX.X, ALU.add))
                    vsn(v, v.reciprocal(col2[:, :], col2[:, :]))
                    return v.tensor_scalar(fw_tm[:, ls8], prb[:, :], col2[:, :], None, ALU.mult)
                return f
            d3 = rp.add_n("dve", mk_d3(), [e2])
            d3_prev, e1_prev = d3, e2
        rp.emit(nc, c, "brp")
    f_sb = sb.enter_context(nc.sbuf_tensor("b_f", [128, NFC * R], BF16))
    wb = sb.enter_context(nc.sbuf_tensor("b_wb", [128, R], F32))
    row_all = sb.enter_context(nc.sbuf_tensor("b_rowall", [1, R], F32))
    wgb = [sb.enter_context(nc.sbuf_tensor(f"b_wg{i}", [128, NIC * 128], BF16)) for i in range(2)]
    wub = [sb.enter_context(nc.sbuf_tensor(f"b_wu{i}", [128, NIC * 128], BF16)) for i in range(2)]
    wdb = [sb.enter_context(nc.sbuf_tensor(f"b_wd{i}", [128, NFC * 128], BF16)) for i in range(2)]
    stmp = sb.enter_context(nc.sbuf_tensor("b_stmp", [128, 4 * G], F32))
    ttmp = sb.enter_context(nc.sbuf_tensor("b_ttmp", [128, G], F32))
    pg = sb.enter_context(nc.psum_tensor("b_pg", [128, 4 * G], F32))
    pu = sb.enter_context(nc.psum_tensor("b_pu", [128, 4 * G], F32))
    wsem = c.enter_context(nc.semaphore("b_wsem"))
    fsem = c.enter_context(nc.semaphore("b_fsem"))
    asem = c.enter_context(nc.semaphore("b_asem"))
    esem = c.enter_context(nc.semaphore("b_esem"))

    W_E = 32 * NFC + 16 * NDC
    F_E = 2 + 2 * NFC + NDC * 2
    A_E = 1 + 4 * NFC
    E_E = 1 + 2 * NFC + 2 * NDC

    with nc.Block() as blk:
        @blk.gpsimd
        def _(gg):
            for e in range(NE):
                for fc in range(NFC):
                    if fc >= 2:
                        gg.wait_ge(fsem, 50 * e + 2 + 2 * (fc - 1))
                    elif e >= 1:
                        gg.wait_ge(fsem, 50 * e - 16 + 2 * fc)
                    gg.dma_start(
                        out=wgb[fc % 2][:, :].rearrange("p (ic f) -> p ic f", ic=NIC),
                        in_=io["wg"][e, :, fc * 128:(fc + 1) * 128].rearrange("(ic p) f -> p ic f", p=128)
                    ).then_inc(wsem, 16)
                    gg.dma_start(
                        out=wub[fc % 2][:, :].rearrange("p (ic f) -> p ic f", ic=NIC),
                        in_=io["wu"][e, :, fc * 128:(fc + 1) * 128].rearrange("(ic p) f -> p ic f", p=128)
                    ).then_inc(wsem, 16)
                for dc in (range(NDC) if DBG["down"] else []):
                    if dc >= 2:
                        gg.wait_ge(fsem, 50 * e + 34 + 2 * (dc - 1))
                    elif e >= 1:
                        gg.wait_ge(fsem, 50 * e - 2 + 2 * dc)
                    gg.dma_start(
                        out=wdb[dc % 2][:, :].rearrange("p (fc d) -> p fc d", fc=NFC),
                        in_=io["wd"][e, :, dc * 128:(dc + 1) * 128].rearrange("(fc p) d -> p fc d", p=128)
                    ).then_inc(wsem, 16)

        @blk.tensor
        def _(t):
            for e in range(NE):
                W0 = e * W_E
                if e >= 1:
                    t.wait_ge(esem, 49 * e)
                for tt in range(NT):
                    ins = t.transpose(pg[0:1, tt * 128:(tt + 1) * 128],
                                      fw_tm[:, tt * 8 + e: tt * 8 + e + 1], ident[:, :])
                ins.then_inc(fsem, 1)
                t.wait_ge(asem, 65 * e + 1)
                for g in range(NG):
                    ins = t.matmul(pu[:, g * G:(g + 1) * G], ones_row[:, :],
                                   row_all[0:1, g * G:(g + 1) * G], start=True, stop=True)
                ins.then_inc(fsem, 1)
                for fc in range(NFC):
                    t.wait_ge(wsem, W0 + 32 * (fc + 1))
                    for half in range(2):
                        u = 2 * fc + half
                        if not DBG["evac"]:
                            pass
                        elif u >= 2:
                            t.wait_ge(esem, 49 * e + u)
                        else:
                            t.wait_ge(esem, 49 * e + 1)
                        bs = (u % 2) * 2 * G
                        for ic in range(NIC):
                            for g2 in range(2):
                                g = half * 2 + g2
                                t.matmul(pg[:, bs + g2 * G: bs + (g2 + 1) * G],
                                         wgb[fc % 2][:, ic * 128:(ic + 1) * 128],
                                         zn_sb[:, ic * R + g * G:ic * R + (g + 1) * G],
                                         start=(ic == 0), stop=(ic == NIC - 1))
                        for ic in range(NIC):
                            for g2 in range(2):
                                g = half * 2 + g2
                                ins = t.matmul(pu[:, bs + g2 * G: bs + (g2 + 1) * G],
                                               wub[fc % 2][:, ic * 128:(ic + 1) * 128],
                                               zn_sb[:, ic * R + g * G:ic * R + (g + 1) * G],
                                               start=(ic == 0), stop=(ic == NIC - 1))
                        ins.then_inc(fsem, 1)
                W1 = W0 + 32 * NFC
                for dc in (range(NDC) if DBG["down"] else []):
                    t.wait_ge(wsem, W1 + 16 * (dc + 1))
                    for half in range(2):
                        u = 2 * dc + half
                        if u >= 2:
                            t.wait_ge(esem, 49 * e + 33 + u - 1)
                        else:
                            t.wait_ge(esem, 49 * e + 33)
                        bs = (u % 2) * 2 * G
                        for fc in range(NFC):
                            for g2 in range(2):
                                g = half * 2 + g2
                                ins = t.matmul(pg[:, bs + g2 * G: bs + (g2 + 1) * G],
                                               wdb[dc % 2][:, fc * 128:(fc + 1) * 128],
                                               f_sb[:, fc * R + g * G:fc * R + (g + 1) * G],
                                               start=(fc == 0), stop=(fc == NFC - 1))
                        ins.then_inc(fsem, 1)

        @blk.scalar
        def _(s):
            for e in range(NE):
                s.wait_ge(fsem, 50 * e + 1)
                s.activation(row_all[0:1, :], pg[0:1, 0:R], AF.Copy).then_inc(asem, 1)
                s.wait_ge(fsem, 50 * e + 2)
                for g in range(NG):
                    ins = s.activation(wb[:, g * G:(g + 1) * G], pu[:, g * G:(g + 1) * G],
                                       AF.Copy)
                ins.then_inc(esem, 1)
                for fc in (range(NFC) if DBG["evac"] else []):
                    for half in range(2):
                        u = 2 * fc + half
                        s.wait_ge(fsem, 50 * e + 2 + u + 1)
                        bs = (u % 2) * 2 * G
                        for g2 in range(2):
                            s.activation(stmp[:, ((u % 2) * 2 + g2) * G: ((u % 2) * 2 + g2 + 1) * G],
                                         pg[:, bs + g2 * G: bs + (g2 + 1) * G],
                                         AF.Silu).then_inc(asem, 1)

        @blk.vector
        def _(v):
            for e in range(NE):
                for fc in (range(NFC) if DBG["evac"] else []):
                    for half in range(2):
                        u = 2 * fc + half
                        bs = (u % 2) * 2 * G
                        for g2 in range(2):
                            g = half * 2 + g2
                            v.wait_ge(asem, 65 * e + 1 + 2 * u + g2 + 1)
                            ins = v.tensor_tensor(f_sb[:, fc * R + g * G:fc * R + (g + 1) * G],
                                                  stmp[:, ((u % 2) * 2 + g2) * G: ((u % 2) * 2 + g2 + 1) * G],
                                                  pu[:, bs + g2 * G: bs + (g2 + 1) * G], ALU.mult)
                        ins.then_inc(esem, 1)
                for dc in (range(NDC) if (DBG["down"] and DBG["evac"]) else []):
                    for half in range(2):
                        u = 2 * dc + half
                        bs = (u % 2) * 2 * G
                        v.wait_ge(fsem, 50 * e + 34 + u + 1)
                        for g2 in range(2):
                            g = half * 2 + g2
                            v.tensor_tensor(ttmp[:, :], pg[:, bs + g2 * G: bs + (g2 + 1) * G],
                                            wb[:, g * G:(g + 1) * G], ALU.mult)
                            xs = x_sb[:, dc * R + g * G:dc * R + (g + 1) * G]
                            ins = v.tensor_tensor(xs, xs, ttmp[:, :], ALU.add)
                        ins.then_inc(esem, 1)
    sb.close()



def emit_blockC(nc, c, io, consts):
    sb = ExitStack()
    H, NCH, M = 16, 16, 64
    ones_col, ones_row = consts["ones_col"], consts["ones_row"]
    eps_cols, ident_bf, triu = consts["eps_cols"], consts["ident_bf"], consts["triu"]
    vsn = consts["vsn"]

    zn = sb.enter_context(nc.sbuf_tensor("c_zn", [128, NIC * R], BF16))
    bcols = sb.enter_context(nc.sbuf_tensor("c_bcols", [64, 48], F32))
    bcols_bf = sb.enter_context(nc.sbuf_tensor("c_bcolsbf", [64, 48], BF16))
    boutc = sb.enter_context(nc.sbuf_tensor("c_bout", [128, 8], F32))
    mcols = sb.enter_context(nc.sbuf_tensor("c_mcols", [128, 8], F32))
    ld = c.enter_context(nc.semaphore("c_ld2"))
    ldn = [0]
    gld = c.enter_context(nc.semaphore("c_gld2"))
    gldn = [0]

    def LD(eng, *dmas):
        for d_ in dmas:
            d_.then_inc(ld, 16)
            ldn[0] += 16
        eng.wait_ge(ld, ldn[0])

    def GLD(eng, *dmas):
        for d_ in dmas:
            d_.then_inc(gld, 16)
            gldn[0] += 16
        eng.wait_ge(gld, gldn[0])

    with nc.Block() as blk:
        @blk.sync
        def _(sy):
            LD(sy, sy.dma_start(out=bcols[:, :], in_=io["bqkvC"]),
               sy.dma_start(out=boutc[:, :], in_=io["boutC"]),
               sy.dma_start(out=mcols[:, :], in_=io["maskin"]))
        @blk.gpsimd
        def _(gg):
            GLD(gg, gg.dma_start(out=bcols_bf[:, :], in_=io["bqkvC"]))

    with ExitStack() as lnx:
        x_sb = lnx.enter_context(nc.sbuf_tensor("c_x", [128, NIC * R], F32))
        with nc.Block() as blk:
            @blk.sync
            def _(sy):
                LD(sy, sy.dma_start(out=x_sb[:, :].rearrange("p (ic t) -> p ic t", ic=NIC),
                                    in_=io["xin"].rearrange("(ic p) t -> p ic t", p=128)))
        emit_ln_zn(nc, c, vsn, x_sb, zn, ones_col, ones_row, 'ln',
                   eps1_col=eps_cols[:, 0:1], name="C", ones_col_bf=consts["ident_bf"][:, 0:1] if False else consts["ones_col_bf"])

    o_all = sb.enter_context(nc.sbuf_tensor("c_oall", [128, 8 * R], BF16))
    mid = ExitStack()
    Wq_all = mid.enter_context(nc.sbuf_tensor("c_wq", [128, H * NIC * 64], BF16))
    Wk_all = mid.enter_context(nc.sbuf_tensor("c_wk", [128, H * NIC * 64], BF16))
    b8 = mid.enter_context(nc.sbuf_tensor("c_b8", [64, 2 * H], F32))
    with ExitStack() as c2s:
        wrow = c2s.enter_context(nc.sbuf_tensor("c_wrow", [64, NIC * 128], BF16))
        prj = c2s.enter_context(nc.sbuf_tensor("c_prj", [64, 64], BF16))
        pcmp = c2s.enter_context(nc.psum_tensor("c_pcmp", [128, 512], F32))
        pbb = c2s.enter_context(nc.psum_tensor("c_pbb", [64, 64], F32))
        c2p = Plan()
        prev_cmm = prev_cevac = None
        for h in range(H):
            for qk in range(2):
                dst = Wq_all if qk == 0 else Wk_all
                rows0 = qk * 1024 + h * 64
                def mk_ldw(h=h, qk=qk, rows0=rows0):
                    def f(gg, sem):
                        gg.dma_start(out=wrow[:, :],
                                     in_=io["wqkvT"][rows0:rows0 + 64, :]).then_inc(sem, 16)
                        gg.dma_start(out=prj[:, :], in_=io["proj"][h]).then_inc(sem, 16)
                    return f
                ldw = c2p.add_n("gp", mk_ldw(), [prev_cmm], inc_total=32)
                def mk_cmm(h=h, qk=qk):
                    def f(t):
                        for ic in range(NIC):
                            t.matmul(pcmp[:, ic * 64:(ic + 1) * 64],
                                     wrow[:, ic * 128:(ic + 1) * 128], prj[:, :],
                                     start=True, stop=True, skip_group_check=True)
                        return t.matmul(pbb[:, 0:1], prj[:, :],
                                        bcols_bf[:, qk * 16 + h: qk * 16 + h + 1],
                                        start=True, stop=True)
                    return f
                cmm = c2p.add_n("pe", mk_cmm(), [ldw, prev_cevac])
                def mk_cevac(h=h, qk=qk, dst=dst):
                    def f(v):
                        for ic in range(NIC):
                            v.tensor_scalar(dst[:, (h * NIC + ic) * 64:(h * NIC + ic + 1) * 64],
                                            pcmp[:, ic * 64:(ic + 1) * 64],
                                            0.125, None, ALU.mult)
                        return v.tensor_scalar(b8[:, qk * H + h: qk * H + h + 1], pbb[:, 0:1],
                                               0.125, None, ALU.mult)
                    return f
                prev_cevac = c2p.add_n("dve", mk_cevac(), [cmm])
                prev_cmm = cmm
        c2p.emit(nc, c, "c2p")
    ktm = mid.enter_context(nc.sbuf_tensor("c_ktm", [128, H * NCH * 64], BF16))
    vext = mid.enter_context(nc.sbuf_tensor("c_vext", [128, H * NCH * 65], BF16))
    contrib = mid.enter_context(nc.sbuf_tensor("c_contrib", [64, H * 65], F32))
    Sg_all = mid.enter_context(nc.sbuf_tensor("c_sg", [64, H * 65], F32))
    with ExitStack() as c3s:
        wvs = c3s.enter_context(nc.sbuf_tensor("c_wvs", [128, NIC * 64], BF16))
        fscr = c3s.enter_context(nc.sbuf_tensor("c_fscr", [64, R], BF16))
        pk3 = c3s.enter_context(nc.psum_tensor("c_pk3", [64, NG * G], F32))
        ptp3 = c3s.enter_context(nc.psum_tensor("c_ptp3", [128, NCH * 64], BF16))
        pst3 = c3s.enter_context(nc.psum_tensor("c_pst3", [64, 65], F32))
        c3p = Plan()
        p_vmm = p_vact = p_vtr = p_vcopy = p_kact = p_kcopy = p_scon = None
        for h in range(H):
            def mk_ldv(h=h):
                def f(gg, sem):
                    gg.dma_start(
                        out=wvs[:, :].rearrange("p (ic f) -> p ic f", ic=NIC),
                        in_=io["wqkv"][:, 2048 + h * 64: 2048 + (h + 1) * 64].rearrange(
                            "(ic p) f -> p ic f", p=128)).then_inc(sem, 16)
                return f
            ldv = c3p.add_n("gp", mk_ldv(), [p_vmm], inc_total=16)
            def mk_vmm(h=h):
                def f(t):
                    for g in range(NG):
                        for ic in range(NIC):
                            ins = t.matmul(pk3[:, g * G:(g + 1) * G],
                                           wvs[:, ic * 64:(ic + 1) * 64],
                                           zn[:, ic * R + g * G: ic * R + (g + 1) * G],
                                           start=(ic == 0), stop=(ic == NIC - 1))
                    return ins
                return f
            vmm = c3p.add_n("pe", mk_vmm(), [ldv, p_kact])
            def mk_vact(h=h):
                def f(s):
                    for g in range(NG):
                        ins = s.activation(fscr[:, g * G:(g + 1) * G], pk3[0:64, g * G:(g + 1) * G],
                                           AF.Identity, bias=bcols[:, 32 + h: 33 + h])
                    return ins
                return f
            vact = c3p.add_n("act", mk_vact(), [vmm])
            def mk_vtr(h=h):
                def f(t):
                    for c2 in range(NCH):
                        ins = t.transpose(ptp3[:, c2 * 64:(c2 + 1) * 64],
                                          fscr[:, c2 * 128:(c2 + 1) * 128], ident_bf[0:64, 0:64])
                    return ins
                return f
            vtr = c3p.add_n("pe", mk_vtr(), [vact, p_kcopy])
            def mk_vcopy(h=h):
                def f(v):
                    v.memset(vext[:, h * NCH * 65:(h + 1) * NCH * 65], 1.0)
                    for c2 in range(NCH):
                        ins = v.tensor_copy(vext[:, (h * NCH + c2) * 65:(h * NCH + c2) * 65 + 64],
                                            ptp3[:, c2 * 64:(c2 + 1) * 64])
                    return ins
                return f
            vcopy = c3p.add_n("dve", mk_vcopy(), [vtr])
            def mk_kmm(h=h):
                def f(t):
                    for g in range(NG):
                        for ic in range(NIC):
                            ins = t.matmul(pk3[:, g * G:(g + 1) * G],
                                           Wk_all[:, (h * NIC + ic) * 64:(h * NIC + ic + 1) * 64],
                                           zn[:, ic * R + g * G: ic * R + (g + 1) * G],
                                           start=(ic == 0), stop=(ic == NIC - 1))
                    return ins
                return f
            kmm = c3p.add_n("pe", mk_kmm(), [vact])
            def mk_kact(h=h):
                def f(s):
                    for g in range(NG):
                        ins = s.activation(fscr[:, g * G:(g + 1) * G], pk3[0:64, g * G:(g + 1) * G],
                                           AF.Relu, bias=b8[:, H + h: H + h + 1])
                    return ins
                return f
            p_kact = c3p.add_n("act", mk_kact(), [kmm, vtr])
            def mk_ktr(h=h):
                def f(t):
                    for c2 in range(NCH):
                        ins = t.transpose(ptp3[:, c2 * 64:(c2 + 1) * 64],
                                          fscr[:, c2 * 128:(c2 + 1) * 128], ident_bf[0:64, 0:64])
                    return ins
                return f
            ktr = c3p.add_n("pe", mk_ktr(), [p_kact, vcopy])
            def mk_kcopy(h=h):
                def f(v):
                    for c2 in range(NCH):
                        ins = v.tensor_copy(ktm[:, (h * NCH + c2) * 64:(h * NCH + c2 + 1) * 64],
                                            ptp3[:, c2 * 64:(c2 + 1) * 64])
                    return ins
                return f
            p_kcopy = c3p.add_n("dve", mk_kcopy(), [ktr])
            def mk_smm(h=h):
                def f(t):
                    for c2 in range(NCH):
                        ins = t.matmul(pst3[:, :],
                                       ktm[:, (h * NCH + c2) * 64:(h * NCH + c2 + 1) * 64],
                                       vext[:, (h * NCH + c2) * 65:(h * NCH + c2 + 1) * 65],
                                       start=(c2 == 0), stop=(c2 == NCH - 1))
                    return ins
                return f
            smm = c3p.add_n("pe", mk_smm(), [p_kcopy, vcopy, p_scon])
            def mk_scon(h=h):
                def f(v):
                    return v.tensor_copy(contrib[:, h * 65:(h + 1) * 65], pst3[:, :])
                return f
            p_scon = c3p.add_n("dve", mk_scon(), [smm])
            p_vmm, p_vtr, p_vcopy = vmm, vtr, vcopy
        c3p.emit(nc, c, "c3p")
    ccs = c.enter_context(nc.semaphore("c_ccs2"))
    with nc.Block() as blk:
        @blk.sync
        def _(sy):
            LD(sy, sy.dma_start(out=io["cc_inC"].rearrange("(h p) d -> p h d", p=64),
                                in_=contrib[:, :].rearrange("p (h d) -> p h d", h=H)))
    with nc.Block() as blk:
        @blk.gpsimd
        def _(gg):
            gg.collective_compute("AllGather", ALU.bypass,
                                  replica_groups=[list(range(8))],
                                  ins=[io["cc_inC"].opt()],
                                  outs=[io["cc_outC"].opt()]).then_inc(ccs, 1)
            gg.wait_ge(ccs, 1)
    with ExitStack() as gsc:
        gath = gsc.enter_context(nc.sbuf_tensor("c_gath", [64, 128 * 65], F32))
        with nc.Block() as blk:
            @blk.gpsimd
            def _(gg):
                GLD(gg, gg.dma_start(out=gath[:, :].rearrange("p (q d) -> p q d", q=128),
                                     in_=io["cc_outC"].rearrange("(q p) d -> p q d", p=64)))
        with nc.Block() as blk:
            @blk.vector
            def _(v):
                v.memset(Sg_all[:, :], 0.0)
                for r in range(8):
                    v.tensor_scalar(gath[:, r * H * 65:(r + 1) * H * 65],
                                    gath[:, r * H * 65:(r + 1) * H * 65],
                                    mcols[0:64, r:r + 1], None, ALU.mult)
                    v.tensor_tensor(Sg_all[:, :], Sg_all[:, :],
                                    gath[:, r * H * 65:(r + 1) * H * 65], ALU.add)

    pe_a = c.enter_context(nc.semaphore("c_pea"))
    pe_os = c.enter_context(nc.semaphore("c_peos"))
    act_oc = c.enter_context(nc.semaphore("c_actoc"))
    dve_am = c.enter_context(nc.semaphore("c_dveam"))
    dve_s = c.enter_context(nc.semaphore("c_dves"))
    with ExitStack() as c4s:
        q_pad = c4s.enter_context(nc.sbuf_tensor("c_qpad", [128, R], BF16))
        kfm = c4s.enter_context(nc.sbuf_tensor("c_kfm", [64, R], BF16))
        of65 = c4s.enter_context(nc.sbuf_tensor("c_of65", [128, R], F32))
        S_sb = c4s.enter_context(nc.sbuf_tensor("c_ssb", [128, 65], F32))
        S_bf = c4s.enter_context(nc.sbuf_tensor("c_sbf", [128, 65], BF16))
        am_sb = c4s.enter_context(nc.sbuf_tensor("c_am", [128, 2 * 128], BF16))
        nrow = c4s.enter_context(nc.sbuf_tensor("c_nrow", [1, R], F32))
        rb_sb = c4s.enter_context(nc.sbuf_tensor("c_rb", [64, G], F32))
        pa = c4s.enter_context(nc.psum_tensor("c_pa", [128, 2 * G], F32))
        po = c4s.enter_context(nc.psum_tensor("c_po", [128, 2 * G], F32))
        ps = c4s.enter_context(nc.psum_tensor("c_ps", [128, 2 * G], F32))
        ptf = c4s.enter_context(nc.psum_tensor("c_ptf", [128, 1024], BF16))
        with nc.Block() as blk:
            @blk.vector
            def _(v):
                v.memset(q_pad[64:128, :], 0.0)
                v.memset(S_sb[64:128, :], 0.0)
                v.memset(S_bf[64:128, :], 0.0)
        plan = Plan()
        last_pa = last_po = prev_omm = prev_mmA = prev_tc1 = None
        for h in range(H):
            qe = None
            qm0 = qe1 = None
            for gp in range(2):
                def mk_qmm(h=h, gp=gp):
                    def f(t):
                        for g2 in range(2):
                            g = gp * 2 + g2
                            ptgt = pa if g2 == 0 else po
                            for ic in range(NIC):
                                ins = t.matmul(ptgt[0:64, 0:G],
                                               Wq_all[:, (h * NIC + ic) * 64:(h * NIC + ic + 1) * 64],
                                               zn[:, ic * R + g * G: ic * R + (g + 1) * G],
                                               start=(ic == 0), stop=(ic == NIC - 1))
                        return ins
                    return f
                deps = [last_pa, last_po] if gp == 0 else [qe]
                qm = plan.add_n("pe", mk_qmm(), deps)
                def mk_qe(h=h, gp=gp):
                    def f(s):
                        for g2 in range(2):
                            g = gp * 2 + g2
                            ptgt = pa if g2 == 0 else po
                            ins = s.activation(q_pad[0:64, g * G:(g + 1) * G], ptgt[0:64, 0:G],
                                               AF.Relu, bias=b8[:, h: h + 1])
                        return ins
                    return f
                qe = plan.add_n("act", mk_qe(), [qm, prev_omm] if gp == 0 else [qm])
            qe1 = qe
            tcs = []
            tprev = None
            for ph in range(2):
                def mk_tf(h=h, ph=ph):
                    def f(t):
                        for c2 in range(8):
                            cc2 = ph * 8 + c2
                            ins = t.transpose(ptf[0:64, c2 * 128:(c2 + 1) * 128],
                                              ktm[:, (h * NCH + cc2) * 64:(h * NCH + cc2 + 1) * 64],
                                              ident_bf[:, :])
                        return ins
                    return f
                tf = plan.add_n("pe", mk_tf(), [prev_tc1 if ph == 0 else tprev])
                def mk_tc(h=h, ph=ph):
                    def f(v):
                        return v.tensor_copy(kfm[:, ph * 1024:(ph + 1) * 1024], ptf[0:64, :])
                    return f
                tprev = plan.add_n("dve", mk_tc(), [tf, prev_mmA])
                tcs.append(tprev)
            def mk_init(h=h):
                def f(v):
                    v.tensor_copy(S_sb[0:64, :], Sg_all[:, h * 65:(h + 1) * 65])
                    return v.tensor_copy(S_bf[0:64, :], Sg_all[:, h * 65:(h + 1) * 65])
                return f
            init = plan.add_n("dve", mk_init(), [prev_omm])
            mmA = [None] * NCH
            mask = [None] * NCH
            omm = [None] * NCH
            ocopy = [None] * NCH
            supd = [None] * NCH
            for c2 in range(NCH):
                pao = (c2 % 2) * G
                def mk_mmA(h=h, c2=c2, pao=pao):
                    def f(t):
                        return t.matmul(pa[:, pao:pao + 128],
                                        kfm[0:64, c2 * 128:(c2 + 1) * 128],
                                        q_pad[0:64, c2 * 128:(c2 + 1) * 128],
                                        start=True, stop=True)
                    return f
                deps = [qe1, tcs[c2 // 8]]
                if c2 >= 2:
                    deps.append(mask[c2 - 2])
                mmA[c2] = plan.add_n("pe", mk_mmA(), deps)
                def mk_mask(h=h, c2=c2, pao=pao):
                    def f(v):
                        return v.tensor_tensor(am_sb[:, (c2 % 2) * 128:(c2 % 2) * 128 + 128],
                                               pa[:, pao:pao + 128], triu[:, :], ALU.mult)
                    return f
                mdeps = [mmA[c2]]
                mdeps.append(omm[c2 - 2] if c2 >= 2 else prev_omm)
                mask[c2] = plan.add_n("dve", mk_mask(), mdeps)
                def mk_omm(h=h, c2=c2, pao=pao):
                    def f(t):
                        t.matmul(po[0:65, pao:pao + 128],
                                 vext[:, (h * NCH + c2) * 65:(h * NCH + c2 + 1) * 65],
                                 am_sb[:, (c2 % 2) * 128:(c2 % 2) * 128 + 128],
                                 start=True, stop=False)
                        t.matmul(po[0:65, pao:pao + 128], S_bf[:, :],
                                 q_pad[:, c2 * 128:(c2 + 1) * 128],
                                 start=False, stop=True)
                        return t.matmul(ps[0:64, pao:pao + 65],
                                        ktm[:, (h * NCH + c2) * 64:(h * NCH + c2 + 1) * 64],
                                        vext[:, (h * NCH + c2) * 65:(h * NCH + c2 + 1) * 65],
                                        start=True, stop=True)
                    return f
                odeps = [mask[c2], supd[c2 - 1] if c2 >= 1 else init]
                if c2 >= 2:
                    odeps += [ocopy[c2 - 2], supd[c2 - 2]]
                else:
                    odeps.append(qe1)
                omm[c2] = plan.add_n("pe", mk_omm(), odeps)
                def mk_ocopy(h=h, c2=c2, pao=pao):
                    def f(s):
                        return s.activation(of65[0:65, c2 * 128:(c2 + 1) * 128],
                                            po[0:65, pao:pao + 128], AF.Copy)
                    return f
                ocopy[c2] = plan.add_n("act", mk_ocopy(), [omm[c2]])
                def mk_supd(h=h, c2=c2, pao=pao):
                    def f(v):
                        v.scalar_tensor_tensor(S_bf[0:64, :], ps[0:64, pao:pao + 65], 1.0,
                                               S_sb[0:64, :], ALU.mult, ALU.add)
                        return v.tensor_tensor(S_sb[0:64, :], S_sb[0:64, :],
                                               ps[0:64, pao:pao + 65], ALU.add)
                    return f
                supd[c2] = plan.add_n("dve", mk_supd(), [omm[c2]])
            def mk_nrow(h=h):
                def f(v):
                    vsn(v, v.tensor_scalar(nrow[0:1, :], of65[64:65, :], eps_cols[0:1, 1:2],
                                           None, ALU.add))
                    return v.reciprocal(nrow[0:1, :], nrow[0:1, :])
                return f
            nr = plan.add_n("dve", mk_nrow(), [ocopy[NCH - 1]])
            rb_prev = None
            for g in range(NG):
                def mk_nb(h=h, g=g):
                    def f(t):
                        return t.matmul(pa[0:64, 0:G], ones_row[:, 0:64],
                                        nrow[0:1, g * G:(g + 1) * G], start=True, stop=True)
                    return f
                nbdeps = [nr, mask[NCH - 1]]
                if rb_prev is not None:
                    nbdeps.append(rb_prev)
                nb = plan.add_n("pe", mk_nb(), nbdeps)
                def mk_rb(h=h, g=g):
                    def f(s):
                        return s.activation(rb_sb[:, :], pa[0:64, 0:G], AF.Copy)
                    return f
                rb_prev = plan.add_n("act", mk_rb(), [nb])
                def mk_on(h=h, g=g):
                    def f(v):
                        base = (h % 2) * 64
                        return v.tensor_tensor(
                            o_all[base:base + 64, (h // 2) * R + g * G:(h // 2) * R + (g + 1) * G],
                            of65[0:64, g * G:(g + 1) * G], rb_sb[:, :], ALU.mult)
                    return f
                onref = plan.add_n("dve", mk_on(), [rb_prev])
            last_pa = rb_prev
            last_po = ocopy[NCH - 1]
            prev_omm = omm[NCH - 1]
            prev_mmA = mmA[NCH - 1]
            prev_tc1 = tcs[1]
        plan.emit(nc, c, "c4p")
    mid.close()
    wosem = c.enter_context(nc.semaphore("c_wosem"))
    woe = c.enter_context(nc.semaphore("c_woe"))
    with ExitStack() as c5s:
        wo_sb = c5s.enter_context(nc.sbuf_tensor("c_wo2", [128, NIC * 1024], BF16))
        xg = c5s.enter_context(nc.sbuf_tensor("c_xg", [128, 8 * G], F32))
        osb = c5s.enter_context(nc.sbuf_tensor("c_osb", [128, 8 * G], F32))
        scr5 = c5s.enter_context(nc.sbuf_tensor("c_scr5", [128, 2 * G], F32))
        pwo = c5s.enter_context(nc.psum_tensor("c_pwo", [128, 2 * G], F32))
        with nc.Block() as blk:
            @blk.gpsimd
            def _(gg):
                GLD(gg, gg.dma_start(out=wo_sb[:, :].rearrange("p (ic f) -> p ic f", ic=NIC),
                                     in_=io["wout"].rearrange("(ic p) f -> p ic f", p=128)))
        won = [0, 0]
        for g in range(NG):
            with nc.Block() as blk:
                @blk.sync
                def _(sy):
                    LD(sy, sy.dma_start(out=xg[:, :].rearrange("p (ic t) -> p ic t", ic=8),
                                        in_=io["xin"][:, g * G:(g + 1) * G].rearrange(
                                            "(ic p) t -> p ic t", p=128)))
            with nc.Block() as blk:
                @blk.tensor
                def _(t):
                    for oc in range(NIC):
                        if oc >= 2:
                            t.wait_ge(woe, won[1] + oc - 1)
                        for kc in range(8):
                            ins = t.matmul(pwo[:, (oc % 2) * G:(oc % 2 + 1) * G],
                                           wo_sb[:, kc * 1024 + oc * 128: kc * 1024 + (oc + 1) * 128],
                                           o_all[:, kc * R + g * G:kc * R + (g + 1) * G],
                                           start=(kc == 0), stop=(kc == 7))
                        ins.then_inc(wosem, 1)
                @blk.scalar
                def _(s):
                    for oc in range(NIC):
                        s.wait_ge(wosem, won[0] + 2 * oc + 1)
                        if oc >= 2:
                            s.wait_ge(woe, won[1] + oc - 1)
                        s.activation(scr5[:, (oc % 2) * G:(oc % 2 + 1) * G],
                                     pwo[:, (oc % 2) * G:(oc % 2 + 1) * G],
                                     AF.Identity, bias=boutc[:, oc:oc + 1]).then_inc(wosem, 1)
                @blk.vector
                def _(v):
                    for oc in range(NIC):
                        v.wait_ge(wosem, won[0] + 2 * oc + 2)
                        v.tensor_tensor(osb[:, oc * G:(oc + 1) * G],
                                        scr5[:, (oc % 2) * G:(oc % 2 + 1) * G],
                                        xg[:, oc * G:(oc + 1) * G], ALU.add).then_inc(woe, 1)
            won[0] += 2 * NIC
            won[1] += NIC
            with nc.Block() as blk:
                @blk.sync
                def _(sy):
                    LD(sy, sy.dma_start(out=io["xout"][:, g * G:(g + 1) * G].rearrange(
                                            "(ic p) t -> p ic t", p=128),
                                        in_=osb[:, :].rearrange("p (ic t) -> p ic t", ic=8)))
    sb.close()


def build_C(_unused=None):
    nc = bass.Bass("TRN2", target_bir_lowering=False, detect_race_conditions=False)
    io = declare_consts(nc)
    io["xin"] = nc.dram_tensor("xin", [D, R], F32, kind="ExternalInput").ap()
    io["wqkv"] = nc.dram_tensor("wqkv", [D, 3072], F32, kind="ExternalInput").ap()
    io["wqkvT"] = nc.dram_tensor("wqkvT", [3072, D], F32, kind="ExternalInput").ap()
    io["proj"] = nc.dram_tensor("proj", [16, 64, 64], F32, kind="ExternalInput").ap()
    io["wout"] = nc.dram_tensor("wout", [D, D], F32, kind="ExternalInput").ap()
    io["bqkvC"] = nc.dram_tensor("bqkvC", [64, 48], F32, kind="ExternalInput").ap()
    io["boutC"] = nc.dram_tensor("boutC", [128, 8], F32, kind="ExternalInput").ap()
    io["maskin"] = nc.dram_tensor("maskin", [128, 8], F32, kind="ExternalInput").ap()
    io["xout"] = nc.dram_tensor("xout", [D, R], F32, kind="ExternalOutput").ap()
    io["cc_inC"] = nc.dram_tensor("cc_inC", [1024, 65], F32).ap()
    io["cc_outC"] = nc.dram_tensor("cc_outC", [8192, 65], F32).ap()
    with ExitStack() as c:
        consts = load_consts(nc, c, io)
        consts["vsn"] = VSync(c, nc, "c_vs")
        emit_blockC(nc, c, io, consts)
    return nc


def dC_host_inputs(inputs):
    m = {}
    m["wqkv"] = f32c(inputs["L1_attn_w_qkv"])
    m["wqkvT"] = f32c(np.asarray(inputs["L1_attn_w_qkv"]).T)
    m["proj"] = f32c(inputs["L1_attn_proj"])
    m["wout"] = f32c(inputs["L1_attn_w_out"])
    m["bqkvC"] = f32c(np.asarray(inputs["L1_attn_b_qkv"]).reshape(48, 64).T)
    m["boutC"] = f32c(np.asarray(inputs["L1_attn_b_out"]).reshape(8, 128).T)
    return m


def run_C(x2_fm_shards, inputs):
    nc = build_C()
    ca = const_arrays()
    dc = dC_host_inputs(inputs)
    mks = core_masks()
    in_maps = []
    for i in range(NCORES):
        m = dict(ca)
        m.update(dc)
        m["xin"] = f32c(x2_fm_shards[i])
        m["maskin"] = mks[i]
        in_maps.append(m)
    res = run_bass_kernel_spmd(nc, in_maps, core_ids=list(range(NCORES)))
    return [np.asarray(r["xout"], dtype=np.float32) for r in res.results]



def emit_ffn(nc, c, sems, name, kind, wins, src_sb, src_cols, src_goffs,
             noc, nic, dst_sb=None, dst_cols=None, dst_goffs=None,
             scol1=None, scol2=None, xin_ap=None, out_ap=None, out_rows0=0,
             tok_offs=None, x_sb=None, plan=None, src_ready=None,
             bufc=None, shared_psum=None):
    ng = len(src_goffs)
    if dst_goffs is None:
        dst_goffs = src_goffs
    nw = len(wins)
    own = plan is None
    if own:
        plan = Plan()
    local = ExitStack()
    cc = bufc if bufc is not None else local
    if True:
        wb_ = [[cc.enter_context(nc.sbuf_tensor(f"{name}_w{j}_{i}", [128, nic * 128], BF16))
                for i in range(2)] for j in range(nw)]
        if shared_psum is None:
            pg = cc.enter_context(nc.psum_tensor(name + "_pg", [128, 2 * ng * G], F32))
            pu = cc.enter_context(nc.psum_tensor(name + "_pu", [128, 2 * ng * G], F32)) if kind == 'silu2' else None
            bank_user = {0: None, 1: None}
        else:
            pg, pu, bank_user = shared_psum
        stmp = cc.enter_context(nc.sbuf_tensor(name + "_st", [128, 2 * ng * G], BF16)) if kind == 'silu2' else None
        if kind == 'w3add':
            xbuf_ = cc.enter_context(nc.sbuf_tensor(name + "_xb", [128, ng * G], F32))
            xbuf = [xbuf_, xbuf_]
            osb = [cc.enter_context(nc.sbuf_tensor(f"{name}_ob{i}", [128, ng * G], F32)) for i in range(2)]
        mm_r = [None] * noc
        ev_r = [None] * noc
        st_r = [None] * noc
        lx_r = [None] * noc
        for u in range(noc):
            def mk_ldw(u=u):
                def f(gg, sem):
                    for j in range(nw):
                        gg.dma_start(
                            out=wb_[j][u % 2][:, :].rearrange("p (ic f) -> p ic f", ic=nic),
                            in_=wins[j][:, u * 128:(u + 1) * 128].rearrange("(ic p) f -> p ic f", p=128)
                        ).then_inc(sem, 16)
                return f
            ldw = plan.add_n("gp", mk_ldw(), [mm_r[u - 2] if u >= 2 else None],
                             inc_total=16 * nw)
            def mk_mm(u=u):
                def f(t):
                    bs = (u % 2) * ng * G
                    for j in range(nw):
                        ptgt = pg if j == 0 else pu
                        for ic in range(nic):
                            for g in range(ng):
                                ins = t.matmul(ptgt[:, bs + g * G: bs + (g + 1) * G],
                                               wb_[j][u % 2][:, ic * 128:(ic + 1) * 128],
                                               src_sb[:, ic * src_cols + src_goffs[g]: ic * src_cols + src_goffs[g] + G],
                                               start=(ic == 0), stop=(ic == nic - 1))
                    return ins
                return f
            mdeps = [ldw, bank_user[u % 2]]
            if u == 0 and src_ready is not None:
                mdeps.append(src_ready)
            mm_r[u] = plan.add_n("pe", mk_mm(), mdeps)
            bs = (u % 2) * ng * G
            if kind == 'silu2':
                def mk_ea(u=u, bs=bs):
                    def f(s):
                        for g in range(ng):
                            kw = {} if scol1 is None else {"scale": scol1[:, u:u + 1]}
                            ins = s.activation(stmp[:, bs + g * G: bs + (g + 1) * G],
                                               pg[:, bs + g * G: bs + (g + 1) * G],
                                               AF.Silu, **kw)
                        return ins
                    return f
                ea = plan.add_n("act", mk_ea(), [mm_r[u], ev_r[u - 2] if u >= 2 else None])
                def mk_ed(u=u, bs=bs):
                    def f(v):
                        for g in range(ng):
                            sc = 1.0 if scol2 is None else scol2[:, u:u + 1]
                            ins = v.scalar_tensor_tensor(
                                dst_sb[:, u * dst_cols + dst_goffs[g]: u * dst_cols + dst_goffs[g] + G],
                                pu[:, bs + g * G: bs + (g + 1) * G], sc,
                                stmp[:, bs + g * G: bs + (g + 1) * G],
                                ALU.mult, ALU.mult)
                        return ins
                    return f
                ev_r[u] = plan.add_n("dve", mk_ed(), [ea])
                bank_user[u % 2] = ev_r[u]
            elif kind == 'gelu':
                def mk_ea(u=u, bs=bs):
                    def f(s):
                        for g in range(ng):
                            ins = s.activation(
                                dst_sb[:, u * dst_cols + dst_goffs[g]: u * dst_cols + dst_goffs[g] + G],
                                pg[:, bs + g * G: bs + (g + 1) * G],
                                AF.Gelu, scale=scol1[:, u:u + 1])
                        return ins
                    return f
                ev_r[u] = plan.add_n("act", mk_ea(), [mm_r[u]])
                bank_user[u % 2] = ev_r[u]
            elif kind == 'addx':
                def mk_ed(u=u, bs=bs):
                    def f(v):
                        for g in range(ng):
                            xsl = x_sb[:, u * R + dst_goffs[g]: u * R + dst_goffs[g] + G]
                            ins = v.tensor_tensor(xsl, xsl, pg[:, bs + g * G: bs + (g + 1) * G],
                                                  ALU.add)
                        return ins
                    return f
                ev_r[u] = plan.add_n("dve", mk_ed(), [mm_r[u]])
                bank_user[u % 2] = ev_r[u]
            elif kind == 'w3add':
                def mk_lx(u=u):
                    def f(sy, sem):
                        sy.dma_start(out=xbuf[u % 2][:, :],
                                     in_=xin_ap[u * 128:(u + 1) * 128,
                                                tok_offs:tok_offs + ng * G]).then_inc(sem, 16)
                    return f
                lx_r[u] = plan.add_n("sy", mk_lx(), [ev_r[u - 1] if u >= 1 else None],
                                     inc_total=16, chan="syl")
                def mk_ed(u=u, bs=bs):
                    def f(v):
                        for g in range(ng):
                            v.tensor_scalar(osb[u % 2][:, g * G:(g + 1) * G],
                                            pg[:, bs + g * G: bs + (g + 1) * G],
                                            scol1[:, u:u + 1], None, ALU.mult)
                            ins = v.tensor_tensor(osb[u % 2][:, g * G:(g + 1) * G],
                                                  osb[u % 2][:, g * G:(g + 1) * G],
                                                  xbuf[u % 2][:, g * G:(g + 1) * G], ALU.add)
                        return ins
                    return f
                ev_r[u] = plan.add_n("dve", mk_ed(), [mm_r[u], lx_r[u],
                                                      st_r[u - 2] if u >= 2 else None])
                def mk_st(u=u):
                    def f(sy, sem):
                        sy.dma_start(out=out_ap[u * 128:(u + 1) * 128,
                                                tok_offs:tok_offs + ng * G],
                                     in_=osb[u % 2][:, :]).then_inc(sem, 16)
                    return f
                st_r[u] = plan.add_n("sy", mk_st(), [ev_r[u]], inc_total=16, chan="sys")
                bank_user[u % 2] = ev_r[u]
        last = st_r[noc - 1] if kind == 'w3add' else ev_r[noc - 1]
        if own:
            plan.emit(nc, c, name)
    local.close()
    return last



def emit_blockA(nc, c, io, consts):
    sb = ExitStack()
    H, NCH = 8, 16
    ones_col, ones_row = consts["ones_col"], consts["ones_row"]
    eps_cols, ident_bf, triu = consts["eps_cols"], consts["ident_bf"], consts["triu"]
    vsn = consts["vsn"]

    rows_s1 = sb.enter_context(nc.sbuf_tensor("a_rs1", [1, R], F32))
    rows_c1 = sb.enter_context(nc.sbuf_tensor("a_rc1", [1, R], F32))
    gw = sb.enter_context(nc.sbuf_tensor("a_gw", [128, 17], F32))
    mcols = sb.enter_context(nc.sbuf_tensor("a_mcols", [128, 8], F32))
    ld = c.enter_context(nc.semaphore("a_ld"))
    ldn = [0]
    gld = c.enter_context(nc.semaphore("a_gld"))
    gldn = [0]

    def LD(eng, *dmas):
        for d_ in dmas:
            d_.then_inc(ld, 16)
            ldn[0] += 16
        eng.wait_ge(ld, ldn[0])

    def GLD(eng, *dmas):
        for d_ in dmas:
            d_.then_inc(gld, 16)
            gldn[0] += 16
        eng.wait_ge(gld, gldn[0])

    with nc.Block() as blk:
        @blk.sync
        def _(sy):
            LD(sy, sy.dma_start(out=gw[:, :], in_=io["gwA"]),
               sy.dma_start(out=mcols[:, :], in_=io["maskin"]))
    wcolA, wcolM, w128 = gw[:, 0:8], gw[:, 8:16], gw[:, 16:17]

    with ExitStack() as att:
        h_attn = att.enter_context(nc.sbuf_tensor("a_h", [128, NIC * R], BF16))
        with ExitStack() as lnx:
            x_sb = lnx.enter_context(nc.sbuf_tensor("a_x", [128, NIC * R], F32))
            gwrow = lnx.enter_context(nc.sbuf_tensor("a_gwrow", [1, D], F32))
            with nc.Block() as blk:
                @blk.sync
                def _(sy):
                    LD(sy, sy.dma_start(out=x_sb[:, :].rearrange("p (ic t) -> p ic t", ic=NIC),
                                        in_=io["xin"].rearrange("(ic p) t -> p ic t", p=128)),
                       sy.dma_start(out=gwrow[:, :], in_=io["wrowA"]))
            emit_ln_zn(nc, c, vsn, x_sb, h_attn, ones_col, ones_row, 'lnrms',
                       eps1_col=eps_cols[:, 0:1], eps2_col=eps_cols[:, 0:1], name="A",
                       gain_rowtile=gwrow, gain_coltile=wcolA,
                       rows_s1=rows_s1, rows_c1=rows_c1,
                       ones_col_bf=consts["ones_col_bf"])

        ktm = att.enter_context(nc.sbuf_tensor("a_ktm", [128, H * R], BF16))
        vtm = att.enter_context(nc.sbuf_tensor("a_vtm", [128, H * R], BF16))
        contrib = att.enter_context(nc.sbuf_tensor("a_contrib", [128, H * 128], F32))
        S_g = att.enter_context(nc.sbuf_tensor("a_sg", [128, H * 128], F32))
        with ExitStack() as a2c:
            wsl = a2c.enter_context(nc.sbuf_tensor("a_wsl", [128, 2 * NIC * 128], BF16))
            fm_scr = a2c.enter_context(nc.sbuf_tensor("a_fmscr", [128, R], BF16))
            pk = a2c.enter_context(nc.psum_tensor("a_pk", [128, NG * G], F32))
            ptp = a2c.enter_context(nc.psum_tensor("a_ptp", [128, NCH * 128], BF16))
            pst = a2c.enter_context(nc.psum_tensor("a_pst", [128, 128], F32))
            a2p = Plan()
            prev_vmm = prev_vevac = prev_ktr = prev_vtr = None
            prev_kcopy = prev_vcopy = prev_scon = None
            for h in range(H):
                def mk_ld(h=h):
                    def f(gg, sem):
                        gg.dma_start(
                            out=wsl[:, 0:NIC * 128].rearrange("p (ic f) -> p ic f", ic=NIC),
                            in_=io["wk"][:, h * 128:(h + 1) * 128].rearrange("(ic p) f -> p ic f", p=128)
                        ).then_inc(sem, 16)
                        gg.dma_start(
                            out=wsl[:, NIC * 128:].rearrange("p (ic f) -> p ic f", ic=NIC),
                            in_=io["wv"][:, h * 128:(h + 1) * 128].rearrange("(ic p) f -> p ic f", p=128)
                        ).then_inc(sem, 16)
                    return f
                ldr = a2p.add_n("gp", mk_ld(), [prev_vmm], inc_total=32)
                def mk_kvmm(h=h, kv=0):
                    def f(t):
                        for g in range(NG):
                            for ic in range(NIC):
                                ins = t.matmul(pk[:, g * G:(g + 1) * G],
                                               wsl[:, kv * NIC * 128 + ic * 128: kv * NIC * 128 + (ic + 1) * 128],
                                               h_attn[:, ic * R + g * G: ic * R + (g + 1) * G],
                                               start=(ic == 0), stop=(ic == NIC - 1))
                        return ins
                    return f
                kmm = a2p.add_n("pe", mk_kvmm(h, 0), [ldr, prev_vevac])
                def mk_evac(h=h):
                    def f(v):
                        for g in range(NG):
                            ins = v.tensor_copy(fm_scr[:, g * G:(g + 1) * G], pk[:, g * G:(g + 1) * G])
                        return ins
                    return f
                kevac = a2p.add_n("dve", mk_evac(), [kmm, prev_ktr])
                def mk_tr(h=h):
                    def f(t):
                        for c2 in range(NCH):
                            ins = t.transpose(ptp[:, c2 * 128:(c2 + 1) * 128],
                                              fm_scr[:, c2 * 128:(c2 + 1) * 128], ident_bf[:, :])
                        return ins
                    return f
                ktr = a2p.add_n("pe", mk_tr(), [kevac, prev_vcopy])
                def mk_copy(h=h, dstsel=0):
                    def f(v):
                        dst = ktm if dstsel == 0 else vtm
                        return v.tensor_copy(dst[:, h * R:(h + 1) * R], ptp[:, :])
                    return f
                kcopy = a2p.add_n("dve", mk_copy(h, 0), [ktr])
                vmm = a2p.add_n("pe", mk_kvmm(h, 1), [kevac])
                vevac = a2p.add_n("dve", mk_evac(), [vmm, ktr])
                vtr = a2p.add_n("pe", mk_tr(), [vevac, kcopy])
                vcopy = a2p.add_n("dve", mk_copy(h, 1), [vtr])
                def mk_smm(h=h):
                    def f(t):
                        for c2 in range(NCH):
                            ins = t.matmul(pst[:, :], ktm[:, h * R + c2 * 128: h * R + (c2 + 1) * 128],
                                           vtm[:, h * R + c2 * 128: h * R + (c2 + 1) * 128],
                                           start=(c2 == 0), stop=(c2 == NCH - 1))
                        return ins
                    return f
                smm = a2p.add_n("pe", mk_smm(), [kcopy, vcopy, prev_scon])
                def mk_scon(h=h):
                    def f(v):
                        return v.tensor_copy(contrib[:, h * 128:(h + 1) * 128], pst[:, :])
                    return f
                scon = a2p.add_n("dve", mk_scon(), [smm])
                prev_vmm, prev_vevac, prev_ktr, prev_vtr = vmm, vevac, ktr, vtr
                prev_kcopy, prev_vcopy, prev_scon = kcopy, vcopy, scon
            a2p.emit(nc, c, "a2p")

        ccs = c.enter_context(nc.semaphore("a_ccs"))
        with nc.Block() as blk:
            @blk.sync
            def _(sy):
                LD(sy, sy.dma_start(out=io["cc_in"].rearrange("(h p) d -> p h d", p=128),
                                    in_=contrib[:, :].rearrange("p (h d) -> p h d", h=H)))
        with nc.Block() as blk:
            @blk.gpsimd
            def _(gg):
                gg.collective_compute("AllGather", ALU.bypass,
                                      replica_groups=[list(range(8))],
                                      ins=[io["cc_in"].opt()],
                                      outs=[io["cc_out"].opt()]).then_inc(ccs, 1)
                gg.wait_ge(ccs, 1)
        with ExitStack() as gsc:
            gath = gsc.enter_context(nc.sbuf_tensor("a_gath", [128, 64 * 128], F32))
            with nc.Block() as blk:
                @blk.gpsimd
                def _(gg):
                    GLD(gg, gg.dma_start(out=gath[:, :].rearrange("p (q d) -> p q d", q=64),
                                         in_=io["cc_out"].rearrange("(q p) d -> p q d", p=128)))
            with nc.Block() as blk:
                @blk.vector
                def _(v):
                    v.memset(S_g[:, :], 0.0)
                    for r in range(8):
                        v.tensor_scalar(gath[:, r * 1024:(r + 1) * 1024],
                                        gath[:, r * 1024:(r + 1) * 1024],
                                        mcols[:, r:r + 1], None, ALU.mult)
                        v.tensor_tensor(S_g[:, :], S_g[:, :], gath[:, r * 1024:(r + 1) * 1024], ALU.add)

        pe_a = c.enter_context(nc.semaphore("a_pea"))
        pe_os = c.enter_context(nc.semaphore("a_peos"))
        act_oc = c.enter_context(nc.semaphore("a_actoc"))
        dve_am = c.enter_context(nc.semaphore("a_dveam"))
        dve_s = c.enter_context(nc.semaphore("a_dves"))
        with ExitStack() as a3c:
            q_bf = a3c.enter_context(nc.sbuf_tensor("a_q", [128, R], BF16))
            kfm = a3c.enter_context(nc.sbuf_tensor("a_kfm", [128, R], BF16))
            o_f32 = a3c.enter_context(nc.sbuf_tensor("a_of", [128, R], F32))
            o_nb = a3c.enter_context(nc.sbuf_tensor("a_onb", [128, R], BF16))
            wsl3 = a3c.enter_context(nc.sbuf_tensor("a_wsl3", [128, NIC * 128], BF16))
            S_sb = a3c.enter_context(nc.sbuf_tensor("a_ssb", [128, 128], F32))
            S_bf = a3c.enter_context(nc.sbuf_tensor("a_sbf", [128, 128], BF16))
            am_sb = a3c.enter_context(nc.sbuf_tensor("a_am", [128, 2 * 128], BF16))
            nr_m = a3c.enter_context(nc.sbuf_tensor("a_nrm", [1, R], F32))
            nr_s = a3c.enter_context(nc.sbuf_tensor("a_nrs", [1, R], F32))
            pa = a3c.enter_context(nc.psum_tensor("a_pa", [128, 2 * G], F32))
            po = a3c.enter_context(nc.psum_tensor("a_po", [128, 2 * G], F32))
            ps = a3c.enter_context(nc.psum_tensor("a_ps", [128, 2 * G], F32))
            ptf = a3c.enter_context(nc.psum_tensor("a_ptf", [128, 1024], BF16))
            sq_scr = a3c.enter_context(nc.sbuf_tensor("a_sqscr", [128, 2 * G], F32))
            rb_scr = a3c.enter_context(nc.sbuf_tensor("a_rbscr", [128, G], F32))
            pnq = a3c.enter_context(nc.psum_tensor("a_pnq", [128, G], F32))
            a3p = Plan()
            prev_qmm = prev_mask15 = prev_ocopy15 = prev_omm15 = None
            prev_supd15 = prev_ktc1 = prev_spill = prev_rbc3 = None
            for h in range(H):
                def mk_ldq(h=h):
                    def f(gg, sem):
                        gg.dma_start(
                            out=wsl3[:, :].rearrange("p (ic f) -> p ic f", ic=NIC),
                            in_=io["wq"][:, h * 128:(h + 1) * 128].rearrange("(ic p) f -> p ic f", p=128)
                        ).then_inc(sem, 16)
                    return f
                ldq = a3p.add_n("gp", mk_ldq(), [prev_qmm], inc_total=16)
                qev = None
                for gp in range(2):
                    def mk_qmm(h=h, gp=gp):
                        def f(t):
                            for g2 in range(2):
                                g = gp * 2 + g2
                                ptgt = pa if g2 == 0 else po
                                for ic in range(NIC):
                                    ins = t.matmul(ptgt[:, 0:G],
                                                   wsl3[:, ic * 128:(ic + 1) * 128],
                                                   h_attn[:, ic * R + g * G: ic * R + (g + 1) * G],
                                                   start=(ic == 0), stop=(ic == NIC - 1))
                            return ins
                        return f
                    deps = [ldq] if gp == 0 else [qev]
                    if gp == 0:
                        deps += [prev_mask15, prev_ocopy15, prev_rbc3]
                    qmm = a3p.add_n("pe", mk_qmm(), deps)
                    def mk_qev(h=h, gp=gp):
                        def f(v):
                            for g2 in range(2):
                                g = gp * 2 + g2
                                ptgt = pa if g2 == 0 else po
                                ins = v.tensor_copy(q_bf[:, g * G:(g + 1) * G], ptgt[:, 0:G])
                            return ins
                        return f
                    qev = a3p.add_n("dve", mk_qev(), [qmm])
                prev_qmm = qmm
                ktcs = []
                tprev = prev_ktc1
                for ph in range(2):
                    def mk_ktf(h=h, ph=ph):
                        def f(t):
                            for c2 in range(8):
                                cc2 = ph * 8 + c2
                                ins = t.transpose(ptf[:, c2 * 128:(c2 + 1) * 128],
                                                  ktm[:, h * R + cc2 * 128: h * R + (cc2 + 1) * 128],
                                                  ident_bf[:, :])
                            return ins
                        return f
                    ktf = a3p.add_n("pe", mk_ktf(), [tprev])
                    def mk_ktc(h=h, ph=ph):
                        def f(v):
                            return v.tensor_copy(kfm[:, ph * 1024:(ph + 1) * 1024], ptf[:, :])
                        return f
                    tprev = a3p.add_n("dve", mk_ktc(), [ktf])
                    ktcs.append(tprev)
                prev_ktc1 = ktcs[1]
                def mk_init(h=h):
                    def f(v):
                        v.tensor_copy(S_sb[:, :], S_g[:, h * 128:(h + 1) * 128])
                        return v.tensor_copy(S_bf[:, :], S_g[:, h * 128:(h + 1) * 128])
                    return f
                init = a3p.add_n("dve", mk_init(), [prev_omm15])
                mmA = [None] * NCH
                mask = [None] * NCH
                omm = [None] * NCH
                ocopy = [None] * NCH
                supd = [None] * NCH
                for c2 in range(NCH):
                    pao = (c2 % 2) * G
                    def mk_mmA(h=h, c2=c2, pao=pao):
                        def f(t):
                            return t.matmul(pa[:, pao:pao + 128],
                                            kfm[:, c2 * 128:(c2 + 1) * 128],
                                            q_bf[:, c2 * 128:(c2 + 1) * 128],
                                            start=True, stop=True)
                        return f
                    deps = [qev, ktcs[c2 // 8]]
                    if c2 >= 2:
                        deps.append(mask[c2 - 2])
                    mmA[c2] = a3p.add_n("pe", mk_mmA(), deps)
                    def mk_mask(h=h, c2=c2, pao=pao):
                        def f(v):
                            return v.tensor_tensor(am_sb[:, (c2 % 2) * 128:(c2 % 2) * 128 + 128],
                                                   pa[:, pao:pao + 128], triu[:, :], ALU.mult)
                        return f
                    mask[c2] = a3p.add_n("dve", mk_mask(),
                                         [mmA[c2], omm[c2 - 2] if c2 >= 2 else prev_omm15])
                    def mk_omm(h=h, c2=c2, pao=pao):
                        def f(t):
                            t.matmul(po[:, pao:pao + 128],
                                     vtm[:, h * R + c2 * 128: h * R + (c2 + 1) * 128],
                                     am_sb[:, (c2 % 2) * 128:(c2 % 2) * 128 + 128],
                                     start=True, stop=False)
                            t.matmul(po[:, pao:pao + 128], S_bf[:, :],
                                     q_bf[:, c2 * 128:(c2 + 1) * 128],
                                     start=False, stop=True)
                            return t.matmul(ps[:, pao:pao + 128],
                                            ktm[:, h * R + c2 * 128: h * R + (c2 + 1) * 128],
                                            vtm[:, h * R + c2 * 128: h * R + (c2 + 1) * 128],
                                            start=True, stop=True)
                        return f
                    odeps = [mask[c2], supd[c2 - 1] if c2 >= 1 else init]
                    if c2 >= 2:
                        odeps += [ocopy[c2 - 2], supd[c2 - 2]]
                    else:
                        odeps += [qev, prev_supd15]
                    omm[c2] = a3p.add_n("pe", mk_omm(), odeps)
                    def mk_ocopy(h=h, c2=c2, pao=pao):
                        def f(s):
                            return s.activation(o_f32[:, c2 * 128:(c2 + 1) * 128],
                                                po[:, pao:pao + 128], AF.Copy)
                        return f
                    ocopy[c2] = a3p.add_n("act", mk_ocopy(), [omm[c2]])
                    def mk_supd(h=h, c2=c2, pao=pao):
                        def f(v):
                            v.scalar_tensor_tensor(S_bf[:, :], ps[:, pao:pao + 128], 1.0,
                                                   S_sb[:, :], ALU.mult, ALU.add)
                            return v.tensor_tensor(S_sb[:, :], S_sb[:, :],
                                                   ps[:, pao:pao + 128], ALU.add)
                        return f
                    supd[c2] = a3p.add_n("dve", mk_supd(), [omm[c2]])
                prev_mask15 = mask[NCH - 1]
                prev_ocopy15 = ocopy[NCH - 1]
                prev_omm15 = omm[NCH - 1]
                prev_supd15 = supd[NCH - 1]
                sqs_ = [None] * NG
                summ_ = [None] * NG
                rowm_ = [None] * NG
                for g in range(NG):
                    def mk_sq(h=h, g=g):
                        def f(s):
                            return s.activation(sq_scr[:, (g % 2) * G:(g % 2) * G + G],
                                                o_f32[:, g * G:(g + 1) * G], AF.Square)
                        return f
                    sdeps = [ocopy[NCH - 1]]
                    if g >= 2:
                        sdeps.append(summ_[g - 2])
                    sqs_[g] = a3p.add_n("act", mk_sq(), sdeps)
                    def mk_summ(h=h, g=g):
                        def f(t):
                            return t.matmul(pnq[0:1, 0:G], ones_col[:, :],
                                            sq_scr[:, (g % 2) * G:(g % 2) * G + G],
                                            start=True, stop=True)
                        return f
                    smdeps = [sqs_[g]]
                    if g >= 1:
                        smdeps.append(rowm_[g - 1])
                    summ_[g] = a3p.add_n("pe", mk_summ(), smdeps)
                    def mk_rowm(h=h, g=g):
                        def f(v):
                            return v.tensor_scalar(nr_m[0:1, g * G:(g + 1) * G], pnq[0:1, 0:G],
                                                   1.0 / 128.0, None, ALU.mult)
                        return f
                    rowm_[g] = a3p.add_n("dve", mk_rowm(), [summ_[g]])
                def mk_sqr(h=h):
                    def f(s):
                        return s.activation(nr_s[0:1, :], nr_m[0:1, :], AF.Sqrt,
                                            bias=eps_cols[0:1, 0:1])
                    return f
                sqr = a3p.add_n("act", mk_sqr(), [rowm_[NG - 1]])
                def mk_rcp(h=h):
                    def f(v):
                        return v.reciprocal(nr_s[0:1, :], nr_s[0:1, :])
                    return f
                rcp = a3p.add_n("dve", mk_rcp(), [sqr])
                rbc = None
                onl = None
                for g in range(NG):
                    def mk_bcast(h=h, g=g):
                        def f(t):
                            return t.matmul(pa[:, 0:G], ones_row[:, :],
                                            nr_s[0:1, g * G:(g + 1) * G], start=True, stop=True)
                        return f
                    bdeps = [rcp, mask[NCH - 1]]
                    if rbc is not None:
                        bdeps.append(rbc)
                    bc = a3p.add_n("pe", mk_bcast(), bdeps)
                    def mk_rbc(h=h, g=g):
                        def f(s):
                            return s.activation(rb_scr[:, :], pa[:, 0:G], AF.Copy)
                        return f
                    rbc = a3p.add_n("act", mk_rbc(), [bc])
                    def mk_on(h=h, g=g):
                        def f(v):
                            v.tensor_tensor(sq_scr[:, 0:G], o_f32[:, g * G:(g + 1) * G],
                                            rb_scr[:, :], ALU.mult)
                            return v.tensor_scalar(o_nb[:, g * G:(g + 1) * G], sq_scr[:, 0:G],
                                                   w128[:, 0:1], None, ALU.mult)
                        return f
                    onl = a3p.add_n("dve", mk_on(), [rbc, prev_spill])
                def mk_spill(h=h):
                    def f(sy, sem):
                        sy.dma_start(out=io["o_dram"][h * 128:(h + 1) * 128, :],
                                     in_=o_nb[:, :]).then_inc(sem, 16)
                    return f
                prev_spill = a3p.add_n("sy", mk_spill(), [onl], inc_total=16)
                prev_rbc3 = rbc
            a3p.emit(nc, c, "a3p")

    u_sb = sb.enter_context(nc.sbuf_tensor("a_u", [128, NIC * R], F32))
    a4f = c.enter_context(nc.semaphore("a_a4f"))
    a4e = c.enter_context(nc.semaphore("a_a4e"))
    with ExitStack() as a4c:
        wo_sb = a4c.enter_context(nc.sbuf_tensor("a_wo", [128, NIC * 1024], BF16))
        og = a4c.enter_context(nc.sbuf_tensor("a_og", [128, 8 * G], BF16))
        xg = a4c.enter_context(nc.sbuf_tensor("a_xg", [128, 8 * G], F32))
        scr4 = a4c.enter_context(nc.sbuf_tensor("a_scr4", [128, G], F32))
        pR = a4c.enter_context(nc.psum_tensor("a_pR", [128, G], F32))
        pC = a4c.enter_context(nc.psum_tensor("a_pC", [128, G], F32))
        pwo = a4c.enter_context(nc.psum_tensor("a_pwo", [128, 2 * G], F32))
        with nc.Block() as blk:
            @blk.gpsimd
            def _(gg):
                GLD(gg, gg.dma_start(out=wo_sb[:, :].rearrange("p (ic f) -> p ic f", ic=NIC),
                                     in_=io["wo"].rearrange("(ic p) f -> p ic f", p=128)))
        a4p = Plan()
        dve_last = None
        bank_user = {0: None, 1: None}
        for g in range(NG):
            def mk_ldog(g=g):
                def f(sy, sem):
                    sy.dma_start(out=og[:, :].rearrange("p (kc t) -> p kc t", kc=8),
                                 in_=io["o_dram"][:, g * G:(g + 1) * G].rearrange(
                                     "(kc p) t -> p kc t", p=128)).then_inc(sem, 16)
                    sy.dma_start(out=xg[:, :].rearrange("p (ic t) -> p ic t", ic=8),
                                 in_=io["xin"][:, g * G:(g + 1) * G].rearrange(
                                     "(ic p) t -> p ic t", p=128)).then_inc(sem, 16)
                return f
            ldog = a4p.add_n("sy", mk_ldog(), [dve_last], inc_total=32)
            def mk_bc(g=g):
                def f(t):
                    t.matmul(pR[:, :], ones_row[:, :], rows_s1[0:1, g * G:(g + 1) * G],
                             start=True, stop=True)
                    return t.matmul(pC[:, :], ones_row[:, :], rows_c1[0:1, g * G:(g + 1) * G],
                                    start=True, stop=True)
                return f
            bc = a4p.add_n("pe", mk_bc(), [dve_last])
            for oc in range(NIC):
                def mk_wmm(g=g, oc=oc):
                    def f(t):
                        for kc in range(8):
                            ins = t.matmul(pwo[:, (oc % 2) * G:(oc % 2 + 1) * G],
                                           wo_sb[:, kc * 1024 + oc * 128: kc * 1024 + (oc + 1) * 128],
                                           og[:, kc * G:(kc + 1) * G],
                                           start=(kc == 0), stop=(kc == 7))
                        return ins
                    return f
                wdeps = [ldog, bank_user[oc % 2]]
                wmm = a4p.add_n("pe", mk_wmm(), wdeps)
                def mk_udve(g=g, oc=oc):
                    def f(v):
                        v.tensor_tensor(scr4[:, :], xg[:, oc * G:(oc + 1) * G], pR[:, :], ALU.mult)
                        v.tensor_tensor(scr4[:, :], scr4[:, :], pC[:, :], ALU.add)
                        return v.tensor_tensor(u_sb[:, oc * R + g * G: oc * R + (g + 1) * G],
                                               scr4[:, :], pwo[:, (oc % 2) * G:(oc % 2 + 1) * G],
                                               ALU.add)
                    return f
                bank_user[oc % 2] = a4p.add_n("dve", mk_udve(), [wmm, bc])
            dve_last = bank_user[(NIC - 1) % 2]
        a4p.emit(nc, c, "a4p")
    with ExitStack() as a5c:
        h2 = sb.enter_context(nc.sbuf_tensor("a_h2", [128, NIC * R], BF16))
        emit_ln_zn(nc, c, vsn, u_sb, h2, ones_col, ones_row, 'rms',
                   eps1_col=eps_cols[:, 0:1], nic=NIC, xcols=R, width=1024, name="AU",
                   rms_gain=True, gain_coltile=wcolM, ones_col_bf=consts["ones_col_bf"])
        xg2 = a5c.enter_context(nc.sbuf_tensor("a_xg2", [128, 2 * R], F32))
        a5p = Plan()
        addr = {0: None, 1: None}
        for oc in range(NIC):
            def mk_ldx(oc=oc):
                def f(sy, sem):
                    sy.dma_start(out=xg2[:, (oc % 2) * R:(oc % 2 + 1) * R],
                                 in_=io["xin"][oc * 128:(oc + 1) * 128, :]).then_inc(sem, 16)
                return f
            ldx = a5p.add_n("sy", mk_ldx(), [addr[oc % 2]], inc_total=16)
            def mk_addx(oc=oc):
                def f(v):
                    for g in range(NG):
                        sl = slice(oc * R + g * G, oc * R + (g + 1) * G)
                        ins = v.tensor_tensor(u_sb[:, sl], u_sb[:, sl],
                                              xg2[:, (oc % 2) * R + g * G:(oc % 2) * R + (g + 1) * G],
                                              ALU.add)
                    return ins
                return f
            addr[oc % 2] = a5p.add_n("dve", mk_addx(), [ldx])
        a5p.emit(nc, c, "a5p")

    sems = None
    for half in range(2):
        toff = half * (R // 2)
        goffs = [toff, toff + G]
        hgoffs = [0, G]
        with ExitStack() as hc:
            m_sb = hc.enter_context(nc.sbuf_tensor(f"a_m{half}", [128, 32 * (R // 2)], BF16))
            aplan = Plan()
            pgS = hc.enter_context(nc.psum_tensor(f"a_pgS{half}", [128, 4 * G], F32))
            puS = hc.enter_context(nc.psum_tensor(f"a_puS{half}", [128, 4 * G], F32))
            spsum = (pgS, puS, {0: None, 1: None})
            r1 = emit_ffn(nc, c, sems, f"am12h{half}", 'silu2', (io["wgate"], io["wup"]),
                          h2, R, goffs, 32, NIC, dst_sb=m_sb, dst_cols=R // 2,
                          dst_goffs=hgoffs, plan=aplan, bufc=hc, shared_psum=spsum)
            emit_ffn(nc, c, sems, f"amdnh{half}", 'addx', (io["wdown"],),
                     m_sb, R // 2, hgoffs, NIC, 32, x_sb=u_sb, dst_goffs=goffs,
                     plan=aplan, bufc=hc, shared_psum=spsum, src_ready=r1)
            aplan.emit(nc, c, f"aph{half}")

    with nc.Block() as blk:
        @blk.sync
        def _(sy):
            LD(sy, sy.dma_start(out=io["xout"].rearrange("(ic p) t -> p ic t", p=128),
                                in_=u_sb[:, :].rearrange("p (ic t) -> p ic t", ic=NIC)))
    sb.close()


def build_A(maskin):
    nc = bass.Bass("TRN2", target_bir_lowering=False, detect_race_conditions=False)
    io = declare_consts(nc)
    io["xin"] = nc.dram_tensor("xin", [D, R], F32, kind="ExternalInput").ap()
    io["wq"] = nc.dram_tensor("wq", [D, D], F32, kind="ExternalInput").ap()
    io["wk"] = nc.dram_tensor("wk", [D, D], F32, kind="ExternalInput").ap()
    io["wv"] = nc.dram_tensor("wv", [D, D], F32, kind="ExternalInput").ap()
    io["wo"] = nc.dram_tensor("wo", [D, D], F32, kind="ExternalInput").ap()
    io["wgate"] = nc.dram_tensor("wgate", [D, 4096], F32, kind="ExternalInput").ap()
    io["wup"] = nc.dram_tensor("wup", [D, 4096], F32, kind="ExternalInput").ap()
    io["wdown"] = nc.dram_tensor("wdown", [4096, D], F32, kind="ExternalInput").ap()
    io["gwA"] = nc.dram_tensor("gwA", [128, 17], F32, kind="ExternalInput").ap()
    io["wrowA"] = nc.dram_tensor("wrowA", [1, D], F32, kind="ExternalInput").ap()
    io["maskin"] = nc.dram_tensor("maskin", [128, 8], F32, kind="ExternalInput").ap()
    io["xout"] = nc.dram_tensor("xout", [D, R], F32, kind="ExternalOutput").ap()
    io["cc_in"] = nc.dram_tensor("cc_in", [1024, 128], F32).ap()
    io["cc_out"] = nc.dram_tensor("cc_out", [8192, 128], F32).ap()
    io["o_dram"] = nc.dram_tensor("o_dram", [1024, R], BF16).ap()
    with ExitStack() as c:
        consts = load_consts(nc, c, io)
        consts["vsn"] = VSync(c, nc, "a_vs")
        emit_blockA(nc, c, io, consts)
    return nc


def dA_host_inputs(inputs):
    m = {}
    m["wq"] = f32c(inputs["L0_attn_Wq"])
    m["wk"] = f32c(inputs["L0_attn_Wk"])
    m["wv"] = f32c(inputs["L0_attn_Wv"])
    m["wo"] = f32c(inputs["L0_attn_Wo"])
    m["wgate"] = f32c(inputs["L0_attn_Wgate"])
    m["wup"] = f32c(inputs["L0_attn_Wup"])
    m["wdown"] = f32c(inputs["L0_attn_Wdown"])
    gw = np.zeros((128, 17), np.float32)
    gw[:, 0:8] = f32c(inputs["L0_attn_attn_norm_w"]).reshape(8, 128).T
    gw[:, 8:16] = f32c(inputs["L0_attn_mlp_norm_w"]).reshape(8, 128).T
    gw[:, 16] = f32c(inputs["L0_attn_attn_out_norm_w"])
    m["gwA"] = gw
    m["wrowA"] = f32c(inputs["L0_attn_attn_norm_w"]).reshape(1, D)
    return m


def core_masks():
    outs = []
    for i in range(NCORES):
        mk = np.zeros((128, 8), np.float32)
        mk[:, :i] = 1.0
        outs.append(mk)
    return outs


def run_A(x0_fm_shards, inputs):
    nc = build_A(None)
    ca = const_arrays()
    da = dA_host_inputs(inputs)
    mks = core_masks()
    in_maps = []
    for i in range(NCORES):
        m = dict(ca)
        m.update(da)
        m["xin"] = f32c(x0_fm_shards[i])
        m["maskin"] = mks[i]
        in_maps.append(m)
    res = run_bass_kernel_spmd(nc, in_maps, core_ids=list(range(NCORES)))
    return [np.asarray(r["xout"], dtype=np.float32) for r in res.results]



def make_ffn_sems(nc, c, tag):
    sems = {}
    for nm in ("w", "f", "a", "e", "x", "s"):
        sems[nm] = c.enter_context(nc.semaphore(f"{tag}_{nm}"))
    return sems


def emit_blockD(nc, c, io, x_sb_ctx, consts):
    sb = ExitStack()
    ones_col, ones_row, eps_cols = consts["ones_col"], consts["ones_row"], consts["eps_cols"]
    vsn = consts["vsn"]
    NHID = 4096
    NOC = NHID // 128
    sems = None
    scol = sb.enter_context(nc.sbuf_tensor("d_scol", [128, 4 * NOC + 8], F32))
    zn = sb.enter_context(nc.sbuf_tensor("d_zn", [128, NIC * R], BF16))
    ld = c.enter_context(nc.semaphore("d_ld"))
    with nc.Block() as blk:
        @blk.sync
        def _(sy):
            sy.dma_start(out=scol[:, :], in_=io["scolD"]).then_inc(ld, 16)
            sy.wait_ge(ld, 16)
    s1c, s2c = scol[:, 0:NOC], scol[:, NOC:2 * NOC]
    hs1c, hs2c = scol[:, 2 * NOC:3 * NOC], scol[:, 3 * NOC:4 * NOC]
    s3c = scol[:, 4 * NOC:4 * NOC + 8]

    with ExitStack() as lnc:
        if x_sb_ctx is None:
            x_sb = lnc.enter_context(nc.sbuf_tensor("d_x", [128, NIC * R], F32))
            with nc.Block() as blk:
                @blk.sync
                def _(sy):
                    sy.dma_start(out=x_sb[:, :].rearrange("p (ic t) -> p ic t", ic=NIC),
                                 in_=io["xin"].rearrange("(ic p) t -> p ic t", p=128)).then_inc(ld, 16)
                    sy.wait_ge(ld, 32)
        else:
            x_sb = x_sb_ctx
        emit_ln_zn(nc, c, vsn, x_sb, zn, ones_col, ones_row, 'ln',
                   eps1_col=eps_cols[:, 0:1], name="D", ones_col_bf=consts["ones_col_bf"])

    for half in range(2):
        toff = half * (R // 2)
        goffs = [toff, toff + G]
        hgoffs = [0, G]
        with ExitStack() as hc:
            g1_sb = hc.enter_context(nc.sbuf_tensor(f"d_g1{half}", [128, NOC * (R // 2)], BF16))
            hplan = Plan()
            pgS = hc.enter_context(nc.psum_tensor(f"d_pgS{half}", [128, 4 * G], F32))
            puS = hc.enter_context(nc.psum_tensor(f"d_puS{half}", [128, 4 * G], F32))
            spsum = (pgS, puS, {0: None, 1: None})
            with ExitStack() as hi:
                h_sb = hi.enter_context(nc.sbuf_tensor(f"d_h{half}", [128, NOC * (R // 2)], BF16))
                r1 = emit_ffn(nc, c, sems, f"d12h{half}", 'silu2', (io["w1T"], io["w2T"]),
                              zn, R, goffs, NOC, NIC, dst_sb=h_sb, dst_cols=R // 2,
                              dst_goffs=hgoffs, scol1=s1c, scol2=s2c,
                              plan=hplan, bufc=hi, shared_psum=spsum)
                r2 = emit_ffn(nc, c, sems, f"d3h{half}", 'gelu', (io["h1T"],),
                              h_sb, R // 2, hgoffs, NOC, NOC, dst_sb=g1_sb, dst_cols=R // 2,
                              dst_goffs=hgoffs, scol1=hs1c,
                              plan=hplan, bufc=hi, shared_psum=spsum, src_ready=r1)
                hplan.emit(nc, c, f"dph{half}")
            g2_sb = hc.enter_context(nc.sbuf_tensor(f"d_g2{half}", [128, NOC * (R // 2)], BF16))
            hplan2 = Plan()
            spsum2 = (pgS, puS, {0: None, 1: None})
            r3 = emit_ffn(nc, c, sems, f"d4h{half}", 'gelu', (io["h2T"],),
                          g1_sb, R // 2, hgoffs, NOC, NOC, dst_sb=g2_sb, dst_cols=R // 2,
                          dst_goffs=hgoffs, scol1=hs2c,
                          plan=hplan2, bufc=hc, shared_psum=spsum2)
            r4 = emit_ffn(nc, c, sems, f"d5h{half}", 'w3add', (io["w3T"],),
                          g2_sb, R // 2, hgoffs, NIC, NOC, scol1=s3c,
                          xin_ap=io["xin"], out_ap=io["xout"], tok_offs=toff,
                          plan=hplan2, bufc=hc, shared_psum=spsum2, src_ready=r3)
            hplan2.emit(nc, c, f"dqh{half}")
    sb.close()


def build_D():
    nc = bass.Bass("TRN2", target_bir_lowering=False, detect_race_conditions=False)
    io = declare_consts(nc)
    io["xin"] = nc.dram_tensor("xin", [D, R], F32, kind="ExternalInput").ap()
    io["w1T"] = nc.dram_tensor("w1T", [D, 4096], F32, kind="ExternalInput").ap()
    io["w2T"] = nc.dram_tensor("w2T", [D, 4096], F32, kind="ExternalInput").ap()
    io["h1T"] = nc.dram_tensor("h1T", [4096, 4096], F32, kind="ExternalInput").ap()
    io["h2T"] = nc.dram_tensor("h2T", [4096, 4096], F32, kind="ExternalInput").ap()
    io["w3T"] = nc.dram_tensor("w3T", [4096, D], F32, kind="ExternalInput").ap()
    io["scolD"] = nc.dram_tensor("scolD", [128, 4 * 32 + 8], F32, kind="ExternalInput").ap()
    io["xout"] = nc.dram_tensor("xout", [D, R], F32, kind="ExternalOutput").ap()
    with ExitStack() as c:
        consts = load_consts(nc, c, io)
        consts["vsn"] = VSync(c, nc, "d_vs")
        emit_blockD(nc, c, io, None, consts)
    return nc


def dD_host_inputs(inputs):
    m = {}
    m["w1T"] = f32c(np.asarray(inputs["L1_ffn_w1_w"]).T)
    m["w2T"] = f32c(np.asarray(inputs["L1_ffn_w2_w"]).T)
    m["h1T"] = f32c(np.asarray(inputs["L1_ffn_h1_w"]).T)
    m["h2T"] = f32c(np.asarray(inputs["L1_ffn_h2_w"]).T)
    m["w3T"] = f32c(np.asarray(inputs["L1_ffn_w3_w"]).T)
    sc = np.zeros((128, 4 * 32 + 8), np.float32)
    sc[:, 0:32] = np.asarray(inputs["L1_ffn_w1_s"]).reshape(32, 128).T
    sc[:, 32:64] = np.asarray(inputs["L1_ffn_w2_s"]).reshape(32, 128).T
    sc[:, 64:96] = np.asarray(inputs["L1_ffn_h1_s"]).reshape(32, 128).T
    sc[:, 96:128] = np.asarray(inputs["L1_ffn_h2_s"]).reshape(32, 128).T
    sc[:, 128:136] = np.asarray(inputs["L1_ffn_w3_s"]).reshape(8, 128).T
    m["scolD"] = sc
    return m


def run_D(x3_fm_shards, inputs):
    nc = build_D()
    ca = const_arrays()
    dm = dD_host_inputs(inputs)
    in_maps = []
    for i in range(NCORES):
        m = dict(ca)
        m.update(dm)
        m["xin"] = f32c(x3_fm_shards[i])
        in_maps.append(m)
    res = run_bass_kernel_spmd(nc, in_maps, core_ids=list(range(NCORES)))
    return [np.asarray(r["xout"], dtype=np.float32) for r in res.results]



def build_B(stage="full"):
    nc = bass.Bass("TRN2", target_bir_lowering=False, detect_race_conditions=False)
    io = declare_consts(nc)
    io["xin"] = nc.dram_tensor("xin", [D, R], F32, kind="ExternalInput").ap()
    io["wr"] = nc.dram_tensor("wr", [D, 8], F32, kind="ExternalInput").ap()
    io["wg"] = nc.dram_tensor("wg", [8, D, 2048], F32, kind="ExternalInput").ap()
    io["wu"] = nc.dram_tensor("wu", [8, D, 2048], F32, kind="ExternalInput").ap()
    io["wd"] = nc.dram_tensor("wd", [8, 2048, D], F32, kind="ExternalInput").ap()
    io["xout"] = nc.dram_tensor("xout", [D, R], F32, kind="ExternalOutput").ap()
    with ExitStack() as c:
        x_sb = c.enter_context(nc.sbuf_tensor("x_sb", [128, NIC * R], F32))
        zn_sb = c.enter_context(nc.sbuf_tensor("zn_sb", [128, NIC * R], BF16))
        xld = c.enter_context(nc.semaphore("xld"))
        with nc.Block() as blk:
            @blk.sync
            def _(sy):
                sy.dma_start(out=x_sb[:, :].rearrange("p (ic t) -> p ic t", ic=NIC),
                             in_=io["xin"].rearrange("(ic p) t -> p ic t", p=128)).then_inc(xld, 16)
                sy.wait_ge(xld, 16)
        cst = load_consts(nc, c, io)
        cst["vsn"] = VSync(c, nc, "b_vs0")
        emit_blockB(nc, c, io, x_sb, zn_sb, cst["ones_col"], cst["ones_row"],
                    cst["ident"], cst["eps_cols"], stage=stage,
                    ones_col_bf=cst["ones_col_bf"])
        xst = c.enter_context(nc.semaphore("xst"))
        with nc.Block() as blk:
            @blk.sync
            def _(sy):
                sy.dma_start(out=io["xout"].rearrange("(ic p) t -> p ic t", p=128),
                             in_=x_sb[:, :].rearrange("p (ic t) -> p ic t", ic=NIC)).then_inc(xst, 16)
                sy.wait_ge(xst, 16)
    return nc


def run_B(x1_fm_shards, inputs, stage="full"):
    nc = build_B(stage)
    ca = const_arrays()
    in_maps = []
    for i in range(NCORES):
        m = dict(ca)
        m["xin"] = f32c(x1_fm_shards[i])
        m["wr"] = f32c(inputs["L0_ffn_router_kernel_DE"])
        m["wg"] = f32c(inputs["L0_ffn_kernel_gating_EDF"])
        m["wu"] = f32c(inputs["L0_ffn_kernel_up_EDF"])
        m["wd"] = f32c(inputs["L0_ffn_kernel_down_EFD"])
        in_maps.append(m)
    res = run_bass_kernel_spmd(nc, in_maps, core_ids=list(range(NCORES)))
    return [np.asarray(r["xout"], dtype=np.float32) for r in res.results]


def build_fused():
    nc = bass.Bass("TRN2", target_bir_lowering=False, detect_race_conditions=False)
    io = declare_consts(nc)
    io["xin"] = nc.dram_tensor("xin", [D, R], F32, kind="ExternalInput").ap()
    io["wq"] = nc.dram_tensor("wq", [D, D], F32, kind="ExternalInput").ap()
    io["wk"] = nc.dram_tensor("wk", [D, D], F32, kind="ExternalInput").ap()
    io["wv"] = nc.dram_tensor("wv", [D, D], F32, kind="ExternalInput").ap()
    io["wo"] = nc.dram_tensor("wo", [D, D], F32, kind="ExternalInput").ap()
    io["wgate"] = nc.dram_tensor("wgate", [D, 4096], F32, kind="ExternalInput").ap()
    io["wup"] = nc.dram_tensor("wup", [D, 4096], F32, kind="ExternalInput").ap()
    io["wdown"] = nc.dram_tensor("wdown", [4096, D], F32, kind="ExternalInput").ap()
    io["gwA"] = nc.dram_tensor("gwA", [128, 17], F32, kind="ExternalInput").ap()
    io["wrowA"] = nc.dram_tensor("wrowA", [1, D], F32, kind="ExternalInput").ap()
    io["maskin"] = nc.dram_tensor("maskin", [128, 8], F32, kind="ExternalInput").ap()
    io["wr"] = nc.dram_tensor("wr", [D, 8], F32, kind="ExternalInput").ap()
    io["wg"] = nc.dram_tensor("wg", [8, D, 2048], F32, kind="ExternalInput").ap()
    io["wu"] = nc.dram_tensor("wu", [8, D, 2048], F32, kind="ExternalInput").ap()
    io["wd"] = nc.dram_tensor("wd", [8, 2048, D], F32, kind="ExternalInput").ap()
    io["wqkv"] = nc.dram_tensor("wqkv", [D, 3072], F32, kind="ExternalInput").ap()
    io["wqkvT"] = nc.dram_tensor("wqkvT", [3072, D], F32, kind="ExternalInput").ap()
    io["proj"] = nc.dram_tensor("proj", [16, 64, 64], F32, kind="ExternalInput").ap()
    io["wout"] = nc.dram_tensor("wout", [D, D], F32, kind="ExternalInput").ap()
    io["bqkvC"] = nc.dram_tensor("bqkvC", [64, 48], F32, kind="ExternalInput").ap()
    io["boutC"] = nc.dram_tensor("boutC", [128, 8], F32, kind="ExternalInput").ap()
    io["w1T"] = nc.dram_tensor("w1T", [D, 4096], F32, kind="ExternalInput").ap()
    io["w2T"] = nc.dram_tensor("w2T", [D, 4096], F32, kind="ExternalInput").ap()
    io["h1T"] = nc.dram_tensor("h1T", [4096, 4096], F32, kind="ExternalInput").ap()
    io["h2T"] = nc.dram_tensor("h2T", [4096, 4096], F32, kind="ExternalInput").ap()
    io["w3T"] = nc.dram_tensor("w3T", [4096, D], F32, kind="ExternalInput").ap()
    io["scolD"] = nc.dram_tensor("scolD", [128, 4 * 32 + 8], F32, kind="ExternalInput").ap()
    io["xout"] = nc.dram_tensor("xout", [D, R], F32, kind="ExternalOutput").ap()
    io["cc_in"] = nc.dram_tensor("cc_in", [1024, 128], F32).ap()
    io["cc_out"] = nc.dram_tensor("cc_out", [8192, 128], F32).ap()
    io["o_dram"] = nc.dram_tensor("o_dram", [1024, R], BF16).ap()
    io["cc_inC"] = nc.dram_tensor("cc_inC", [1024, 65], F32).ap()
    io["cc_outC"] = nc.dram_tensor("cc_outC", [8192, 65], F32).ap()
    x1s = nc.dram_tensor("x1s", [D, R], F32).ap()
    x2s = nc.dram_tensor("x2s", [D, R], F32).ap()
    x3s = nc.dram_tensor("x3s", [D, R], F32).ap()

    with ExitStack() as c:
        consts = load_consts(nc, c, io)
        consts["vsn"] = VSync(c, nc, "f_vs")
        ioA = dict(io); ioA["xout"] = x1s
        emit_blockA(nc, c, ioA, consts)
        ioB = dict(io); ioB["xin"] = x1s; ioB["xout"] = x2s
        with ExitStack() as bscope:
            x_sb = bscope.enter_context(nc.sbuf_tensor("f_bx", [128, NIC * R], F32))
            zn_sb = bscope.enter_context(nc.sbuf_tensor("f_bzn", [128, NIC * R], BF16))
            bld = c.enter_context(nc.semaphore("f_bld"))
            with nc.Block() as blk:
                @blk.sync
                def _(sy):
                    sy.dma_start(out=x_sb[:, :].rearrange("p (ic t) -> p ic t", ic=NIC),
                                 in_=x1s.rearrange("(ic p) t -> p ic t", p=128)).then_inc(bld, 16)
                    sy.wait_ge(bld, 16)
            emit_blockB(nc, c, ioB, x_sb, zn_sb, consts["ones_col"], consts["ones_row"],
                        consts["ident"], consts["eps_cols"],
                        ones_col_bf=consts["ones_col_bf"])
            with nc.Block() as blk:
                @blk.sync
                def _(sy):
                    sy.dma_start(out=x2s.rearrange("(ic p) t -> p ic t", p=128),
                                 in_=x_sb[:, :].rearrange("p (ic t) -> p ic t", ic=NIC)).then_inc(bld, 16)
                    sy.wait_ge(bld, 32)
        ioC = dict(io); ioC["xin"] = x2s; ioC["xout"] = x3s
        emit_blockC(nc, c, ioC, consts)
        ioD = dict(io); ioD["xin"] = x3s
        emit_blockD(nc, c, ioD, None, consts)
    return nc


_FUSED_NC = None


def kernel_fused(**inputs):
    global _FUSED_NC
    if _FUSED_NC is None:
        _FUSED_NC = build_fused()
    nc = _FUSED_NC
    x0 = np.asarray(inputs["L0_attn_hidden_states"], dtype=np.float32)[0]
    ca = const_arrays()
    da = dA_host_inputs(inputs)
    dc = dC_host_inputs(inputs)
    dd = dD_host_inputs(inputs)
    mks = core_masks()
    base = dict(ca)
    base.update(da); base.update(dc); base.update(dd)
    base["wr"] = f32c(inputs["L0_ffn_router_kernel_DE"])
    base["wg"] = f32c(inputs["L0_ffn_kernel_gating_EDF"])
    base["wu"] = f32c(inputs["L0_ffn_kernel_up_EDF"])
    base["wd"] = f32c(inputs["L0_ffn_kernel_down_EFD"])
    in_maps = []
    for i in range(NCORES):
        m = dict(base)
        m["xin"] = np.ascontiguousarray(x0[i * R:(i + 1) * R, :].T)
        m["maskin"] = mks[i]
        in_maps.append(m)
    res = run_bass_kernel_spmd(nc, in_maps, core_ids=list(range(NCORES)))
    out = np.concatenate([np.asarray(r["xout"], np.float32).T for r in res.results], axis=0)
    return np.ascontiguousarray(out[None, :, :], dtype=np.float32)


def kernel_unfused(**inputs):
    x0 = np.asarray(inputs["L0_attn_hidden_states"], dtype=np.float32)[0]
    shards = [np.ascontiguousarray(x0[i * R:(i + 1) * R, :].T) for i in range(NCORES)]
    shards = run_A(shards, inputs)
    shards = run_B(shards, inputs)
    shards = run_C(shards, inputs)
    shards = run_D(shards, inputs)
    out = np.concatenate([s.T for s in shards], axis=0)[None, :, :]
    return np.ascontiguousarray(out, dtype=np.float32)


def kernel(**inputs):
    return kernel_fused(**inputs)


# version 13
# speedup vs baseline: 1.0619x; 1.0619x over previous
import numpy as np
import ml_dtypes
import concourse.bass as bass
import concourse.mybir as mybir
from concourse.bass_utils import run_bass_kernel_spmd
from contextlib import ExitStack

F32 = mybir.dt.float32
BF16 = mybir.dt.bfloat16
AF = mybir.ActivationFunctionType
ALU = mybir.AluOpType
AX = mybir.AxisListType

NCORES = 8
R = 2048
D = 1024
NIC = D // 128
G = 512
NG = R // G
NT = R // 128
DBG = {"gu": True, "down": True, "evac": True}


def f32c(a):
    return np.ascontiguousarray(np.asarray(a, dtype=np.float32))


class Plan:
    ENGS = ("pe", "act", "dve", "gp", "sy")

    def __init__(self):
        self.stages = {e: [] for e in self.ENGS}

    def __init2__(self):
        pass

    def add(self, eng, fn, deps=(), inc=1):
        cnt = getattr(self, "_cnt", None)
        if cnt is None:
            self._cnt = {e: 0 for e in self.ENGS}
            cnt = self._cnt
        self.stages[eng].append((fn, [d for d in deps if d is not None], inc))
        self._pending = getattr(self, "_pending", {})
        raise RuntimeError("use add_n")

    def add_n(self, eng, fn, deps=(), inc_total=1, chan=None):
        if chan is None:
            chan = eng
        if not hasattr(self, "_cnt"):
            self._cnt = {}
        self._cnt.setdefault(chan, 0)
        self.stages[eng].append((fn, [d for d in deps if d is not None], inc_total, chan))
        self._cnt[chan] += inc_total
        return (chan, self._cnt[chan])

    def emit(self, nc, c, tag):
        chans = {ch for e in self.ENGS for (_, _, _, ch) in self.stages[e]}
        for e in self.ENGS:
            for (_, deps, _, _) in self.stages[e]:
                for (ch, _) in deps:
                    chans.add(ch)
        sems = {ch: c.enter_context(nc.semaphore(f"{tag}_{ch}")) for ch in chans}
        attr = {"pe": "tensor", "act": "scalar", "dve": "vector",
                "gp": "gpsimd", "sy": "sync"}
        with nc.Block() as blk:
            for e in self.ENGS:
                if not self.stages[e]:
                    continue
                def mk(e):
                    def body(eng):
                        for fn, deps, inc_total, ch in self.stages[e]:
                            for (pc_, pn) in deps:
                                eng.wait_ge(sems[pc_], pn)
                            if inc_total == 1:
                                ins = fn(eng)
                                ins.then_inc(sems[ch], 1)
                            else:
                                fn(eng, sems[ch])
                    return body
                getattr(blk, attr[e])(mk(e))


class VSync:
    def __init__(self, c, nc, name="vsync"):
        self.h = c.enter_context(nc.semaphore(name))
        self.n = 0

    def __call__(self, eng, ins):
        self.n += 1
        ins.then_inc(self.h, 1)
        eng.wait_ge(self.h, self.n)
        return ins



def emit_ln_rows(nc, semc, vsn, x_sb, ones_col, rows_m, rows_s, rows_c, kind,
                 eps1_col=None, eps2_col=None, nic=NIC, xcols=R, name="",
                 rows_s1=None, rows_c1=None, width=None, xoff=0, shared=None):
    with ExitStack() as c:
        p_sx = None if kind == 'rms' else c.enter_context(nc.psum_tensor("p_sx" + name, [128, G], F32))
        p_sq = c.enter_context(nc.psum_tensor("p_sq" + name, [128, G], F32))
        sqs = [c.enter_context(nc.sbuf_tensor(f"lnsq{i}" + name, [128, G], F32)) for i in range(2)]
        rt1 = c.enter_context(nc.sbuf_tensor("lnrt1" + name, [1, G], F32))
        rt2 = c.enter_context(nc.sbuf_tensor("lnrt2" + name, [1, G], F32))
        width = float(nic * 128) if width is None else float(width)
        if shared is None:
            sqd = semc.enter_context(nc.semaphore("lnsqd" + name))
            mmd = semc.enter_context(nc.semaphore("lnmmd" + name))
            b_sq = b_mm = 0
        else:
            sqd, mmd = shared["sqd"], shared["mmd"]
            b_sq, b_mm = shared["n_sq"], shared["n_mm"]
        for g in range(NG):
            gs = slice(g * G, (g + 1) * G)
            if True:
                b0 = b_sq + g * nic
                b1 = b_mm + g * nic
                with nc.Block() as blk:
                    @blk.scalar
                    def _(s):
                        for ic in range(nic):
                            s.activation(sqs[ic % 2][:, :],
                                         x_sb[:, xoff + ic * xcols + g * G: xoff + ic * xcols + (g + 1) * G],
                                         AF.Square).then_inc(sqd, 1)
                            if ic >= 1:
                                s.wait_ge(mmd, b1 + ic)
                    @blk.tensor
                    def _(t):
                        if kind != 'rms':
                            for ic in range(nic):
                                t.matmul(p_sx[0:1, :], ones_col[:, :],
                                         x_sb[:, xoff + ic * xcols + g * G: xoff + ic * xcols + (g + 1) * G],
                                         start=(ic == 0), stop=(ic == nic - 1))
                        for ic in range(nic):
                            t.wait_ge(sqd, b0 + ic + 1)
                            t.matmul(p_sq[0:1, :], ones_col[:, :], sqs[ic % 2][:, :],
                                     start=(ic == 0), stop=(ic == nic - 1)).then_inc(mmd, 1)
            with nc.Block() as blk:
                @blk.vector
                def _(v):
                    if kind == 'rms':
                        v.tensor_scalar(rt1[:, :], p_sq[0:1, :], 1.0 / width, None, ALU.mult)
                    else:
                        v.tensor_scalar(rows_m[:, gs], p_sx[0:1, :], 1.0 / width, None, ALU.mult)
                        v.tensor_tensor(rt2[:, :], rows_m[:, gs], rows_m[:, gs], ALU.mult)
                        v.scalar_tensor_tensor(rt1[:, :], p_sq[0:1, :], 1.0 / width, rt2[:, :],
                                               ALU.mult, ALU.subtract)
            with nc.Block() as blk:
                @blk.scalar
                def _(s):
                    s.activation(rt2[:, :], rt1[:, :], AF.Sqrt, bias=eps1_col[0:1, 0:1])
            with nc.Block() as blk:
                @blk.vector
                def _(v):
                    vsn(v, v.reciprocal(rows_s[:, gs], rt2[:, :]))
                    if kind == 'lnrms':
                        if rows_s1 is not None:
                            v.tensor_copy(rows_s1[:, gs], rows_s[:, gs])
                            v.scalar_tensor_tensor(rows_c1[:, gs], rows_m[:, gs], -1.0,
                                                   rows_s[:, gs], ALU.mult, ALU.mult)
                        v.tensor_tensor(rt2[:, :], rows_s[:, gs], rows_s[:, gs], ALU.mult)
                        v.tensor_tensor(rt2[:, :], rt1[:, :], rt2[:, :], ALU.mult)
            if kind == 'lnrms':
                with nc.Block() as blk:
                    @blk.scalar
                    def _(s):
                        s.activation(rt2[:, :], rt2[:, :], AF.Sqrt, bias=eps2_col[0:1, 0:1])
                with nc.Block() as blk:
                    @blk.vector
                    def _(v):
                        v.reciprocal(rt1[:, :], rt2[:, :])
                        v.tensor_tensor(rows_s[:, gs], rows_s[:, gs], rt1[:, :], ALU.mult)
            if kind != 'rms':
                with nc.Block() as blk:
                    @blk.vector
                    def _(v):
                        v.scalar_tensor_tensor(rows_c[:, gs], rows_m[:, gs], -1.0, rows_s[:, gs],
                                               ALU.mult, ALU.mult)
        if shared is not None:
            shared["n_sq"] = b_sq + NG * nic
            shared["n_mm"] = b_mm + NG * nic


def emit_ln_zn(nc, c, vsn, x_sb, zn_sb, ones_col, ones_row, kind,
               eps1_col=None, eps2_col=None, nic=NIC, xcols=R, zcols=R,
               gain_rowtile=None, gain_coltile=None, rms_gain=False,
               rows_s1=None, rows_c1=None, width=None, name="", ones_col_bf=None):
    width = float(nic * 128) if width is None else float(width)
    with ExitStack() as sbx:
        p_sx = None if kind == 'rms' else sbx.enter_context(nc.psum_tensor("lz_sx" + name, [128, G], F32))
        p_sq = sbx.enter_context(nc.psum_tensor("lz_sq" + name, [128, G], F32))
        pR = sbx.enter_context(nc.psum_tensor("lz_pR" + name, [128, G], F32))
        pC = None
        if not rms_gain:
            pC = [sbx.enter_context(nc.psum_tensor(f"lz_pC{i}" + name, [128, G], F32)) for i in range(2)]
        sqb = [sbx.enter_context(nc.sbuf_tensor(f"lz_sqb{i}" + name, [128, nic * G], BF16)) for i in range(2)]
        xbb = [sbx.enter_context(nc.sbuf_tensor(f"lz_xbb{i}" + name, [128, nic * G], BF16)) for i in range(2)]
        rt1 = sbx.enter_context(nc.sbuf_tensor("lz_rt1" + name, [1, NG * G], F32))
        rt2 = sbx.enter_context(nc.sbuf_tensor("lz_rt2" + name, [1, NG * G], F32))
        rS = sbx.enter_context(nc.sbuf_tensor("lz_rS" + name, [1, NG * G], F32))
        rC = sbx.enter_context(nc.sbuf_tensor("lz_rC" + name, [1, NG * G], F32))
        scr = sbx.enter_context(nc.sbuf_tensor("lz_scr" + name, [128, G], F32))
        plan = Plan()
        prev_summ = [None, None]
        prev_apply_last = None
        aplist = [None, None]
        for g in range(NG):
            gs = slice(g * G, (g + 1) * G)
            def mk_sqall(g=g):
                def f(s):
                    for ic in range(nic):
                        ins = s.activation(sqb[g % 2][:, ic * G:(ic + 1) * G],
                                           x_sb[:, ic * xcols + g * G: ic * xcols + (g + 1) * G],
                                           AF.Square)
                        if kind != 'rms':
                            ins = s.activation(xbb[g % 2][:, ic * G:(ic + 1) * G],
                                               x_sb[:, ic * xcols + g * G: ic * xcols + (g + 1) * G],
                                               AF.Copy)
                    return ins
                return f
            sq = plan.add_n("act", mk_sqall(), [prev_summ[g % 2]])
            def mk_summ(g=g):
                def f(t):
                    if kind != 'rms':
                        for ic in range(nic):
                            t.matmul(p_sx[0:1, :], ones_col_bf[:, :],
                                     xbb[g % 2][:, ic * G:(ic + 1) * G],
                                     start=(ic == 0), stop=(ic == nic - 1))
                    for ic in range(nic):
                        ins = t.matmul(p_sq[0:1, :], ones_col_bf[:, :], sqb[g % 2][:, ic * G:(ic + 1) * G],
                                       start=(ic == 0), stop=(ic == nic - 1))
                    return ins
                return f
            sdeps = [sq]
            if g >= 1:
                sdeps.append(rows1_ref)
            summ = plan.add_n("pe", mk_summ(), sdeps)
            prev_summ[g % 2] = summ
            def mk_rows1(g=g, gs=gs):
                def f(v):
                    if kind == 'rms':
                        return v.tensor_scalar(rt1[0:1, gs], p_sq[0:1, :], 1.0 / width, None, ALU.mult)
                    v.tensor_scalar(rt2[0:1, gs], p_sx[0:1, :], 1.0 / width, None, ALU.mult)
                    v.tensor_tensor(scr[0:1, 0:G], rt2[0:1, gs], rt2[0:1, gs], ALU.mult)
                    return v.scalar_tensor_tensor(rt1[0:1, gs], p_sq[0:1, :], 1.0 / width,
                                                  scr[0:1, 0:G], ALU.mult, ALU.subtract)
                return f
            rows1_ref = plan.add_n("dve", mk_rows1(), [summ])
            def mk_sqrt1(g=g, gs=gs):
                def f(s):
                    return s.activation(rS[0:1, gs], rt1[0:1, gs], AF.Sqrt, bias=eps1_col[0:1, 0:1])
                return f
            sq1 = plan.add_n("act", mk_sqrt1(), [rows1_ref])
            def mk_rows2(g=g, gs=gs):
                def f(v):
                    vsn(v, v.reciprocal(rS[0:1, gs], rS[0:1, gs]))
                    ins = None
                    if kind == 'lnrms':
                        if rows_s1 is not None:
                            v.tensor_copy(rows_s1[0:1, gs], rS[0:1, gs])
                            v.scalar_tensor_tensor(rows_c1[0:1, gs], rt2[0:1, gs], -1.0,
                                                   rS[0:1, gs], ALU.mult, ALU.mult)
                        v.tensor_tensor(scr[0:1, 0:G], rS[0:1, gs], rS[0:1, gs], ALU.mult)
                        ins = v.tensor_tensor(scr[0:1, 0:G], rt1[0:1, gs], scr[0:1, 0:G], ALU.mult)
                    if ins is None:
                        ins = v.tensor_copy(scr[0:1, 0:G], rS[0:1, gs])
                    return ins
                return f
            rows2_ref = plan.add_n("dve", mk_rows2(), [sq1])
            if kind == 'lnrms':
                def mk_sqrt2(g=g, gs=gs):
                    def f(s):
                        return s.activation(scr[0:1, 0:G], scr[0:1, 0:G], AF.Sqrt,
                                            bias=eps2_col[0:1, 0:1])
                    return f
                sq2 = plan.add_n("act", mk_sqrt2(), [rows2_ref])
                def mk_rows3(g=g, gs=gs):
                    def f(v):
                        vsn(v, v.reciprocal(scr[0:1, 0:G], scr[0:1, 0:G]))
                        ins = v.tensor_tensor(rS[0:1, gs], rS[0:1, gs], scr[0:1, 0:G], ALU.mult)
                        if True:
                            vsn(v, ins)
                            ins = v.scalar_tensor_tensor(rC[0:1, gs], rt2[0:1, gs], -1.0,
                                                         rS[0:1, gs], ALU.mult, ALU.mult)
                        return ins
                    return f
                rows_final = plan.add_n("dve", mk_rows3(), [sq2])
            elif kind == 'ln':
                def mk_rows3(g=g, gs=gs):
                    def f(v):
                        return v.scalar_tensor_tensor(rC[0:1, gs], rt2[0:1, gs], -1.0,
                                                      rS[0:1, gs], ALU.mult, ALU.mult)
                    return f
                rows_final = plan.add_n("dve", mk_rows3(), [rows2_ref])
            else:
                rows_final = rows2_ref
            def mk_bR(g=g, gs=gs):
                def f(t):
                    return t.matmul(pR[:, :], ones_row[:, :], rS[0:1, gs], start=True, stop=True)
                return f
            bR = plan.add_n("pe", mk_bR(), [rows_final, prev_apply_last])
            if rms_gain:
                for ic in range(nic):
                    def mk_ap(g=g, ic=ic):
                        def f(v):
                            v.tensor_tensor(scr[:, :], x_sb[:, ic * xcols + g * G: ic * xcols + (g + 1) * G],
                                            pR[:, :], ALU.mult)
                            return v.tensor_scalar(
                                zn_sb[:, ic * zcols + g * G: ic * zcols + (g + 1) * G],
                                scr[:, :], gain_coltile[:, ic:ic + 1], None, ALU.mult)
                        return f
                    prev_apply_last = plan.add_n("dve", mk_ap(), [bR])
            elif gain_rowtile is not None:
                def mk_bC0g(g=g, gs=gs):
                    def f(t):
                        return t.matmul(pC[0][:, :], ones_row[:, :], rC[0:1, gs],
                                        start=True, stop=True)
                    return f
                bC0 = plan.add_n("pe", mk_bC0g(), [rows_final, prev_apply_last])
                def mk_apg(g=g):
                    def f(v):
                        for ic in range(nic):
                            v.tensor_tensor(scr[:, :],
                                            x_sb[:, ic * xcols + g * G: ic * xcols + (g + 1) * G],
                                            pR[:, :], ALU.mult)
                            v.tensor_tensor(scr[:, :], scr[:, :], pC[0][:, :], ALU.add)
                            ins = v.tensor_scalar(
                                zn_sb[:, ic * zcols + g * G: ic * zcols + (g + 1) * G],
                                scr[:, :], gain_coltile[:, ic:ic + 1], None, ALU.mult)
                        return ins
                    return f
                prev_apply_last = plan.add_n("dve", mk_apg(), [bC0, bR])
            else:
                def mk_bC0(g=g, gs=gs):
                    def f(t):
                        return t.matmul(pC[0][:, :], ones_row[:, :], rC[0:1, gs],
                                        start=True, stop=True)
                    return f
                bC0 = plan.add_n("pe", mk_bC0(), [rows_final, prev_apply_last])
                def mk_ap(g=g):
                    def f(v):
                        for ic in range(nic):
                            v.tensor_tensor(scr[:, :],
                                            x_sb[:, ic * xcols + g * G: ic * xcols + (g + 1) * G],
                                            pR[:, :], ALU.mult)
                            ins = v.tensor_tensor(
                                zn_sb[:, ic * zcols + g * G: ic * zcols + (g + 1) * G],
                                scr[:, :], pC[0][:, :], ALU.add)
                        return ins
                    return f
                prev_apply_last = plan.add_n("dve", mk_ap(), [bC0, bR])
        plan.emit(nc, c, "lz" + name)


def emit_zn(nc, x_sb, zn_sb, rows_s, rows_c, ones_row, gain_rowtile=None, gain_coltile=None,
            nic=NIC, xin_cols=R, zout_cols=R, name="", rms_gain=False, zoff=0, xoff=0):
    with ExitStack() as c:
        pR = c.enter_context(nc.psum_tensor("pR" + name, [128, G], F32))
        pC = None if rms_gain else c.enter_context(nc.psum_tensor("pC" + name, [128, G], F32))
        scr = c.enter_context(nc.sbuf_tensor("znscr" + name, [128, G], F32))
        for g in range(NG):
            if rms_gain:
                with nc.Block() as blk:
                    @blk.tensor
                    def _(t):
                        t.matmul(pR[:, :], ones_row[:, :], rows_s[0:1, g * G:(g + 1) * G],
                                 start=True, stop=True)
                with nc.Block() as blk:
                    @blk.vector
                    def _(v):
                        for ic in range(nic):
                            xs = x_sb[:, xoff + ic * xin_cols + g * G: xoff + ic * xin_cols + (g + 1) * G]
                            zs = zn_sb[:, zoff + ic * zout_cols + g * G: zoff + ic * zout_cols + (g + 1) * G]
                            v.tensor_tensor(scr[:, :], xs, pR[:, :], ALU.mult)
                            v.tensor_scalar(zs, scr[:, :], gain_coltile[:, ic:ic + 1], None, ALU.mult)
                continue
            if gain_rowtile is None:
                with nc.Block() as blk:
                    @blk.tensor
                    def _(t):
                        t.matmul(pR[:, :], ones_row[:, :], rows_s[0:1, g * G:(g + 1) * G],
                                 start=True, stop=True)
                        t.matmul(pC[:, :], ones_row[:, :], rows_c[0:1, g * G:(g + 1) * G],
                                 start=True, stop=True)
                with nc.Block() as blk:
                    @blk.vector
                    def _(v):
                        for ic in range(nic):
                            xs = x_sb[:, ic * xin_cols + g * G: ic * xin_cols + (g + 1) * G]
                            zs = zn_sb[:, ic * zout_cols + g * G: ic * zout_cols + (g + 1) * G]
                            v.tensor_tensor(scr[:, :], xs, pR[:, :], ALU.mult)
                            v.tensor_tensor(zs, scr[:, :], pC[:, :], ALU.add)
            else:
                for ic in range(nic):
                    with nc.Block() as blk:
                        @blk.tensor
                        def _(t):
                            if ic == 0:
                                t.matmul(pR[:, :], ones_row[:, :], rows_s[0:1, g * G:(g + 1) * G],
                                         start=True, stop=True)
                            t.matmul(pC[:, :], gain_rowtile[0:1, ic * 128:(ic + 1) * 128],
                                     rows_c[0:1, g * G:(g + 1) * G], start=True, stop=True)
                    with nc.Block() as blk:
                        @blk.vector
                        def _(v):
                            xs = x_sb[:, ic * xin_cols + g * G: ic * xin_cols + (g + 1) * G]
                            zs = zn_sb[:, ic * zout_cols + g * G: ic * zout_cols + (g + 1) * G]
                            v.tensor_tensor(scr[:, :], xs, pR[:, :], ALU.mult)
                            v.scalar_tensor_tensor(zs, scr[:, :], gain_coltile[:, ic:ic + 1],
                                                   pC[:, :], ALU.mult, ALU.add)


def declare_consts(nc):
    ios = {}
    ios["ones_col"] = nc.dram_tensor("ones_col", [128, 1], F32, kind="ExternalInput").ap()
    ios["ones_row"] = nc.dram_tensor("ones_row", [1, 128], F32, kind="ExternalInput").ap()
    ios["ident_f32"] = nc.dram_tensor("ident_f32", [128, 128], F32, kind="ExternalInput").ap()
    ios["eps_cols"] = nc.dram_tensor("eps_cols", [128, 4], F32, kind="ExternalInput").ap()
    ios["ident_bf"] = nc.dram_tensor("ident_bf", [128, 128], BF16, kind="ExternalInput").ap()
    ios["ones_col_bf"] = nc.dram_tensor("ones_col_bf", [128, 1], BF16, kind="ExternalInput").ap()
    ios["triu"] = nc.dram_tensor("triu", [128, 128], F32, kind="ExternalInput").ap()
    return ios


def const_arrays():
    return {
        "ones_col": np.ones((128, 1), np.float32),
        "ones_row": np.ones((1, 128), np.float32),
        "ident_f32": np.eye(128, dtype=np.float32),
        "eps_cols": np.tile(np.array([[1e-5, 1e-6, 0.0, 0.0]], np.float32), (128, 1)),
        "ident_bf": np.eye(128, dtype=ml_dtypes.bfloat16),
        "ones_col_bf": np.ones((128, 1), ml_dtypes.bfloat16),
        "triu": np.triu(np.ones((128, 128), np.float32)),
    }


def load_consts(nc, c, io):
    ones_col = c.enter_context(nc.sbuf_tensor("c_onescol", [128, 1], F32))
    ones_row = c.enter_context(nc.sbuf_tensor("c_onesrow", [1, 128], F32))
    ident = c.enter_context(nc.sbuf_tensor("c_ident", [128, 128], F32))
    eps_cols = c.enter_context(nc.sbuf_tensor("c_eps", [128, 4], F32))
    ident_bf = c.enter_context(nc.sbuf_tensor("c_identbf", [128, 128], BF16))
    ones_col_bf = c.enter_context(nc.sbuf_tensor("c_onescolbf", [128, 1], BF16))
    triu = c.enter_context(nc.sbuf_tensor("c_triu", [128, 128], F32))
    ld = c.enter_context(nc.semaphore("c_ld"))
    with nc.Block() as blk:
        @blk.sync
        def _(sy):
            sy.dma_start(out=ones_col[:, :], in_=io["ones_col"]).then_inc(ld, 16)
            sy.dma_start(out=ones_row[:, :], in_=io["ones_row"]).then_inc(ld, 16)
            sy.dma_start(out=ident[:, :], in_=io["ident_f32"]).then_inc(ld, 16)
            sy.dma_start(out=eps_cols[:, :], in_=io["eps_cols"]).then_inc(ld, 16)
            sy.dma_start(out=ident_bf[:, :], in_=io["ident_bf"]).then_inc(ld, 16)
            sy.dma_start(out=ones_col_bf[:, :], in_=io["ones_col_bf"]).then_inc(ld, 16)
            sy.dma_start(out=triu[:, :], in_=io["triu"]).then_inc(ld, 16)
            sy.wait_ge(ld, 112)
    return {"ones_col": ones_col, "ones_row": ones_row, "ident": ident,
            "eps_cols": eps_cols, "ident_bf": ident_bf, "ones_col_bf": ones_col_bf,
            "triu": triu}



def emit_blockB(nc, c, io, x_sb, zn_sb, ones_col, ones_row, ident, eps_cols, stage="full", ones_col_bf=None):
    sb = ExitStack()
    NE, F = 8, 2048
    if stage.startswith("exp"):
        NE = int(stage[3:])
    NFC = F // 128
    NDC = 8
    vsn = VSync(c, nc, "b_vs")
    if True:
        emit_ln_zn(nc, c, vsn, x_sb, zn_sb, ones_col, ones_row, 'ln',
                   eps1_col=eps_cols[:, 0:1], name="B", ones_col_bf=ones_col_bf)
    if stage == "ln":
        return

    fw_tm = sb.enter_context(nc.sbuf_tensor("b_fwtm", [128, NT * 8], F32))
    with ExitStack() as cc:
        wrs = cc.enter_context(nc.sbuf_tensor("b_wr", [128, NIC * 8], BF16))
        rld = c.enter_context(nc.semaphore("b_rld"))
        with nc.Block() as blk:
            @blk.gpsimd
            def _(g):
                g.dma_start(out=wrs[:, :].rearrange("p (ic e) -> p ic e", ic=NIC),
                            in_=io["wr"].rearrange("(ic p) e -> p ic e", p=128)).then_inc(rld, 16)
                g.wait_ge(rld, 16)
        pl = cc.enter_context(nc.psum_tensor("b_pl", [128, 2 * G], F32))
        ltm = cc.enter_context(nc.sbuf_tensor("b_ltm", [128, NT * 8], F32))
        prb = cc.enter_context(nc.sbuf_tensor("b_prb", [128, 8], F32))
        col1 = cc.enter_context(nc.sbuf_tensor("b_col1", [128, 1], F32))
        col2 = cc.enter_context(nc.sbuf_tensor("b_col2", [128, 1], F32))
        col3 = cc.enter_context(nc.sbuf_tensor("b_col3", [128, 1], F32))
        msk = cc.enter_context(nc.sbuf_tensor("b_msk", [128, 8], F32))
        rp = Plan()
        d3_prev = None
        e1_prev = None
        d1list = [None] * NT
        for tt in range(NT):
            ts8 = slice((tt % 2) * G + (tt // 2) * 8, (tt % 2) * G + (tt // 2) * 8 + 8)
            def mk_lmm(tt=tt, ts8=ts8):
                def f(t):
                    for ic in range(NIC):
                        ins = t.matmul(pl[:, ts8],
                                       zn_sb[:, ic * R + tt * 128:ic * R + (tt + 1) * 128],
                                       wrs[:, ic * 8:(ic + 1) * 8],
                                       start=(tt < 2 and ic == 0), stop=(ic == NIC - 1),
                                       skip_group_check=True)
                    return ins
                return f
            lmm = rp.add_n("pe", mk_lmm(), [d1list[tt - 2] if tt >= 2 else None])
            def mk_d1(tt=tt, ts8=ts8):
                def f(v):
                    return v.tensor_reduce(col1[:, :], pl[:, ts8], AX.X, ALU.max, negate=True)
                return f
            d1 = rp.add_n("dve", mk_d1(), [lmm, e1_prev])
            d1list[tt] = d1
            def mk_e1(tt=tt, ts8=ts8):
                def f(s):
                    return s.activation(prb[:, :], pl[:, ts8], AF.Exp, bias=col1[:, :])
                return f
            e1 = rp.add_n("act", mk_e1(), [d1, d3_prev])
            d1list[tt] = e1
            ls8 = slice(tt * 8, (tt + 1) * 8)
            def mk_d2(tt=tt, ls8=ls8):
                def f(v):
                    vsn(v, v.tensor_reduce(col2[:, :], prb[:, :], AX.X, ALU.add))
                    vsn(v, v.reciprocal(col2[:, :], col2[:, :]))
                    vsn(v, v.tensor_scalar(prb[:, :], prb[:, :], col2[:, :], None, ALU.mult))
                    vsn(v, v.tensor_reduce(col1[:, :], prb[:, :], AX.X, ALU.max))
                    vsn(v, v.tensor_scalar(msk[:, :], prb[:, :], col1[:, :], None, ALU.is_ge))
                    vsn(v, v.scalar_tensor_tensor(ltm[:, ls8], msk[:, :], -3.0, prb[:, :],
                                                  ALU.mult, ALU.add))
                    vsn(v, v.tensor_reduce(col3[:, :], ltm[:, ls8], AX.X, ALU.max))
                    return v.tensor_scalar(msk[:, :], prb[:, :], col3[:, :], None, ALU.is_ge)
                return f
            d2 = rp.add_n("dve", mk_d2(), [e1])
            def mk_e2(tt=tt):
                def f(s):
                    return s.activation(prb[:, :], prb[:, :], AF.Exp)
                return f
            e2 = rp.add_n("act", mk_e2(), [d2])
            def mk_d3(tt=tt, ls8=ls8):
                def f(v):
                    vsn(v, v.tensor_tensor(prb[:, :], prb[:, :], msk[:, :], ALU.mult))
                    vsn(v, v.tensor_reduce(col2[:, :], prb[:, :], AX.X, ALU.add))
                    vsn(v, v.reciprocal(col2[:, :], col2[:, :]))
                    return v.tensor_scalar(fw_tm[:, ls8], prb[:, :], col2[:, :], None, ALU.mult)
                return f
            d3 = rp.add_n("dve", mk_d3(), [e2])
            d3_prev, e1_prev = d3, e2
        rp.emit(nc, c, "brp")
    f_sb = sb.enter_context(nc.sbuf_tensor("b_f", [128, NFC * R], BF16))
    wb = sb.enter_context(nc.sbuf_tensor("b_wb", [128, R], F32))
    row_all = sb.enter_context(nc.sbuf_tensor("b_rowall", [1, R], F32))
    wgb = [sb.enter_context(nc.sbuf_tensor(f"b_wg{i}", [128, NIC * 128], BF16)) for i in range(2)]
    wub = [sb.enter_context(nc.sbuf_tensor(f"b_wu{i}", [128, NIC * 128], BF16)) for i in range(2)]
    wdb = [sb.enter_context(nc.sbuf_tensor(f"b_wd{i}", [128, NFC * 128], BF16)) for i in range(2)]
    stmp = sb.enter_context(nc.sbuf_tensor("b_stmp", [128, 4 * G], F32))
    ttmp = sb.enter_context(nc.sbuf_tensor("b_ttmp", [128, G], F32))
    pg = sb.enter_context(nc.psum_tensor("b_pg", [128, 4 * G], F32))
    pu = sb.enter_context(nc.psum_tensor("b_pu", [128, 4 * G], F32))
    wsem = c.enter_context(nc.semaphore("b_wsem"))
    fsem = c.enter_context(nc.semaphore("b_fsem"))
    asem = c.enter_context(nc.semaphore("b_asem"))
    esem = c.enter_context(nc.semaphore("b_esem"))

    W_E = 32 * NFC + 16 * NDC
    F_E = 2 + 2 * NFC + NDC * 2
    A_E = 1 + 4 * NFC
    E_E = 1 + 2 * NFC + 2 * NDC

    with nc.Block() as blk:
        @blk.gpsimd
        def _(gg):
            for e in range(NE):
                for fc in range(NFC):
                    if fc >= 2:
                        gg.wait_ge(fsem, 50 * e + 2 + 2 * (fc - 1))
                    elif e >= 1:
                        gg.wait_ge(fsem, 50 * e - 16 + 2 * fc)
                    gg.dma_start(
                        out=wgb[fc % 2][:, :].rearrange("p (ic f) -> p ic f", ic=NIC),
                        in_=io["wg"][e, :, fc * 128:(fc + 1) * 128].rearrange("(ic p) f -> p ic f", p=128)
                    ).then_inc(wsem, 16)
                    gg.dma_start(
                        out=wub[fc % 2][:, :].rearrange("p (ic f) -> p ic f", ic=NIC),
                        in_=io["wu"][e, :, fc * 128:(fc + 1) * 128].rearrange("(ic p) f -> p ic f", p=128)
                    ).then_inc(wsem, 16)
                for dc in (range(NDC) if DBG["down"] else []):
                    if dc >= 2:
                        gg.wait_ge(fsem, 50 * e + 34 + 2 * (dc - 1))
                    elif e >= 1:
                        gg.wait_ge(fsem, 50 * e - 2 + 2 * dc)
                    gg.dma_start(
                        out=wdb[dc % 2][:, :].rearrange("p (fc d) -> p fc d", fc=NFC),
                        in_=io["wd"][e, :, dc * 128:(dc + 1) * 128].rearrange("(fc p) d -> p fc d", p=128)
                    ).then_inc(wsem, 16)

        @blk.tensor
        def _(t):
            for e in range(NE):
                W0 = e * W_E
                if e >= 1:
                    t.wait_ge(esem, 49 * e)
                for tt in range(NT):
                    ins = t.transpose(pg[0:1, tt * 128:(tt + 1) * 128],
                                      fw_tm[:, tt * 8 + e: tt * 8 + e + 1], ident[:, :])
                ins.then_inc(fsem, 1)
                t.wait_ge(asem, 65 * e + 1)
                for g in range(NG):
                    ins = t.matmul(pu[:, g * G:(g + 1) * G], ones_row[:, :],
                                   row_all[0:1, g * G:(g + 1) * G], start=True, stop=True)
                ins.then_inc(fsem, 1)
                for fc in range(NFC):
                    t.wait_ge(wsem, W0 + 32 * (fc + 1))
                    for half in range(2):
                        u = 2 * fc + half
                        if not DBG["evac"]:
                            pass
                        elif u >= 2:
                            t.wait_ge(esem, 49 * e + u)
                        else:
                            t.wait_ge(esem, 49 * e + 1)
                        bs = (u % 2) * 2 * G
                        for ic in range(NIC):
                            for g2 in range(2):
                                g = half * 2 + g2
                                t.matmul(pg[:, bs + g2 * G: bs + (g2 + 1) * G],
                                         wgb[fc % 2][:, ic * 128:(ic + 1) * 128],
                                         zn_sb[:, ic * R + g * G:ic * R + (g + 1) * G],
                                         start=(ic == 0), stop=(ic == NIC - 1))
                        for ic in range(NIC):
                            for g2 in range(2):
                                g = half * 2 + g2
                                ins = t.matmul(pu[:, bs + g2 * G: bs + (g2 + 1) * G],
                                               wub[fc % 2][:, ic * 128:(ic + 1) * 128],
                                               zn_sb[:, ic * R + g * G:ic * R + (g + 1) * G],
                                               start=(ic == 0), stop=(ic == NIC - 1))
                        ins.then_inc(fsem, 1)
                W1 = W0 + 32 * NFC
                for dc in (range(NDC) if DBG["down"] else []):
                    t.wait_ge(wsem, W1 + 16 * (dc + 1))
                    for half in range(2):
                        u = 2 * dc + half
                        if u >= 2:
                            t.wait_ge(esem, 49 * e + 33 + u - 1)
                        else:
                            t.wait_ge(esem, 49 * e + 33)
                        bs = (u % 2) * 2 * G
                        for fc in range(NFC):
                            for g2 in range(2):
                                g = half * 2 + g2
                                ins = t.matmul(pg[:, bs + g2 * G: bs + (g2 + 1) * G],
                                               wdb[dc % 2][:, fc * 128:(fc + 1) * 128],
                                               f_sb[:, fc * R + g * G:fc * R + (g + 1) * G],
                                               start=(fc == 0), stop=(fc == NFC - 1))
                        ins.then_inc(fsem, 1)

        @blk.scalar
        def _(s):
            for e in range(NE):
                s.wait_ge(fsem, 50 * e + 1)
                s.activation(row_all[0:1, :], pg[0:1, 0:R], AF.Copy).then_inc(asem, 1)
                s.wait_ge(fsem, 50 * e + 2)
                for g in range(NG):
                    ins = s.activation(wb[:, g * G:(g + 1) * G], pu[:, g * G:(g + 1) * G],
                                       AF.Copy)
                ins.then_inc(esem, 1)
                for fc in (range(NFC) if DBG["evac"] else []):
                    for half in range(2):
                        u = 2 * fc + half
                        s.wait_ge(fsem, 50 * e + 2 + u + 1)
                        bs = (u % 2) * 2 * G
                        for g2 in range(2):
                            s.activation(stmp[:, ((u % 2) * 2 + g2) * G: ((u % 2) * 2 + g2 + 1) * G],
                                         pg[:, bs + g2 * G: bs + (g2 + 1) * G],
                                         AF.Silu).then_inc(asem, 1)

        @blk.vector
        def _(v):
            for e in range(NE):
                for fc in (range(NFC) if DBG["evac"] else []):
                    for half in range(2):
                        u = 2 * fc + half
                        bs = (u % 2) * 2 * G
                        for g2 in range(2):
                            g = half * 2 + g2
                            v.wait_ge(asem, 65 * e + 1 + 2 * u + g2 + 1)
                            ins = v.tensor_tensor(f_sb[:, fc * R + g * G:fc * R + (g + 1) * G],
                                                  stmp[:, ((u % 2) * 2 + g2) * G: ((u % 2) * 2 + g2 + 1) * G],
                                                  pu[:, bs + g2 * G: bs + (g2 + 1) * G], ALU.mult)
                        ins.then_inc(esem, 1)
                for dc in (range(NDC) if (DBG["down"] and DBG["evac"]) else []):
                    for half in range(2):
                        u = 2 * dc + half
                        bs = (u % 2) * 2 * G
                        v.wait_ge(fsem, 50 * e + 34 + u + 1)
                        for g2 in range(2):
                            g = half * 2 + g2
                            v.tensor_tensor(ttmp[:, :], pg[:, bs + g2 * G: bs + (g2 + 1) * G],
                                            wb[:, g * G:(g + 1) * G], ALU.mult)
                            xs = x_sb[:, dc * R + g * G:dc * R + (g + 1) * G]
                            ins = v.tensor_tensor(xs, xs, ttmp[:, :], ALU.add)
                        ins.then_inc(esem, 1)
    sb.close()



def emit_blockC(nc, c, io, consts):
    sb = ExitStack()
    H, NCH, M = 16, 16, 64
    ones_col, ones_row = consts["ones_col"], consts["ones_row"]
    eps_cols, ident_bf, triu = consts["eps_cols"], consts["ident_bf"], consts["triu"]
    vsn = consts["vsn"]

    zn = sb.enter_context(nc.sbuf_tensor("c_zn", [128, NIC * R], BF16))
    bcols = sb.enter_context(nc.sbuf_tensor("c_bcols", [64, 48], F32))
    bcols_bf = sb.enter_context(nc.sbuf_tensor("c_bcolsbf", [64, 48], BF16))
    boutc = sb.enter_context(nc.sbuf_tensor("c_bout", [128, 8], F32))
    mcols = sb.enter_context(nc.sbuf_tensor("c_mcols", [128, 8], F32))
    ld = c.enter_context(nc.semaphore("c_ld2"))
    ldn = [0]
    gld = c.enter_context(nc.semaphore("c_gld2"))
    gldn = [0]

    def LD(eng, *dmas):
        for d_ in dmas:
            d_.then_inc(ld, 16)
            ldn[0] += 16
        eng.wait_ge(ld, ldn[0])

    def GLD(eng, *dmas):
        for d_ in dmas:
            d_.then_inc(gld, 16)
            gldn[0] += 16
        eng.wait_ge(gld, gldn[0])

    with nc.Block() as blk:
        @blk.sync
        def _(sy):
            LD(sy, sy.dma_start(out=bcols[:, :], in_=io["bqkvC"]),
               sy.dma_start(out=boutc[:, :], in_=io["boutC"]),
               sy.dma_start(out=mcols[:, :], in_=io["maskin"]))
        @blk.gpsimd
        def _(gg):
            GLD(gg, gg.dma_start(out=bcols_bf[:, :], in_=io["bqkvC"]))

    with ExitStack() as lnx:
        x_sb = lnx.enter_context(nc.sbuf_tensor("c_x", [128, NIC * R], F32))
        with nc.Block() as blk:
            @blk.sync
            def _(sy):
                LD(sy, sy.dma_start(out=x_sb[:, :].rearrange("p (ic t) -> p ic t", ic=NIC),
                                    in_=io["xin"].rearrange("(ic p) t -> p ic t", p=128)))
        emit_ln_zn(nc, c, vsn, x_sb, zn, ones_col, ones_row, 'ln',
                   eps1_col=eps_cols[:, 0:1], name="C", ones_col_bf=consts["ident_bf"][:, 0:1] if False else consts["ones_col_bf"])

    o_all = sb.enter_context(nc.sbuf_tensor("c_oall", [128, 8 * R], BF16))
    mid = ExitStack()
    Wq_all = mid.enter_context(nc.sbuf_tensor("c_wq", [128, H * NIC * 64], BF16))
    Wk_all = mid.enter_context(nc.sbuf_tensor("c_wk", [128, H * NIC * 64], BF16))
    b8 = mid.enter_context(nc.sbuf_tensor("c_b8", [64, 2 * H], F32))
    with ExitStack() as c2s:
        wrow = [c2s.enter_context(nc.sbuf_tensor(f"c_wrow{i}", [64, NIC * 128], BF16)) for i in range(2)]
        prjall = c2s.enter_context(nc.sbuf_tensor("c_prjall", [64, H * 64], BF16))
        pcmp = c2s.enter_context(nc.psum_tensor("c_pcmp", [128, 1024], F32))
        pbb = [c2s.enter_context(nc.psum_tensor(f"c_pbb{i}", [64, 64], F32)) for i in range(2)]
        ldp = c.enter_context(nc.semaphore("c_ldp"))
        with nc.Block() as blk:
            @blk.gpsimd
            def _(gg):
                gg.dma_start(out=prjall[:, :].rearrange("p (h m) -> p h m", h=H),
                             in_=io["proj"].rearrange("h d m -> d h m")).then_inc(ldp, 16)
                gg.wait_ge(ldp, 16)
        c2p = Plan()
        prev = {0: None, 1: None}
        prev_mm = {0: None, 1: None}
        for h in range(H):
            for qk in range(2):
                i2 = (2 * h + qk) % 2
                dst = Wq_all if qk == 0 else Wk_all
                rows0 = qk * 1024 + h * 64
                def mk_ldw(h=h, qk=qk, rows0=rows0, i2=i2):
                    def f(gg, sem):
                        gg.dma_start(out=wrow[i2][:, :],
                                     in_=io["wqkvT"][rows0:rows0 + 64, :]).then_inc(sem, 16)
                    return f
                ldw = c2p.add_n("gp", mk_ldw(), [prev_mm[i2]], inc_total=16)
                def mk_cmm(h=h, qk=qk, i2=i2):
                    def f(t):
                        for ic in range(NIC):
                            t.matmul(pcmp[:, i2 * 512 + ic * 64: i2 * 512 + (ic + 1) * 64],
                                     wrow[i2][:, ic * 128:(ic + 1) * 128],
                                     prjall[:, h * 64:(h + 1) * 64],
                                     start=True, stop=True, skip_group_check=True)
                        return t.matmul(pbb[i2][:, 0:1], prjall[:, h * 64:(h + 1) * 64],
                                        bcols_bf[:, qk * 16 + h: qk * 16 + h + 1],
                                        start=True, stop=True)
                    return f
                cmm = c2p.add_n("pe", mk_cmm(), [ldw, prev[i2]])
                prev_mm[i2] = cmm
                def mk_cevac(h=h, qk=qk, dst=dst, i2=i2):
                    def f(v):
                        for ic in range(NIC):
                            v.tensor_scalar(dst[:, (h * NIC + ic) * 64:(h * NIC + ic + 1) * 64],
                                            pcmp[:, i2 * 512 + ic * 64: i2 * 512 + (ic + 1) * 64],
                                            0.125, None, ALU.mult)
                        return v.tensor_scalar(b8[:, qk * H + h: qk * H + h + 1], pbb[i2][:, 0:1],
                                               0.125, None, ALU.mult)
                    return f
                prev[i2] = c2p.add_n("dve", mk_cevac(), [cmm])
        c2p.emit(nc, c, "c2p")
    ktm = mid.enter_context(nc.sbuf_tensor("c_ktm", [128, H * NCH * 64], BF16))
    vext = mid.enter_context(nc.sbuf_tensor("c_vext", [128, H * NCH * 65], BF16))
    contrib = mid.enter_context(nc.sbuf_tensor("c_contrib", [64, H * 65], F32))
    Sg_all = mid.enter_context(nc.sbuf_tensor("c_sg", [64, H * 65], F32))
    with ExitStack() as c3s:
        wvs = c3s.enter_context(nc.sbuf_tensor("c_wvs", [128, NIC * 64], BF16))
        fscr = c3s.enter_context(nc.sbuf_tensor("c_fscr", [64, R], BF16))
        pk3 = c3s.enter_context(nc.psum_tensor("c_pk3", [64, NG * G], F32))
        ptp3 = c3s.enter_context(nc.psum_tensor("c_ptp3", [128, NCH * 64], BF16))
        pst3 = c3s.enter_context(nc.psum_tensor("c_pst3", [64, 65], F32))
        c3p = Plan()
        p_vmm = p_vact = p_vtr = p_vcopy = p_kact = p_kcopy = p_scon = None
        for h in range(H):
            def mk_ldv(h=h):
                def f(gg, sem):
                    gg.dma_start(
                        out=wvs[:, :].rearrange("p (ic f) -> p ic f", ic=NIC),
                        in_=io["wqkv"][:, 2048 + h * 64: 2048 + (h + 1) * 64].rearrange(
                            "(ic p) f -> p ic f", p=128)).then_inc(sem, 16)
                return f
            ldv = c3p.add_n("gp", mk_ldv(), [p_vmm], inc_total=16)
            def mk_vmm(h=h):
                def f(t):
                    for g in range(NG):
                        for ic in range(NIC):
                            ins = t.matmul(pk3[:, g * G:(g + 1) * G],
                                           wvs[:, ic * 64:(ic + 1) * 64],
                                           zn[:, ic * R + g * G: ic * R + (g + 1) * G],
                                           start=(ic == 0), stop=(ic == NIC - 1))
                    return ins
                return f
            vmm = c3p.add_n("pe", mk_vmm(), [ldv, p_kact])
            def mk_vact(h=h):
                def f(s):
                    for g in range(NG):
                        ins = s.activation(fscr[:, g * G:(g + 1) * G], pk3[0:64, g * G:(g + 1) * G],
                                           AF.Identity, bias=bcols[:, 32 + h: 33 + h])
                    return ins
                return f
            vact = c3p.add_n("act", mk_vact(), [vmm])
            def mk_vtr(h=h):
                def f(t):
                    for c2 in range(NCH):
                        ins = t.transpose(ptp3[:, c2 * 64:(c2 + 1) * 64],
                                          fscr[:, c2 * 128:(c2 + 1) * 128], ident_bf[0:64, 0:64])
                    return ins
                return f
            vtr = c3p.add_n("pe", mk_vtr(), [vact, p_kcopy])
            def mk_vcopy(h=h):
                def f(v):
                    v.memset(vext[:, h * NCH * 65:(h + 1) * NCH * 65], 1.0)
                    for c2 in range(NCH):
                        ins = v.tensor_copy(vext[:, (h * NCH + c2) * 65:(h * NCH + c2) * 65 + 64],
                                            ptp3[:, c2 * 64:(c2 + 1) * 64])
                    return ins
                return f
            vcopy = c3p.add_n("dve", mk_vcopy(), [vtr])
            def mk_kmm(h=h):
                def f(t):
                    for g in range(NG):
                        for ic in range(NIC):
                            ins = t.matmul(pk3[:, g * G:(g + 1) * G],
                                           Wk_all[:, (h * NIC + ic) * 64:(h * NIC + ic + 1) * 64],
                                           zn[:, ic * R + g * G: ic * R + (g + 1) * G],
                                           start=(ic == 0), stop=(ic == NIC - 1))
                    return ins
                return f
            kmm = c3p.add_n("pe", mk_kmm(), [vact])
            def mk_kact(h=h):
                def f(s):
                    for g in range(NG):
                        ins = s.activation(fscr[:, g * G:(g + 1) * G], pk3[0:64, g * G:(g + 1) * G],
                                           AF.Relu, bias=b8[:, H + h: H + h + 1])
                    return ins
                return f
            p_kact = c3p.add_n("act", mk_kact(), [kmm, vtr])
            def mk_ktr(h=h):
                def f(t):
                    for c2 in range(NCH):
                        ins = t.transpose(ptp3[:, c2 * 64:(c2 + 1) * 64],
                                          fscr[:, c2 * 128:(c2 + 1) * 128], ident_bf[0:64, 0:64])
                    return ins
                return f
            ktr = c3p.add_n("pe", mk_ktr(), [p_kact, vcopy])
            def mk_kcopy(h=h):
                def f(v):
                    for c2 in range(NCH):
                        ins = v.tensor_copy(ktm[:, (h * NCH + c2) * 64:(h * NCH + c2 + 1) * 64],
                                            ptp3[:, c2 * 64:(c2 + 1) * 64])
                    return ins
                return f
            p_kcopy = c3p.add_n("dve", mk_kcopy(), [ktr])
            def mk_smm(h=h):
                def f(t):
                    for c2 in range(NCH):
                        ins = t.matmul(pst3[:, :],
                                       ktm[:, (h * NCH + c2) * 64:(h * NCH + c2 + 1) * 64],
                                       vext[:, (h * NCH + c2) * 65:(h * NCH + c2 + 1) * 65],
                                       start=(c2 == 0), stop=(c2 == NCH - 1))
                    return ins
                return f
            smm = c3p.add_n("pe", mk_smm(), [p_kcopy, vcopy, p_scon])
            def mk_scon(h=h):
                def f(v):
                    return v.tensor_copy(contrib[:, h * 65:(h + 1) * 65], pst3[:, :])
                return f
            p_scon = c3p.add_n("dve", mk_scon(), [smm])
            p_vmm, p_vtr, p_vcopy = vmm, vtr, vcopy
        c3p.emit(nc, c, "c3p")
    ccs = c.enter_context(nc.semaphore("c_ccs2"))
    with nc.Block() as blk:
        @blk.sync
        def _(sy):
            LD(sy, sy.dma_start(out=io["cc_inC"].rearrange("(h p) d -> p h d", p=64),
                                in_=contrib[:, :].rearrange("p (h d) -> p h d", h=H)))
    with nc.Block() as blk:
        @blk.gpsimd
        def _(gg):
            gg.collective_compute("AllGather", ALU.bypass,
                                  replica_groups=[list(range(8))],
                                  ins=[io["cc_inC"].opt()],
                                  outs=[io["cc_outC"].opt()]).then_inc(ccs, 1)
            gg.wait_ge(ccs, 1)
    with ExitStack() as gsc:
        gath = gsc.enter_context(nc.sbuf_tensor("c_gath", [64, 128 * 65], F32))
        with nc.Block() as blk:
            @blk.gpsimd
            def _(gg):
                GLD(gg, gg.dma_start(out=gath[:, :].rearrange("p (q d) -> p q d", q=128),
                                     in_=io["cc_outC"].rearrange("(q p) d -> p q d", p=64)))
        with nc.Block() as blk:
            @blk.vector
            def _(v):
                v.memset(Sg_all[:, :], 0.0)
                for r in range(8):
                    v.tensor_scalar(gath[:, r * H * 65:(r + 1) * H * 65],
                                    gath[:, r * H * 65:(r + 1) * H * 65],
                                    mcols[0:64, r:r + 1], None, ALU.mult)
                    v.tensor_tensor(Sg_all[:, :], Sg_all[:, :],
                                    gath[:, r * H * 65:(r + 1) * H * 65], ALU.add)

    pe_a = c.enter_context(nc.semaphore("c_pea"))
    pe_os = c.enter_context(nc.semaphore("c_peos"))
    act_oc = c.enter_context(nc.semaphore("c_actoc"))
    dve_am = c.enter_context(nc.semaphore("c_dveam"))
    dve_s = c.enter_context(nc.semaphore("c_dves"))
    with ExitStack() as c4s:
        q_pad = c4s.enter_context(nc.sbuf_tensor("c_qpad", [128, R], BF16))
        kfm = c4s.enter_context(nc.sbuf_tensor("c_kfm", [64, R], BF16))
        of65 = c4s.enter_context(nc.sbuf_tensor("c_of65", [128, R], F32))
        S_sb = c4s.enter_context(nc.sbuf_tensor("c_ssb", [128, 65], F32))
        S_bf = c4s.enter_context(nc.sbuf_tensor("c_sbf", [128, NCH * 65], BF16))
        am_sb = c4s.enter_context(nc.sbuf_tensor("c_am", [128, 2 * 128], BF16))
        nrow = c4s.enter_context(nc.sbuf_tensor("c_nrow", [1, R], F32))
        rb_sb = c4s.enter_context(nc.sbuf_tensor("c_rb", [64, G], F32))
        pa = c4s.enter_context(nc.psum_tensor("c_pa", [128, 2 * G], F32))
        po = c4s.enter_context(nc.psum_tensor("c_po", [128, 2 * G], F32))
        ps = c4s.enter_context(nc.psum_tensor("c_ps", [128, 2 * G], F32))
        ptf = c4s.enter_context(nc.psum_tensor("c_ptf", [128, 1024], BF16))
        with nc.Block() as blk:
            @blk.vector
            def _(v):
                v.memset(q_pad[64:128, :], 0.0)
                v.memset(S_sb[64:128, :], 0.0)
        plan = Plan()
        last_pa = last_po = prev_omm = prev_mmA = prev_tc1 = None
        prev_sacc_c = prev_oadd_c = prev_inter_c = None
        for h in range(H):
            qe = None
            qm0 = qe1 = None
            for gp in range(2):
                def mk_qmm(h=h, gp=gp):
                    def f(t):
                        for g2 in range(2):
                            g = gp * 2 + g2
                            ptgt = pa if g2 == 0 else po
                            for ic in range(NIC):
                                ins = t.matmul(ptgt[0:64, 0:G],
                                               Wq_all[:, (h * NIC + ic) * 64:(h * NIC + ic + 1) * 64],
                                               zn[:, ic * R + g * G: ic * R + (g + 1) * G],
                                               start=(ic == 0), stop=(ic == NIC - 1))
                        return ins
                    return f
                deps = [last_pa, last_po] if gp == 0 else [qe]
                qm = plan.add_n("pe", mk_qmm(), deps)
                def mk_qe(h=h, gp=gp):
                    def f(s):
                        for g2 in range(2):
                            g = gp * 2 + g2
                            ptgt = pa if g2 == 0 else po
                            ins = s.activation(q_pad[0:64, g * G:(g + 1) * G], ptgt[0:64, 0:G],
                                               AF.Relu, bias=b8[:, h: h + 1])
                        return ins
                    return f
                qe = plan.add_n("act", mk_qe(), [qm, prev_omm] if gp == 0 else [qm])
            qe1 = qe
            tcs = []
            tprev = None
            for ph in range(2):
                def mk_tf(h=h, ph=ph):
                    def f(t):
                        for c2 in range(8):
                            cc2 = ph * 8 + c2
                            ins = t.transpose(ptf[0:64, c2 * 128:(c2 + 1) * 128],
                                              ktm[:, (h * NCH + cc2) * 64:(h * NCH + cc2 + 1) * 64],
                                              ident_bf[:, :])
                        return ins
                    return f
                tf = plan.add_n("pe", mk_tf(), [prev_tc1 if ph == 0 else tprev])
                def mk_tc(h=h, ph=ph):
                    def f(v):
                        return v.tensor_copy(kfm[:, ph * 1024:(ph + 1) * 1024], ptf[0:64, :])
                    return f
                tprev = plan.add_n("dve", mk_tc(), [tf, prev_mmA])
                tcs.append(tprev)
            def mk_init(h=h):
                def f(v):
                    return v.tensor_copy(S_sb[0:64, :], Sg_all[:, h * 65:(h + 1) * 65])
                return f
            init = plan.add_n("dve", mk_init(), [prev_omm])
            mmA = [None] * NCH
            mask = [None] * NCH
            intra = [None] * NCH
            smm = [None] * NCH
            ocopy = [None] * NCH
            sacc = [None] * NCH
            inter = [None] * NCH
            oadd = [None] * NCH
            for c2 in range(NCH):
                pao = (c2 % 2) * G
                def mk_mmA(h=h, c2=c2, pao=pao):
                    def f(t):
                        return t.matmul(pa[:, pao:pao + 128],
                                        kfm[0:64, c2 * 128:(c2 + 1) * 128],
                                        q_pad[0:64, c2 * 128:(c2 + 1) * 128],
                                        start=True, stop=True)
                    return f
                deps = [qe1, tcs[c2 // 8]]
                if c2 >= 2:
                    deps.append(mask[c2 - 2])
                mmA[c2] = plan.add_n("pe", mk_mmA(), deps)
                def mk_mask(h=h, c2=c2, pao=pao):
                    def f(v):
                        return v.tensor_tensor(am_sb[:, (c2 % 2) * 128:(c2 % 2) * 128 + 128],
                                               pa[:, pao:pao + 128], triu[:, :], ALU.mult)
                    return f
                mdeps = [mmA[c2]]
                mdeps.append(intra[c2 - 2] if c2 >= 2 else prev_omm)
                mask[c2] = plan.add_n("dve", mk_mask(), mdeps)
                def mk_intra(h=h, c2=c2, pao=pao):
                    def f(t):
                        return t.matmul(po[0:65, pao:pao + 128],
                                        vext[:, (h * NCH + c2) * 65:(h * NCH + c2 + 1) * 65],
                                        am_sb[:, (c2 % 2) * 128:(c2 % 2) * 128 + 128],
                                        start=True, stop=True)
                    return f
                ideps = [mask[c2]]
                if c2 >= 2:
                    ideps.append(ocopy[c2 - 2])
                else:
                    ideps += [qe1, prev_oadd_c]
                intra[c2] = plan.add_n("pe", mk_intra(), ideps)
                def mk_smm(h=h, c2=c2, pao=pao):
                    def f(t):
                        return t.matmul(ps[0:64, pao:pao + 65],
                                        ktm[:, (h * NCH + c2) * 64:(h * NCH + c2 + 1) * 64],
                                        vext[:, (h * NCH + c2) * 65:(h * NCH + c2 + 1) * 65],
                                        start=True, stop=True)
                    return f
                smm[c2] = plan.add_n("pe", mk_smm(), [sacc[c2 - 2] if c2 >= 2 else prev_sacc_c])
                def mk_ocopy(h=h, c2=c2, pao=pao):
                    def f(s):
                        return s.activation(of65[0:65, c2 * 128:(c2 + 1) * 128],
                                            po[0:65, pao:pao + 128], AF.Copy)
                    return f
                ocopy[c2] = plan.add_n("act", mk_ocopy(), [intra[c2]])
                def mk_sacc(h=h, c2=c2, pao=pao):
                    def f(v):
                        v.tensor_copy(S_bf[:, c2 * 65:(c2 + 1) * 65], S_sb[:, :])
                        return v.tensor_tensor(S_sb[0:64, :], S_sb[0:64, :],
                                               ps[0:64, pao:pao + 65], ALU.add)
                    return f
                sadeps = [smm[c2], init if c2 == 0 else None,
                          prev_inter_c if c2 <= 1 else None]
                sacc[c2] = plan.add_n("dve", mk_sacc(), sadeps)
            for c2 in range(NCH):
                pao = (c2 % 2) * G
                def mk_inter(h=h, c2=c2, pao=pao):
                    def f(t):
                        return t.matmul(po[0:65, pao:pao + 128],
                                        S_bf[:, c2 * 65:(c2 + 1) * 65],
                                        q_pad[:, c2 * 128:(c2 + 1) * 128],
                                        start=True, stop=True)
                    return f
                rdeps = [sacc[c2], ocopy[NCH - 2 + (c2 % 2)] if c2 < 2 else oadd[c2 - 2]]
                inter[c2] = plan.add_n("pe", mk_inter(), rdeps)
                def mk_oadd(h=h, c2=c2, pao=pao):
                    def f(v):
                        return v.tensor_tensor(of65[0:65, c2 * 128:(c2 + 1) * 128],
                                               of65[0:65, c2 * 128:(c2 + 1) * 128],
                                               po[0:65, pao:pao + 128], ALU.add)
                    return f
                oadd[c2] = plan.add_n("dve", mk_oadd(), [inter[c2], ocopy[c2]])
            def mk_nrow(h=h):
                def f(v):
                    vsn(v, v.tensor_scalar(nrow[0:1, :], of65[64:65, :], eps_cols[0:1, 1:2],
                                           None, ALU.add))
                    return v.reciprocal(nrow[0:1, :], nrow[0:1, :])
                return f
            nr = plan.add_n("dve", mk_nrow(), [oadd[NCH - 1]])
            rb_prev = None
            for g in range(NG):
                def mk_nb(h=h, g=g):
                    def f(t):
                        return t.matmul(pa[0:64, 0:G], ones_row[:, 0:64],
                                        nrow[0:1, g * G:(g + 1) * G], start=True, stop=True)
                    return f
                nbdeps = [nr, mask[NCH - 1]]
                if rb_prev is not None:
                    nbdeps.append(rb_prev)
                nb = plan.add_n("pe", mk_nb(), nbdeps)
                def mk_rb(h=h, g=g):
                    def f(s):
                        return s.activation(rb_sb[:, :], pa[0:64, 0:G], AF.Copy)
                    return f
                rb_prev = plan.add_n("act", mk_rb(), [nb])
                def mk_on(h=h, g=g):
                    def f(v):
                        base = (h % 2) * 64
                        return v.tensor_tensor(
                            o_all[base:base + 64, (h // 2) * R + g * G:(h // 2) * R + (g + 1) * G],
                            of65[0:64, g * G:(g + 1) * G], rb_sb[:, :], ALU.mult)
                    return f
                onref = plan.add_n("dve", mk_on(), [rb_prev])
            last_pa = rb_prev
            last_po = oadd[NCH - 1]
            prev_omm = inter[NCH - 1]
            prev_mmA = mmA[NCH - 1]
            prev_tc1 = tcs[1]
            prev_sacc_c = sacc[NCH - 1]
            prev_oadd_c = oadd[NCH - 1]
            prev_inter_c = inter[NCH - 1]
        plan.emit(nc, c, "c4p")
    mid.close()
    wosem = c.enter_context(nc.semaphore("c_wosem"))
    woe = c.enter_context(nc.semaphore("c_woe"))
    with ExitStack() as c5s:
        wo_sb = c5s.enter_context(nc.sbuf_tensor("c_wo2", [128, NIC * 1024], BF16))
        xg = c5s.enter_context(nc.sbuf_tensor("c_xg", [128, 8 * G], F32))
        osb = c5s.enter_context(nc.sbuf_tensor("c_osb", [128, 8 * G], F32))
        scr5 = c5s.enter_context(nc.sbuf_tensor("c_scr5", [128, 2 * G], F32))
        pwo = c5s.enter_context(nc.psum_tensor("c_pwo", [128, 2 * G], F32))
        with nc.Block() as blk:
            @blk.gpsimd
            def _(gg):
                GLD(gg, gg.dma_start(out=wo_sb[:, :].rearrange("p (ic f) -> p ic f", ic=NIC),
                                     in_=io["wout"].rearrange("(ic p) f -> p ic f", p=128)))
        won = [0, 0]
        for g in range(NG):
            with nc.Block() as blk:
                @blk.sync
                def _(sy):
                    LD(sy, sy.dma_start(out=xg[:, :].rearrange("p (ic t) -> p ic t", ic=8),
                                        in_=io["xin"][:, g * G:(g + 1) * G].rearrange(
                                            "(ic p) t -> p ic t", p=128)))
            with nc.Block() as blk:
                @blk.tensor
                def _(t):
                    for oc in range(NIC):
                        if oc >= 2:
                            t.wait_ge(woe, won[1] + oc - 1)
                        for kc in range(8):
                            ins = t.matmul(pwo[:, (oc % 2) * G:(oc % 2 + 1) * G],
                                           wo_sb[:, kc * 1024 + oc * 128: kc * 1024 + (oc + 1) * 128],
                                           o_all[:, kc * R + g * G:kc * R + (g + 1) * G],
                                           start=(kc == 0), stop=(kc == 7))
                        ins.then_inc(wosem, 1)
                @blk.scalar
                def _(s):
                    for oc in range(NIC):
                        s.wait_ge(wosem, won[0] + 2 * oc + 1)
                        if oc >= 2:
                            s.wait_ge(woe, won[1] + oc - 1)
                        s.activation(scr5[:, (oc % 2) * G:(oc % 2 + 1) * G],
                                     pwo[:, (oc % 2) * G:(oc % 2 + 1) * G],
                                     AF.Identity, bias=boutc[:, oc:oc + 1]).then_inc(wosem, 1)
                @blk.vector
                def _(v):
                    for oc in range(NIC):
                        v.wait_ge(wosem, won[0] + 2 * oc + 2)
                        v.tensor_tensor(osb[:, oc * G:(oc + 1) * G],
                                        scr5[:, (oc % 2) * G:(oc % 2 + 1) * G],
                                        xg[:, oc * G:(oc + 1) * G], ALU.add).then_inc(woe, 1)
            won[0] += 2 * NIC
            won[1] += NIC
            with nc.Block() as blk:
                @blk.sync
                def _(sy):
                    LD(sy, sy.dma_start(out=io["xout"][:, g * G:(g + 1) * G].rearrange(
                                            "(ic p) t -> p ic t", p=128),
                                        in_=osb[:, :].rearrange("p (ic t) -> p ic t", ic=8)))
    sb.close()


def build_C(_unused=None):
    nc = bass.Bass("TRN2", target_bir_lowering=False, detect_race_conditions=False)
    io = declare_consts(nc)
    io["xin"] = nc.dram_tensor("xin", [D, R], F32, kind="ExternalInput").ap()
    io["wqkv"] = nc.dram_tensor("wqkv", [D, 3072], F32, kind="ExternalInput").ap()
    io["wqkvT"] = nc.dram_tensor("wqkvT", [3072, D], F32, kind="ExternalInput").ap()
    io["proj"] = nc.dram_tensor("proj", [16, 64, 64], F32, kind="ExternalInput").ap()
    io["wout"] = nc.dram_tensor("wout", [D, D], F32, kind="ExternalInput").ap()
    io["bqkvC"] = nc.dram_tensor("bqkvC", [64, 48], F32, kind="ExternalInput").ap()
    io["boutC"] = nc.dram_tensor("boutC", [128, 8], F32, kind="ExternalInput").ap()
    io["maskin"] = nc.dram_tensor("maskin", [128, 8], F32, kind="ExternalInput").ap()
    io["xout"] = nc.dram_tensor("xout", [D, R], F32, kind="ExternalOutput").ap()
    io["cc_inC"] = nc.dram_tensor("cc_inC", [1024, 65], F32).ap()
    io["cc_outC"] = nc.dram_tensor("cc_outC", [8192, 65], F32).ap()
    with ExitStack() as c:
        consts = load_consts(nc, c, io)
        consts["vsn"] = VSync(c, nc, "c_vs")
        emit_blockC(nc, c, io, consts)
    return nc


def dC_host_inputs(inputs):
    m = {}
    m["wqkv"] = f32c(inputs["L1_attn_w_qkv"])
    m["wqkvT"] = f32c(np.asarray(inputs["L1_attn_w_qkv"]).T)
    m["proj"] = f32c(inputs["L1_attn_proj"])
    m["wout"] = f32c(inputs["L1_attn_w_out"])
    m["bqkvC"] = f32c(np.asarray(inputs["L1_attn_b_qkv"]).reshape(48, 64).T)
    m["boutC"] = f32c(np.asarray(inputs["L1_attn_b_out"]).reshape(8, 128).T)
    return m


def run_C(x2_fm_shards, inputs):
    nc = build_C()
    ca = const_arrays()
    dc = dC_host_inputs(inputs)
    mks = core_masks()
    in_maps = []
    for i in range(NCORES):
        m = dict(ca)
        m.update(dc)
        m["xin"] = f32c(x2_fm_shards[i])
        m["maskin"] = mks[i]
        in_maps.append(m)
    res = run_bass_kernel_spmd(nc, in_maps, core_ids=list(range(NCORES)))
    return [np.asarray(r["xout"], dtype=np.float32) for r in res.results]



def emit_ffn(nc, c, sems, name, kind, wins, src_sb, src_cols, src_goffs,
             noc, nic, dst_sb=None, dst_cols=None, dst_goffs=None,
             scol1=None, scol2=None, xin_ap=None, out_ap=None, out_rows0=0,
             tok_offs=None, x_sb=None, plan=None, src_ready=None,
             bufc=None, shared_psum=None):
    ng = len(src_goffs)
    if dst_goffs is None:
        dst_goffs = src_goffs
    nw = len(wins)
    own = plan is None
    if own:
        plan = Plan()
    local = ExitStack()
    cc = bufc if bufc is not None else local
    if True:
        wb_ = [[cc.enter_context(nc.sbuf_tensor(f"{name}_w{j}_{i}", [128, nic * 128], BF16))
                for i in range(2)] for j in range(nw)]
        if shared_psum is None:
            pg = cc.enter_context(nc.psum_tensor(name + "_pg", [128, 2 * ng * G], F32))
            pu = cc.enter_context(nc.psum_tensor(name + "_pu", [128, 2 * ng * G], F32)) if kind == 'silu2' else None
            bank_user = {0: None, 1: None}
        else:
            pg, pu, bank_user = shared_psum
        stmp = cc.enter_context(nc.sbuf_tensor(name + "_st", [128, 2 * ng * G], BF16)) if kind == 'silu2' else None
        if kind == 'w3add':
            xbuf_ = cc.enter_context(nc.sbuf_tensor(name + "_xb", [128, ng * G], F32))
            xbuf = [xbuf_, xbuf_]
            osb = [cc.enter_context(nc.sbuf_tensor(f"{name}_ob{i}", [128, ng * G], F32)) for i in range(2)]
        mm_r = [None] * noc
        ev_r = [None] * noc
        st_r = [None] * noc
        lx_r = [None] * noc
        for u in range(noc):
            def mk_ldw(u=u):
                def f(gg, sem):
                    for j in range(nw):
                        gg.dma_start(
                            out=wb_[j][u % 2][:, :].rearrange("p (ic f) -> p ic f", ic=nic),
                            in_=wins[j][:, u * 128:(u + 1) * 128].rearrange("(ic p) f -> p ic f", p=128)
                        ).then_inc(sem, 16)
                return f
            ldw = plan.add_n("gp", mk_ldw(), [mm_r[u - 2] if u >= 2 else None],
                             inc_total=16 * nw)
            def mk_mm(u=u):
                def f(t):
                    bs = (u % 2) * ng * G
                    for j in range(nw):
                        ptgt = pg if j == 0 else pu
                        for ic in range(nic):
                            for g in range(ng):
                                ins = t.matmul(ptgt[:, bs + g * G: bs + (g + 1) * G],
                                               wb_[j][u % 2][:, ic * 128:(ic + 1) * 128],
                                               src_sb[:, ic * src_cols + src_goffs[g]: ic * src_cols + src_goffs[g] + G],
                                               start=(ic == 0), stop=(ic == nic - 1))
                    return ins
                return f
            mdeps = [ldw, bank_user[u % 2]]
            if u == 0 and src_ready is not None:
                mdeps.append(src_ready)
            mm_r[u] = plan.add_n("pe", mk_mm(), mdeps)
            bs = (u % 2) * ng * G
            if kind == 'silu2':
                def mk_ea(u=u, bs=bs):
                    def f(s):
                        for g in range(ng):
                            kw = {} if scol1 is None else {"scale": scol1[:, u:u + 1]}
                            ins = s.activation(stmp[:, bs + g * G: bs + (g + 1) * G],
                                               pg[:, bs + g * G: bs + (g + 1) * G],
                                               AF.Silu, **kw)
                        return ins
                    return f
                ea = plan.add_n("act", mk_ea(), [mm_r[u], ev_r[u - 2] if u >= 2 else None])
                def mk_ed(u=u, bs=bs):
                    def f(v):
                        for g in range(ng):
                            sc = 1.0 if scol2 is None else scol2[:, u:u + 1]
                            ins = v.scalar_tensor_tensor(
                                dst_sb[:, u * dst_cols + dst_goffs[g]: u * dst_cols + dst_goffs[g] + G],
                                pu[:, bs + g * G: bs + (g + 1) * G], sc,
                                stmp[:, bs + g * G: bs + (g + 1) * G],
                                ALU.mult, ALU.mult)
                        return ins
                    return f
                ev_r[u] = plan.add_n("dve", mk_ed(), [ea])
                bank_user[u % 2] = ev_r[u]
            elif kind == 'gelu':
                def mk_ea(u=u, bs=bs):
                    def f(s):
                        for g in range(ng):
                            ins = s.activation(
                                dst_sb[:, u * dst_cols + dst_goffs[g]: u * dst_cols + dst_goffs[g] + G],
                                pg[:, bs + g * G: bs + (g + 1) * G],
                                AF.Gelu, scale=scol1[:, u:u + 1])
                        return ins
                    return f
                ev_r[u] = plan.add_n("act", mk_ea(), [mm_r[u]])
                bank_user[u % 2] = ev_r[u]
            elif kind == 'addx':
                def mk_ed(u=u, bs=bs):
                    def f(v):
                        for g in range(ng):
                            xsl = x_sb[:, u * R + dst_goffs[g]: u * R + dst_goffs[g] + G]
                            ins = v.tensor_tensor(xsl, xsl, pg[:, bs + g * G: bs + (g + 1) * G],
                                                  ALU.add)
                        return ins
                    return f
                ev_r[u] = plan.add_n("dve", mk_ed(), [mm_r[u]])
                bank_user[u % 2] = ev_r[u]
            elif kind == 'w3add':
                def mk_lx(u=u):
                    def f(sy, sem):
                        sy.dma_start(out=xbuf[u % 2][:, :],
                                     in_=xin_ap[u * 128:(u + 1) * 128,
                                                tok_offs:tok_offs + ng * G]).then_inc(sem, 16)
                    return f
                lx_r[u] = plan.add_n("sy", mk_lx(), [ev_r[u - 1] if u >= 1 else None],
                                     inc_total=16, chan="syl")
                def mk_ed(u=u, bs=bs):
                    def f(v):
                        for g in range(ng):
                            v.tensor_scalar(osb[u % 2][:, g * G:(g + 1) * G],
                                            pg[:, bs + g * G: bs + (g + 1) * G],
                                            scol1[:, u:u + 1], None, ALU.mult)
                            ins = v.tensor_tensor(osb[u % 2][:, g * G:(g + 1) * G],
                                                  osb[u % 2][:, g * G:(g + 1) * G],
                                                  xbuf[u % 2][:, g * G:(g + 1) * G], ALU.add)
                        return ins
                    return f
                ev_r[u] = plan.add_n("dve", mk_ed(), [mm_r[u], lx_r[u],
                                                      st_r[u - 2] if u >= 2 else None])
                def mk_st(u=u):
                    def f(sy, sem):
                        sy.dma_start(out=out_ap[u * 128:(u + 1) * 128,
                                                tok_offs:tok_offs + ng * G],
                                     in_=osb[u % 2][:, :]).then_inc(sem, 16)
                    return f
                st_r[u] = plan.add_n("sy", mk_st(), [ev_r[u]], inc_total=16, chan="sys")
                bank_user[u % 2] = ev_r[u]
        last = st_r[noc - 1] if kind == 'w3add' else ev_r[noc - 1]
        if own:
            plan.emit(nc, c, name)
    local.close()
    return last



def emit_blockA(nc, c, io, consts):
    sb = ExitStack()
    H, NCH = 8, 16
    ones_col, ones_row = consts["ones_col"], consts["ones_row"]
    eps_cols, ident_bf, triu = consts["eps_cols"], consts["ident_bf"], consts["triu"]
    vsn = consts["vsn"]

    rows_s1 = sb.enter_context(nc.sbuf_tensor("a_rs1", [1, R], F32))
    rows_c1 = sb.enter_context(nc.sbuf_tensor("a_rc1", [1, R], F32))
    gw = sb.enter_context(nc.sbuf_tensor("a_gw", [128, 17], F32))
    mcols = sb.enter_context(nc.sbuf_tensor("a_mcols", [128, 8], F32))
    ld = c.enter_context(nc.semaphore("a_ld"))
    ldn = [0]
    gld = c.enter_context(nc.semaphore("a_gld"))
    gldn = [0]

    def LD(eng, *dmas):
        for d_ in dmas:
            d_.then_inc(ld, 16)
            ldn[0] += 16
        eng.wait_ge(ld, ldn[0])

    def GLD(eng, *dmas):
        for d_ in dmas:
            d_.then_inc(gld, 16)
            gldn[0] += 16
        eng.wait_ge(gld, gldn[0])

    with nc.Block() as blk:
        @blk.sync
        def _(sy):
            LD(sy, sy.dma_start(out=gw[:, :], in_=io["gwA"]),
               sy.dma_start(out=mcols[:, :], in_=io["maskin"]))
    wcolA, wcolM, w128 = gw[:, 0:8], gw[:, 8:16], gw[:, 16:17]

    with ExitStack() as att:
        h_attn = att.enter_context(nc.sbuf_tensor("a_h", [128, NIC * R], BF16))
        with ExitStack() as lnx:
            x_sb = lnx.enter_context(nc.sbuf_tensor("a_x", [128, NIC * R], F32))
            gwrow = lnx.enter_context(nc.sbuf_tensor("a_gwrow", [1, D], F32))
            with nc.Block() as blk:
                @blk.sync
                def _(sy):
                    LD(sy, sy.dma_start(out=x_sb[:, :].rearrange("p (ic t) -> p ic t", ic=NIC),
                                        in_=io["xin"].rearrange("(ic p) t -> p ic t", p=128)),
                       sy.dma_start(out=gwrow[:, :], in_=io["wrowA"]))
            emit_ln_zn(nc, c, vsn, x_sb, h_attn, ones_col, ones_row, 'lnrms',
                       eps1_col=eps_cols[:, 0:1], eps2_col=eps_cols[:, 0:1], name="A",
                       gain_rowtile=gwrow, gain_coltile=wcolA,
                       rows_s1=rows_s1, rows_c1=rows_c1,
                       ones_col_bf=consts["ones_col_bf"])

        ktm = att.enter_context(nc.sbuf_tensor("a_ktm", [128, H * R], BF16))
        vtm = att.enter_context(nc.sbuf_tensor("a_vtm", [128, H * R], BF16))
        contrib = att.enter_context(nc.sbuf_tensor("a_contrib", [128, H * 128], F32))
        S_g = att.enter_context(nc.sbuf_tensor("a_sg", [128, H * 128], F32))
        with ExitStack() as a2c:
            wsl = a2c.enter_context(nc.sbuf_tensor("a_wsl", [128, 2 * NIC * 128], BF16))
            fm_scr = a2c.enter_context(nc.sbuf_tensor("a_fmscr", [128, R], BF16))
            pk = a2c.enter_context(nc.psum_tensor("a_pk", [128, NG * G], F32))
            ptp = a2c.enter_context(nc.psum_tensor("a_ptp", [128, NCH * 128], BF16))
            pst = a2c.enter_context(nc.psum_tensor("a_pst", [128, 128], F32))
            a2p = Plan()
            prev_vmm = prev_vevac = prev_ktr = prev_vtr = None
            prev_kcopy = prev_vcopy = prev_scon = None
            for h in range(H):
                def mk_ld(h=h):
                    def f(gg, sem):
                        gg.dma_start(
                            out=wsl[:, 0:NIC * 128].rearrange("p (ic f) -> p ic f", ic=NIC),
                            in_=io["wk"][:, h * 128:(h + 1) * 128].rearrange("(ic p) f -> p ic f", p=128)
                        ).then_inc(sem, 16)
                        gg.dma_start(
                            out=wsl[:, NIC * 128:].rearrange("p (ic f) -> p ic f", ic=NIC),
                            in_=io["wv"][:, h * 128:(h + 1) * 128].rearrange("(ic p) f -> p ic f", p=128)
                        ).then_inc(sem, 16)
                    return f
                ldr = a2p.add_n("gp", mk_ld(), [prev_vmm], inc_total=32)
                def mk_kvmm(h=h, kv=0):
                    def f(t):
                        for g in range(NG):
                            for ic in range(NIC):
                                ins = t.matmul(pk[:, g * G:(g + 1) * G],
                                               wsl[:, kv * NIC * 128 + ic * 128: kv * NIC * 128 + (ic + 1) * 128],
                                               h_attn[:, ic * R + g * G: ic * R + (g + 1) * G],
                                               start=(ic == 0), stop=(ic == NIC - 1))
                        return ins
                    return f
                kmm = a2p.add_n("pe", mk_kvmm(h, 0), [ldr, prev_vevac])
                def mk_evac(h=h):
                    def f(v):
                        for g in range(NG):
                            ins = v.tensor_copy(fm_scr[:, g * G:(g + 1) * G], pk[:, g * G:(g + 1) * G])
                        return ins
                    return f
                kevac = a2p.add_n("dve", mk_evac(), [kmm, prev_ktr])
                def mk_tr(h=h):
                    def f(t):
                        for c2 in range(NCH):
                            ins = t.transpose(ptp[:, c2 * 128:(c2 + 1) * 128],
                                              fm_scr[:, c2 * 128:(c2 + 1) * 128], ident_bf[:, :])
                        return ins
                    return f
                ktr = a2p.add_n("pe", mk_tr(), [kevac, prev_vcopy])
                def mk_copy(h=h, dstsel=0):
                    def f(v):
                        dst = ktm if dstsel == 0 else vtm
                        return v.tensor_copy(dst[:, h * R:(h + 1) * R], ptp[:, :])
                    return f
                kcopy = a2p.add_n("dve", mk_copy(h, 0), [ktr])
                vmm = a2p.add_n("pe", mk_kvmm(h, 1), [kevac])
                vevac = a2p.add_n("dve", mk_evac(), [vmm, ktr])
                vtr = a2p.add_n("pe", mk_tr(), [vevac, kcopy])
                vcopy = a2p.add_n("dve", mk_copy(h, 1), [vtr])
                def mk_smm(h=h):
                    def f(t):
                        for c2 in range(NCH):
                            ins = t.matmul(pst[:, :], ktm[:, h * R + c2 * 128: h * R + (c2 + 1) * 128],
                                           vtm[:, h * R + c2 * 128: h * R + (c2 + 1) * 128],
                                           start=(c2 == 0), stop=(c2 == NCH - 1))
                        return ins
                    return f
                smm = a2p.add_n("pe", mk_smm(), [kcopy, vcopy, prev_scon])
                def mk_scon(h=h):
                    def f(v):
                        return v.tensor_copy(contrib[:, h * 128:(h + 1) * 128], pst[:, :])
                    return f
                scon = a2p.add_n("dve", mk_scon(), [smm])
                prev_vmm, prev_vevac, prev_ktr, prev_vtr = vmm, vevac, ktr, vtr
                prev_kcopy, prev_vcopy, prev_scon = kcopy, vcopy, scon
            a2p.emit(nc, c, "a2p")

        ccs = c.enter_context(nc.semaphore("a_ccs"))
        with nc.Block() as blk:
            @blk.sync
            def _(sy):
                LD(sy, sy.dma_start(out=io["cc_in"].rearrange("(h p) d -> p h d", p=128),
                                    in_=contrib[:, :].rearrange("p (h d) -> p h d", h=H)))
        with nc.Block() as blk:
            @blk.gpsimd
            def _(gg):
                gg.collective_compute("AllGather", ALU.bypass,
                                      replica_groups=[list(range(8))],
                                      ins=[io["cc_in"].opt()],
                                      outs=[io["cc_out"].opt()]).then_inc(ccs, 1)
                gg.wait_ge(ccs, 1)
        with ExitStack() as gsc:
            gath = gsc.enter_context(nc.sbuf_tensor("a_gath", [128, 64 * 128], F32))
            with nc.Block() as blk:
                @blk.gpsimd
                def _(gg):
                    GLD(gg, gg.dma_start(out=gath[:, :].rearrange("p (q d) -> p q d", q=64),
                                         in_=io["cc_out"].rearrange("(q p) d -> p q d", p=128)))
            with nc.Block() as blk:
                @blk.vector
                def _(v):
                    v.memset(S_g[:, :], 0.0)
                    for r in range(8):
                        v.tensor_scalar(gath[:, r * 1024:(r + 1) * 1024],
                                        gath[:, r * 1024:(r + 1) * 1024],
                                        mcols[:, r:r + 1], None, ALU.mult)
                        v.tensor_tensor(S_g[:, :], S_g[:, :], gath[:, r * 1024:(r + 1) * 1024], ALU.add)

        pe_a = c.enter_context(nc.semaphore("a_pea"))
        pe_os = c.enter_context(nc.semaphore("a_peos"))
        act_oc = c.enter_context(nc.semaphore("a_actoc"))
        dve_am = c.enter_context(nc.semaphore("a_dveam"))
        dve_s = c.enter_context(nc.semaphore("a_dves"))
        with ExitStack() as a3c:
            q_bf = a3c.enter_context(nc.sbuf_tensor("a_q", [128, R], BF16))
            kfm = a3c.enter_context(nc.sbuf_tensor("a_kfm", [128, R], BF16))
            o_f32 = a3c.enter_context(nc.sbuf_tensor("a_of", [128, R], F32))
            o_nb = a3c.enter_context(nc.sbuf_tensor("a_onb", [128, R], BF16))
            wsl3 = a3c.enter_context(nc.sbuf_tensor("a_wsl3", [128, NIC * 128], BF16))
            S_sb = a3c.enter_context(nc.sbuf_tensor("a_ssb", [128, 128], F32))
            S_bf = a3c.enter_context(nc.sbuf_tensor("a_sbf", [128, NCH * 128], BF16))
            am_sb = a3c.enter_context(nc.sbuf_tensor("a_am", [128, 2 * 128], BF16))
            nr_m = a3c.enter_context(nc.sbuf_tensor("a_nrm", [1, R], F32))
            nr_s = a3c.enter_context(nc.sbuf_tensor("a_nrs", [1, R], F32))
            pa = a3c.enter_context(nc.psum_tensor("a_pa", [128, 2 * G], F32))
            po = a3c.enter_context(nc.psum_tensor("a_po", [128, 2 * G], F32))
            ps = a3c.enter_context(nc.psum_tensor("a_ps", [128, 2 * G], F32))
            ptf = a3c.enter_context(nc.psum_tensor("a_ptf", [128, 1024], BF16))
            sq_scr = a3c.enter_context(nc.sbuf_tensor("a_sqscr", [128, 2 * G], F32))
            rb_scr = a3c.enter_context(nc.sbuf_tensor("a_rbscr", [128, G], F32))
            pnq = a3c.enter_context(nc.psum_tensor("a_pnq", [128, G], F32))
            a3p = Plan()
            prev_qmm = prev_mask15 = prev_ocopy15 = prev_omm15 = None
            prev_supd15 = prev_ktc1 = prev_spill = prev_rbc3 = None
            prev_sacc15 = prev_oadd15 = prev_inter15 = None
            for h in range(H):
                def mk_ldq(h=h):
                    def f(gg, sem):
                        gg.dma_start(
                            out=wsl3[:, :].rearrange("p (ic f) -> p ic f", ic=NIC),
                            in_=io["wq"][:, h * 128:(h + 1) * 128].rearrange("(ic p) f -> p ic f", p=128)
                        ).then_inc(sem, 16)
                    return f
                ldq = a3p.add_n("gp", mk_ldq(), [prev_qmm], inc_total=16)
                qev = None
                for gp in range(2):
                    def mk_qmm(h=h, gp=gp):
                        def f(t):
                            for g2 in range(2):
                                g = gp * 2 + g2
                                ptgt = pa if g2 == 0 else po
                                for ic in range(NIC):
                                    ins = t.matmul(ptgt[:, 0:G],
                                                   wsl3[:, ic * 128:(ic + 1) * 128],
                                                   h_attn[:, ic * R + g * G: ic * R + (g + 1) * G],
                                                   start=(ic == 0), stop=(ic == NIC - 1))
                            return ins
                        return f
                    deps = [ldq] if gp == 0 else [qev]
                    if gp == 0:
                        deps += [prev_mask15, prev_ocopy15, prev_rbc3]
                    qmm = a3p.add_n("pe", mk_qmm(), deps)
                    def mk_qev(h=h, gp=gp):
                        def f(v):
                            for g2 in range(2):
                                g = gp * 2 + g2
                                ptgt = pa if g2 == 0 else po
                                ins = v.tensor_copy(q_bf[:, g * G:(g + 1) * G], ptgt[:, 0:G])
                            return ins
                        return f
                    qev = a3p.add_n("dve", mk_qev(), [qmm])
                prev_qmm = qmm
                ktcs = []
                tprev = prev_ktc1
                for ph in range(2):
                    def mk_ktf(h=h, ph=ph):
                        def f(t):
                            for c2 in range(8):
                                cc2 = ph * 8 + c2
                                ins = t.transpose(ptf[:, c2 * 128:(c2 + 1) * 128],
                                                  ktm[:, h * R + cc2 * 128: h * R + (cc2 + 1) * 128],
                                                  ident_bf[:, :])
                            return ins
                        return f
                    ktf = a3p.add_n("pe", mk_ktf(), [tprev])
                    def mk_ktc(h=h, ph=ph):
                        def f(v):
                            return v.tensor_copy(kfm[:, ph * 1024:(ph + 1) * 1024], ptf[:, :])
                        return f
                    tprev = a3p.add_n("dve", mk_ktc(), [ktf])
                    ktcs.append(tprev)
                prev_ktc1 = ktcs[1]
                def mk_init(h=h):
                    def f(v):
                        return v.tensor_copy(S_sb[:, :], S_g[:, h * 128:(h + 1) * 128])
                    return f
                init = a3p.add_n("dve", mk_init(), [prev_omm15])
                mmA = [None] * NCH
                mask = [None] * NCH
                intra = [None] * NCH
                smm = [None] * NCH
                ocopy = [None] * NCH
                sacc = [None] * NCH
                inter = [None] * NCH
                oadd = [None] * NCH
                for c2 in range(NCH):
                    pao = (c2 % 2) * G
                    def mk_mmA(h=h, c2=c2, pao=pao):
                        def f(t):
                            return t.matmul(pa[:, pao:pao + 128],
                                            kfm[:, c2 * 128:(c2 + 1) * 128],
                                            q_bf[:, c2 * 128:(c2 + 1) * 128],
                                            start=True, stop=True)
                        return f
                    deps = [qev, ktcs[c2 // 8]]
                    if c2 >= 2:
                        deps.append(mask[c2 - 2])
                    mmA[c2] = a3p.add_n("pe", mk_mmA(), deps)
                    def mk_mask(h=h, c2=c2, pao=pao):
                        def f(v):
                            return v.tensor_tensor(am_sb[:, (c2 % 2) * 128:(c2 % 2) * 128 + 128],
                                                   pa[:, pao:pao + 128], triu[:, :], ALU.mult)
                        return f
                    mask[c2] = a3p.add_n("dve", mk_mask(),
                                         [mmA[c2], intra[c2 - 2] if c2 >= 2 else prev_omm15])
                    def mk_intra(h=h, c2=c2, pao=pao):
                        def f(t):
                            return t.matmul(po[:, pao:pao + 128],
                                            vtm[:, h * R + c2 * 128: h * R + (c2 + 1) * 128],
                                            am_sb[:, (c2 % 2) * 128:(c2 % 2) * 128 + 128],
                                            start=True, stop=True)
                        return f
                    ideps = [mask[c2]]
                    if c2 >= 2:
                        ideps.append(ocopy[c2 - 2])
                    else:
                        ideps += [qev, prev_oadd15]
                    intra[c2] = a3p.add_n("pe", mk_intra(), ideps)
                    def mk_smm(h=h, c2=c2, pao=pao):
                        def f(t):
                            return t.matmul(ps[:, pao:pao + 128],
                                            ktm[:, h * R + c2 * 128: h * R + (c2 + 1) * 128],
                                            vtm[:, h * R + c2 * 128: h * R + (c2 + 1) * 128],
                                            start=True, stop=True)
                        return f
                    sdeps = [sacc[c2 - 2] if c2 >= 2 else prev_sacc15]
                    smm[c2] = a3p.add_n("pe", mk_smm(), sdeps)
                    def mk_ocopy(h=h, c2=c2, pao=pao):
                        def f(s):
                            return s.activation(o_f32[:, c2 * 128:(c2 + 1) * 128],
                                                po[:, pao:pao + 128], AF.Copy)
                        return f
                    ocopy[c2] = a3p.add_n("act", mk_ocopy(), [intra[c2]])
                    def mk_sacc(h=h, c2=c2, pao=pao):
                        def f(v):
                            v.tensor_copy(S_bf[:, c2 * 128:(c2 + 1) * 128], S_sb[:, :])
                            return v.tensor_tensor(S_sb[:, :], S_sb[:, :],
                                                   ps[:, pao:pao + 128], ALU.add)
                        return f
                    sadeps = [smm[c2], init if c2 == 0 else None, prev_inter15 if c2 <= 1 else None]
                    sacc[c2] = a3p.add_n("dve", mk_sacc(), sadeps)
                for c2 in range(NCH):
                    pao = (c2 % 2) * G
                    def mk_inter(h=h, c2=c2, pao=pao):
                        def f(t):
                            return t.matmul(po[:, pao:pao + 128],
                                            S_bf[:, c2 * 128:(c2 + 1) * 128],
                                            q_bf[:, c2 * 128:(c2 + 1) * 128],
                                            start=True, stop=True)
                        return f
                    rdeps = [sacc[c2], ocopy[NCH - 2 + (c2 % 2)] if c2 < 2 else oadd[c2 - 2]]
                    inter[c2] = a3p.add_n("pe", mk_inter(), rdeps)
                    def mk_oadd(h=h, c2=c2, pao=pao):
                        def f(v):
                            return v.tensor_tensor(o_f32[:, c2 * 128:(c2 + 1) * 128],
                                                   o_f32[:, c2 * 128:(c2 + 1) * 128],
                                                   po[:, pao:pao + 128], ALU.add)
                        return f
                    oadd[c2] = a3p.add_n("dve", mk_oadd(), [inter[c2], ocopy[c2]])
                prev_mask15 = mask[NCH - 1]
                prev_omm15 = inter[NCH - 1]
                prev_sacc15 = sacc[NCH - 1]
                prev_oadd15 = oadd[NCH - 1]
                prev_inter15 = inter[NCH - 1]
                prev_ocopy15 = oadd[NCH - 1]
                sqs_ = [None] * NG
                summ_ = [None] * NG
                rowm_ = [None] * NG
                for g in range(NG):
                    def mk_sq(h=h, g=g):
                        def f(s):
                            return s.activation(sq_scr[:, (g % 2) * G:(g % 2) * G + G],
                                                o_f32[:, g * G:(g + 1) * G], AF.Square)
                        return f
                    sdeps = [oadd[NCH - 1]]
                    if g >= 2:
                        sdeps.append(summ_[g - 2])
                    sqs_[g] = a3p.add_n("act", mk_sq(), sdeps)
                    def mk_summ(h=h, g=g):
                        def f(t):
                            return t.matmul(pnq[0:1, 0:G], ones_col[:, :],
                                            sq_scr[:, (g % 2) * G:(g % 2) * G + G],
                                            start=True, stop=True)
                        return f
                    smdeps = [sqs_[g]]
                    if g >= 1:
                        smdeps.append(rowm_[g - 1])
                    summ_[g] = a3p.add_n("pe", mk_summ(), smdeps)
                    def mk_rowm(h=h, g=g):
                        def f(v):
                            return v.tensor_scalar(nr_m[0:1, g * G:(g + 1) * G], pnq[0:1, 0:G],
                                                   1.0 / 128.0, None, ALU.mult)
                        return f
                    rowm_[g] = a3p.add_n("dve", mk_rowm(), [summ_[g]])
                def mk_sqr(h=h):
                    def f(s):
                        return s.activation(nr_s[0:1, :], nr_m[0:1, :], AF.Sqrt,
                                            bias=eps_cols[0:1, 0:1])
                    return f
                sqr = a3p.add_n("act", mk_sqr(), [rowm_[NG - 1]])
                def mk_rcp(h=h):
                    def f(v):
                        return v.reciprocal(nr_s[0:1, :], nr_s[0:1, :])
                    return f
                rcp = a3p.add_n("dve", mk_rcp(), [sqr])
                rbc = None
                onl = None
                for g in range(NG):
                    def mk_bcast(h=h, g=g):
                        def f(t):
                            return t.matmul(pa[:, 0:G], ones_row[:, :],
                                            nr_s[0:1, g * G:(g + 1) * G], start=True, stop=True)
                        return f
                    bdeps = [rcp, mask[NCH - 1]]
                    if rbc is not None:
                        bdeps.append(rbc)
                    bc = a3p.add_n("pe", mk_bcast(), bdeps)
                    def mk_rbc(h=h, g=g):
                        def f(s):
                            return s.activation(rb_scr[:, :], pa[:, 0:G], AF.Copy)
                        return f
                    rbc = a3p.add_n("act", mk_rbc(), [bc])
                    def mk_on(h=h, g=g):
                        def f(v):
                            v.tensor_tensor(sq_scr[:, 0:G], o_f32[:, g * G:(g + 1) * G],
                                            rb_scr[:, :], ALU.mult)
                            return v.tensor_scalar(o_nb[:, g * G:(g + 1) * G], sq_scr[:, 0:G],
                                                   w128[:, 0:1], None, ALU.mult)
                        return f
                    onl = a3p.add_n("dve", mk_on(), [rbc, prev_spill])
                def mk_spill(h=h):
                    def f(sy, sem):
                        sy.dma_start(out=io["o_dram"][h * 128:(h + 1) * 128, :],
                                     in_=o_nb[:, :]).then_inc(sem, 16)
                    return f
                prev_spill = a3p.add_n("sy", mk_spill(), [onl], inc_total=16)
                prev_rbc3 = rbc
            a3p.emit(nc, c, "a3p")

    u_sb = sb.enter_context(nc.sbuf_tensor("a_u", [128, NIC * R], F32))
    a4f = c.enter_context(nc.semaphore("a_a4f"))
    a4e = c.enter_context(nc.semaphore("a_a4e"))
    with ExitStack() as a4c:
        wo_sb = a4c.enter_context(nc.sbuf_tensor("a_wo", [128, NIC * 1024], BF16))
        og = a4c.enter_context(nc.sbuf_tensor("a_og", [128, 8 * G], BF16))
        xg = a4c.enter_context(nc.sbuf_tensor("a_xg", [128, 8 * G], F32))
        scr4 = a4c.enter_context(nc.sbuf_tensor("a_scr4", [128, G], F32))
        pR = a4c.enter_context(nc.psum_tensor("a_pR", [128, G], F32))
        pC = a4c.enter_context(nc.psum_tensor("a_pC", [128, G], F32))
        pwo = a4c.enter_context(nc.psum_tensor("a_pwo", [128, 2 * G], F32))
        with nc.Block() as blk:
            @blk.gpsimd
            def _(gg):
                GLD(gg, gg.dma_start(out=wo_sb[:, :].rearrange("p (ic f) -> p ic f", ic=NIC),
                                     in_=io["wo"].rearrange("(ic p) f -> p ic f", p=128)))
        a4p = Plan()
        dve_last = None
        bank_user = {0: None, 1: None}
        for g in range(NG):
            def mk_ldog(g=g):
                def f(sy, sem):
                    sy.dma_start(out=og[:, :].rearrange("p (kc t) -> p kc t", kc=8),
                                 in_=io["o_dram"][:, g * G:(g + 1) * G].rearrange(
                                     "(kc p) t -> p kc t", p=128)).then_inc(sem, 16)
                    sy.dma_start(out=xg[:, :].rearrange("p (ic t) -> p ic t", ic=8),
                                 in_=io["xin"][:, g * G:(g + 1) * G].rearrange(
                                     "(ic p) t -> p ic t", p=128)).then_inc(sem, 16)
                return f
            ldog = a4p.add_n("sy", mk_ldog(), [dve_last], inc_total=32)
            def mk_bc(g=g):
                def f(t):
                    t.matmul(pR[:, :], ones_row[:, :], rows_s1[0:1, g * G:(g + 1) * G],
                             start=True, stop=True)
                    return t.matmul(pC[:, :], ones_row[:, :], rows_c1[0:1, g * G:(g + 1) * G],
                                    start=True, stop=True)
                return f
            bc = a4p.add_n("pe", mk_bc(), [dve_last])
            for oc in range(NIC):
                def mk_wmm(g=g, oc=oc):
                    def f(t):
                        for kc in range(8):
                            ins = t.matmul(pwo[:, (oc % 2) * G:(oc % 2 + 1) * G],
                                           wo_sb[:, kc * 1024 + oc * 128: kc * 1024 + (oc + 1) * 128],
                                           og[:, kc * G:(kc + 1) * G],
                                           start=(kc == 0), stop=(kc == 7))
                        return ins
                    return f
                wdeps = [ldog, bank_user[oc % 2]]
                wmm = a4p.add_n("pe", mk_wmm(), wdeps)
                def mk_udve(g=g, oc=oc):
                    def f(v):
                        v.tensor_tensor(scr4[:, :], xg[:, oc * G:(oc + 1) * G], pR[:, :], ALU.mult)
                        v.tensor_tensor(scr4[:, :], scr4[:, :], pC[:, :], ALU.add)
                        return v.tensor_tensor(u_sb[:, oc * R + g * G: oc * R + (g + 1) * G],
                                               scr4[:, :], pwo[:, (oc % 2) * G:(oc % 2 + 1) * G],
                                               ALU.add)
                    return f
                bank_user[oc % 2] = a4p.add_n("dve", mk_udve(), [wmm, bc])
            dve_last = bank_user[(NIC - 1) % 2]
        a4p.emit(nc, c, "a4p")
    with ExitStack() as a5c:
        h2 = sb.enter_context(nc.sbuf_tensor("a_h2", [128, NIC * R], BF16))
        emit_ln_zn(nc, c, vsn, u_sb, h2, ones_col, ones_row, 'rms',
                   eps1_col=eps_cols[:, 0:1], nic=NIC, xcols=R, width=1024, name="AU",
                   rms_gain=True, gain_coltile=wcolM, ones_col_bf=consts["ones_col_bf"])
        xg2 = a5c.enter_context(nc.sbuf_tensor("a_xg2", [128, 2 * R], F32))
        a5p = Plan()
        addr = {0: None, 1: None}
        for oc in range(NIC):
            def mk_ldx(oc=oc):
                def f(sy, sem):
                    sy.dma_start(out=xg2[:, (oc % 2) * R:(oc % 2 + 1) * R],
                                 in_=io["xin"][oc * 128:(oc + 1) * 128, :]).then_inc(sem, 16)
                return f
            ldx = a5p.add_n("sy", mk_ldx(), [addr[oc % 2]], inc_total=16)
            def mk_addx(oc=oc):
                def f(v):
                    for g in range(NG):
                        sl = slice(oc * R + g * G, oc * R + (g + 1) * G)
                        ins = v.tensor_tensor(u_sb[:, sl], u_sb[:, sl],
                                              xg2[:, (oc % 2) * R + g * G:(oc % 2) * R + (g + 1) * G],
                                              ALU.add)
                    return ins
                return f
            addr[oc % 2] = a5p.add_n("dve", mk_addx(), [ldx])
        a5p.emit(nc, c, "a5p")

    sems = None
    for half in range(2):
        toff = half * (R // 2)
        goffs = [toff, toff + G]
        hgoffs = [0, G]
        with ExitStack() as hc:
            m_sb = hc.enter_context(nc.sbuf_tensor(f"a_m{half}", [128, 32 * (R // 2)], BF16))
            aplan = Plan()
            pgS = hc.enter_context(nc.psum_tensor(f"a_pgS{half}", [128, 4 * G], F32))
            puS = hc.enter_context(nc.psum_tensor(f"a_puS{half}", [128, 4 * G], F32))
            spsum = (pgS, puS, {0: None, 1: None})
            r1 = emit_ffn(nc, c, sems, f"am12h{half}", 'silu2', (io["wgate"], io["wup"]),
                          h2, R, goffs, 32, NIC, dst_sb=m_sb, dst_cols=R // 2,
                          dst_goffs=hgoffs, plan=aplan, bufc=hc, shared_psum=spsum)
            emit_ffn(nc, c, sems, f"amdnh{half}", 'addx', (io["wdown"],),
                     m_sb, R // 2, hgoffs, NIC, 32, x_sb=u_sb, dst_goffs=goffs,
                     plan=aplan, bufc=hc, shared_psum=spsum, src_ready=r1)
            aplan.emit(nc, c, f"aph{half}")

    with nc.Block() as blk:
        @blk.sync
        def _(sy):
            LD(sy, sy.dma_start(out=io["xout"].rearrange("(ic p) t -> p ic t", p=128),
                                in_=u_sb[:, :].rearrange("p (ic t) -> p ic t", ic=NIC)))
    sb.close()


def build_A(maskin):
    nc = bass.Bass("TRN2", target_bir_lowering=False, detect_race_conditions=False)
    io = declare_consts(nc)
    io["xin"] = nc.dram_tensor("xin", [D, R], F32, kind="ExternalInput").ap()
    io["wq"] = nc.dram_tensor("wq", [D, D], F32, kind="ExternalInput").ap()
    io["wk"] = nc.dram_tensor("wk", [D, D], F32, kind="ExternalInput").ap()
    io["wv"] = nc.dram_tensor("wv", [D, D], F32, kind="ExternalInput").ap()
    io["wo"] = nc.dram_tensor("wo", [D, D], F32, kind="ExternalInput").ap()
    io["wgate"] = nc.dram_tensor("wgate", [D, 4096], F32, kind="ExternalInput").ap()
    io["wup"] = nc.dram_tensor("wup", [D, 4096], F32, kind="ExternalInput").ap()
    io["wdown"] = nc.dram_tensor("wdown", [4096, D], F32, kind="ExternalInput").ap()
    io["gwA"] = nc.dram_tensor("gwA", [128, 17], F32, kind="ExternalInput").ap()
    io["wrowA"] = nc.dram_tensor("wrowA", [1, D], F32, kind="ExternalInput").ap()
    io["maskin"] = nc.dram_tensor("maskin", [128, 8], F32, kind="ExternalInput").ap()
    io["xout"] = nc.dram_tensor("xout", [D, R], F32, kind="ExternalOutput").ap()
    io["cc_in"] = nc.dram_tensor("cc_in", [1024, 128], F32).ap()
    io["cc_out"] = nc.dram_tensor("cc_out", [8192, 128], F32).ap()
    io["o_dram"] = nc.dram_tensor("o_dram", [1024, R], BF16).ap()
    with ExitStack() as c:
        consts = load_consts(nc, c, io)
        consts["vsn"] = VSync(c, nc, "a_vs")
        emit_blockA(nc, c, io, consts)
    return nc


def dA_host_inputs(inputs):
    m = {}
    m["wq"] = f32c(inputs["L0_attn_Wq"])
    m["wk"] = f32c(inputs["L0_attn_Wk"])
    m["wv"] = f32c(inputs["L0_attn_Wv"])
    m["wo"] = f32c(inputs["L0_attn_Wo"])
    m["wgate"] = f32c(inputs["L0_attn_Wgate"])
    m["wup"] = f32c(inputs["L0_attn_Wup"])
    m["wdown"] = f32c(inputs["L0_attn_Wdown"])
    gw = np.zeros((128, 17), np.float32)
    gw[:, 0:8] = f32c(inputs["L0_attn_attn_norm_w"]).reshape(8, 128).T
    gw[:, 8:16] = f32c(inputs["L0_attn_mlp_norm_w"]).reshape(8, 128).T
    gw[:, 16] = f32c(inputs["L0_attn_attn_out_norm_w"])
    m["gwA"] = gw
    m["wrowA"] = f32c(inputs["L0_attn_attn_norm_w"]).reshape(1, D)
    return m


def core_masks():
    outs = []
    for i in range(NCORES):
        mk = np.zeros((128, 8), np.float32)
        mk[:, :i] = 1.0
        outs.append(mk)
    return outs


def run_A(x0_fm_shards, inputs):
    nc = build_A(None)
    ca = const_arrays()
    da = dA_host_inputs(inputs)
    mks = core_masks()
    in_maps = []
    for i in range(NCORES):
        m = dict(ca)
        m.update(da)
        m["xin"] = f32c(x0_fm_shards[i])
        m["maskin"] = mks[i]
        in_maps.append(m)
    res = run_bass_kernel_spmd(nc, in_maps, core_ids=list(range(NCORES)))
    return [np.asarray(r["xout"], dtype=np.float32) for r in res.results]



def make_ffn_sems(nc, c, tag):
    sems = {}
    for nm in ("w", "f", "a", "e", "x", "s"):
        sems[nm] = c.enter_context(nc.semaphore(f"{tag}_{nm}"))
    return sems


def emit_blockD(nc, c, io, x_sb_ctx, consts):
    sb = ExitStack()
    ones_col, ones_row, eps_cols = consts["ones_col"], consts["ones_row"], consts["eps_cols"]
    vsn = consts["vsn"]
    NHID = 4096
    NOC = NHID // 128
    sems = None
    scol = sb.enter_context(nc.sbuf_tensor("d_scol", [128, 4 * NOC + 8], F32))
    zn = sb.enter_context(nc.sbuf_tensor("d_zn", [128, NIC * R], BF16))
    ld = c.enter_context(nc.semaphore("d_ld"))
    with nc.Block() as blk:
        @blk.sync
        def _(sy):
            sy.dma_start(out=scol[:, :], in_=io["scolD"]).then_inc(ld, 16)
            sy.wait_ge(ld, 16)
    s1c, s2c = scol[:, 0:NOC], scol[:, NOC:2 * NOC]
    hs1c, hs2c = scol[:, 2 * NOC:3 * NOC], scol[:, 3 * NOC:4 * NOC]
    s3c = scol[:, 4 * NOC:4 * NOC + 8]

    with ExitStack() as lnc:
        if x_sb_ctx is None:
            x_sb = lnc.enter_context(nc.sbuf_tensor("d_x", [128, NIC * R], F32))
            with nc.Block() as blk:
                @blk.sync
                def _(sy):
                    sy.dma_start(out=x_sb[:, :].rearrange("p (ic t) -> p ic t", ic=NIC),
                                 in_=io["xin"].rearrange("(ic p) t -> p ic t", p=128)).then_inc(ld, 16)
                    sy.wait_ge(ld, 32)
        else:
            x_sb = x_sb_ctx
        emit_ln_zn(nc, c, vsn, x_sb, zn, ones_col, ones_row, 'ln',
                   eps1_col=eps_cols[:, 0:1], name="D", ones_col_bf=consts["ones_col_bf"])

    for half in range(2):
        toff = half * (R // 2)
        goffs = [toff, toff + G]
        hgoffs = [0, G]
        with ExitStack() as hc:
            g1_sb = hc.enter_context(nc.sbuf_tensor(f"d_g1{half}", [128, NOC * (R // 2)], BF16))
            hplan = Plan()
            pgS = hc.enter_context(nc.psum_tensor(f"d_pgS{half}", [128, 4 * G], F32))
            puS = hc.enter_context(nc.psum_tensor(f"d_puS{half}", [128, 4 * G], F32))
            spsum = (pgS, puS, {0: None, 1: None})
            with ExitStack() as hi:
                h_sb = hi.enter_context(nc.sbuf_tensor(f"d_h{half}", [128, NOC * (R // 2)], BF16))
                r1 = emit_ffn(nc, c, sems, f"d12h{half}", 'silu2', (io["w1T"], io["w2T"]),
                              zn, R, goffs, NOC, NIC, dst_sb=h_sb, dst_cols=R // 2,
                              dst_goffs=hgoffs, scol1=s1c, scol2=s2c,
                              plan=hplan, bufc=hi, shared_psum=spsum)
                r2 = emit_ffn(nc, c, sems, f"d3h{half}", 'gelu', (io["h1T"],),
                              h_sb, R // 2, hgoffs, NOC, NOC, dst_sb=g1_sb, dst_cols=R // 2,
                              dst_goffs=hgoffs, scol1=hs1c,
                              plan=hplan, bufc=hi, shared_psum=spsum, src_ready=r1)
                hplan.emit(nc, c, f"dph{half}")
            g2_sb = hc.enter_context(nc.sbuf_tensor(f"d_g2{half}", [128, NOC * (R // 2)], BF16))
            hplan2 = Plan()
            spsum2 = (pgS, puS, {0: None, 1: None})
            r3 = emit_ffn(nc, c, sems, f"d4h{half}", 'gelu', (io["h2T"],),
                          g1_sb, R // 2, hgoffs, NOC, NOC, dst_sb=g2_sb, dst_cols=R // 2,
                          dst_goffs=hgoffs, scol1=hs2c,
                          plan=hplan2, bufc=hc, shared_psum=spsum2)
            r4 = emit_ffn(nc, c, sems, f"d5h{half}", 'w3add', (io["w3T"],),
                          g2_sb, R // 2, hgoffs, NIC, NOC, scol1=s3c,
                          xin_ap=io["xin"], out_ap=io["xout"], tok_offs=toff,
                          plan=hplan2, bufc=hc, shared_psum=spsum2, src_ready=r3)
            hplan2.emit(nc, c, f"dqh{half}")
    sb.close()


def build_D():
    nc = bass.Bass("TRN2", target_bir_lowering=False, detect_race_conditions=False)
    io = declare_consts(nc)
    io["xin"] = nc.dram_tensor("xin", [D, R], F32, kind="ExternalInput").ap()
    io["w1T"] = nc.dram_tensor("w1T", [D, 4096], F32, kind="ExternalInput").ap()
    io["w2T"] = nc.dram_tensor("w2T", [D, 4096], F32, kind="ExternalInput").ap()
    io["h1T"] = nc.dram_tensor("h1T", [4096, 4096], F32, kind="ExternalInput").ap()
    io["h2T"] = nc.dram_tensor("h2T", [4096, 4096], F32, kind="ExternalInput").ap()
    io["w3T"] = nc.dram_tensor("w3T", [4096, D], F32, kind="ExternalInput").ap()
    io["scolD"] = nc.dram_tensor("scolD", [128, 4 * 32 + 8], F32, kind="ExternalInput").ap()
    io["xout"] = nc.dram_tensor("xout", [D, R], F32, kind="ExternalOutput").ap()
    with ExitStack() as c:
        consts = load_consts(nc, c, io)
        consts["vsn"] = VSync(c, nc, "d_vs")
        emit_blockD(nc, c, io, None, consts)
    return nc


def dD_host_inputs(inputs):
    m = {}
    m["w1T"] = f32c(np.asarray(inputs["L1_ffn_w1_w"]).T)
    m["w2T"] = f32c(np.asarray(inputs["L1_ffn_w2_w"]).T)
    m["h1T"] = f32c(np.asarray(inputs["L1_ffn_h1_w"]).T)
    m["h2T"] = f32c(np.asarray(inputs["L1_ffn_h2_w"]).T)
    m["w3T"] = f32c(np.asarray(inputs["L1_ffn_w3_w"]).T)
    sc = np.zeros((128, 4 * 32 + 8), np.float32)
    sc[:, 0:32] = np.asarray(inputs["L1_ffn_w1_s"]).reshape(32, 128).T
    sc[:, 32:64] = np.asarray(inputs["L1_ffn_w2_s"]).reshape(32, 128).T
    sc[:, 64:96] = np.asarray(inputs["L1_ffn_h1_s"]).reshape(32, 128).T
    sc[:, 96:128] = np.asarray(inputs["L1_ffn_h2_s"]).reshape(32, 128).T
    sc[:, 128:136] = np.asarray(inputs["L1_ffn_w3_s"]).reshape(8, 128).T
    m["scolD"] = sc
    return m


def run_D(x3_fm_shards, inputs):
    nc = build_D()
    ca = const_arrays()
    dm = dD_host_inputs(inputs)
    in_maps = []
    for i in range(NCORES):
        m = dict(ca)
        m.update(dm)
        m["xin"] = f32c(x3_fm_shards[i])
        in_maps.append(m)
    res = run_bass_kernel_spmd(nc, in_maps, core_ids=list(range(NCORES)))
    return [np.asarray(r["xout"], dtype=np.float32) for r in res.results]



def build_B(stage="full"):
    nc = bass.Bass("TRN2", target_bir_lowering=False, detect_race_conditions=False)
    io = declare_consts(nc)
    io["xin"] = nc.dram_tensor("xin", [D, R], F32, kind="ExternalInput").ap()
    io["wr"] = nc.dram_tensor("wr", [D, 8], F32, kind="ExternalInput").ap()
    io["wg"] = nc.dram_tensor("wg", [8, D, 2048], F32, kind="ExternalInput").ap()
    io["wu"] = nc.dram_tensor("wu", [8, D, 2048], F32, kind="ExternalInput").ap()
    io["wd"] = nc.dram_tensor("wd", [8, 2048, D], F32, kind="ExternalInput").ap()
    io["xout"] = nc.dram_tensor("xout", [D, R], F32, kind="ExternalOutput").ap()
    with ExitStack() as c:
        x_sb = c.enter_context(nc.sbuf_tensor("x_sb", [128, NIC * R], F32))
        zn_sb = c.enter_context(nc.sbuf_tensor("zn_sb", [128, NIC * R], BF16))
        xld = c.enter_context(nc.semaphore("xld"))
        with nc.Block() as blk:
            @blk.sync
            def _(sy):
                sy.dma_start(out=x_sb[:, :].rearrange("p (ic t) -> p ic t", ic=NIC),
                             in_=io["xin"].rearrange("(ic p) t -> p ic t", p=128)).then_inc(xld, 16)
                sy.wait_ge(xld, 16)
        cst = load_consts(nc, c, io)
        cst["vsn"] = VSync(c, nc, "b_vs0")
        emit_blockB(nc, c, io, x_sb, zn_sb, cst["ones_col"], cst["ones_row"],
                    cst["ident"], cst["eps_cols"], stage=stage,
                    ones_col_bf=cst["ones_col_bf"])
        xst = c.enter_context(nc.semaphore("xst"))
        with nc.Block() as blk:
            @blk.sync
            def _(sy):
                sy.dma_start(out=io["xout"].rearrange("(ic p) t -> p ic t", p=128),
                             in_=x_sb[:, :].rearrange("p (ic t) -> p ic t", ic=NIC)).then_inc(xst, 16)
                sy.wait_ge(xst, 16)
    return nc


def run_B(x1_fm_shards, inputs, stage="full"):
    nc = build_B(stage)
    ca = const_arrays()
    in_maps = []
    for i in range(NCORES):
        m = dict(ca)
        m["xin"] = f32c(x1_fm_shards[i])
        m["wr"] = f32c(inputs["L0_ffn_router_kernel_DE"])
        m["wg"] = f32c(inputs["L0_ffn_kernel_gating_EDF"])
        m["wu"] = f32c(inputs["L0_ffn_kernel_up_EDF"])
        m["wd"] = f32c(inputs["L0_ffn_kernel_down_EFD"])
        in_maps.append(m)
    res = run_bass_kernel_spmd(nc, in_maps, core_ids=list(range(NCORES)))
    return [np.asarray(r["xout"], dtype=np.float32) for r in res.results]


def build_fused():
    nc = bass.Bass("TRN2", target_bir_lowering=False, detect_race_conditions=False)
    io = declare_consts(nc)
    io["xin"] = nc.dram_tensor("xin", [D, R], F32, kind="ExternalInput").ap()
    io["wq"] = nc.dram_tensor("wq", [D, D], F32, kind="ExternalInput").ap()
    io["wk"] = nc.dram_tensor("wk", [D, D], F32, kind="ExternalInput").ap()
    io["wv"] = nc.dram_tensor("wv", [D, D], F32, kind="ExternalInput").ap()
    io["wo"] = nc.dram_tensor("wo", [D, D], F32, kind="ExternalInput").ap()
    io["wgate"] = nc.dram_tensor("wgate", [D, 4096], F32, kind="ExternalInput").ap()
    io["wup"] = nc.dram_tensor("wup", [D, 4096], F32, kind="ExternalInput").ap()
    io["wdown"] = nc.dram_tensor("wdown", [4096, D], F32, kind="ExternalInput").ap()
    io["gwA"] = nc.dram_tensor("gwA", [128, 17], F32, kind="ExternalInput").ap()
    io["wrowA"] = nc.dram_tensor("wrowA", [1, D], F32, kind="ExternalInput").ap()
    io["maskin"] = nc.dram_tensor("maskin", [128, 8], F32, kind="ExternalInput").ap()
    io["wr"] = nc.dram_tensor("wr", [D, 8], F32, kind="ExternalInput").ap()
    io["wg"] = nc.dram_tensor("wg", [8, D, 2048], F32, kind="ExternalInput").ap()
    io["wu"] = nc.dram_tensor("wu", [8, D, 2048], F32, kind="ExternalInput").ap()
    io["wd"] = nc.dram_tensor("wd", [8, 2048, D], F32, kind="ExternalInput").ap()
    io["wqkv"] = nc.dram_tensor("wqkv", [D, 3072], F32, kind="ExternalInput").ap()
    io["wqkvT"] = nc.dram_tensor("wqkvT", [3072, D], F32, kind="ExternalInput").ap()
    io["proj"] = nc.dram_tensor("proj", [16, 64, 64], F32, kind="ExternalInput").ap()
    io["wout"] = nc.dram_tensor("wout", [D, D], F32, kind="ExternalInput").ap()
    io["bqkvC"] = nc.dram_tensor("bqkvC", [64, 48], F32, kind="ExternalInput").ap()
    io["boutC"] = nc.dram_tensor("boutC", [128, 8], F32, kind="ExternalInput").ap()
    io["w1T"] = nc.dram_tensor("w1T", [D, 4096], F32, kind="ExternalInput").ap()
    io["w2T"] = nc.dram_tensor("w2T", [D, 4096], F32, kind="ExternalInput").ap()
    io["h1T"] = nc.dram_tensor("h1T", [4096, 4096], F32, kind="ExternalInput").ap()
    io["h2T"] = nc.dram_tensor("h2T", [4096, 4096], F32, kind="ExternalInput").ap()
    io["w3T"] = nc.dram_tensor("w3T", [4096, D], F32, kind="ExternalInput").ap()
    io["scolD"] = nc.dram_tensor("scolD", [128, 4 * 32 + 8], F32, kind="ExternalInput").ap()
    io["xout"] = nc.dram_tensor("xout", [D, R], F32, kind="ExternalOutput").ap()
    io["cc_in"] = nc.dram_tensor("cc_in", [1024, 128], F32).ap()
    io["cc_out"] = nc.dram_tensor("cc_out", [8192, 128], F32).ap()
    io["o_dram"] = nc.dram_tensor("o_dram", [1024, R], BF16).ap()
    io["cc_inC"] = nc.dram_tensor("cc_inC", [1024, 65], F32).ap()
    io["cc_outC"] = nc.dram_tensor("cc_outC", [8192, 65], F32).ap()
    x1s = nc.dram_tensor("x1s", [D, R], F32).ap()
    x2s = nc.dram_tensor("x2s", [D, R], F32).ap()
    x3s = nc.dram_tensor("x3s", [D, R], F32).ap()

    with ExitStack() as c:
        consts = load_consts(nc, c, io)
        consts["vsn"] = VSync(c, nc, "f_vs")
        ioA = dict(io); ioA["xout"] = x1s
        emit_blockA(nc, c, ioA, consts)
        ioB = dict(io); ioB["xin"] = x1s; ioB["xout"] = x2s
        with ExitStack() as bscope:
            x_sb = bscope.enter_context(nc.sbuf_tensor("f_bx", [128, NIC * R], F32))
            zn_sb = bscope.enter_context(nc.sbuf_tensor("f_bzn", [128, NIC * R], BF16))
            bld = c.enter_context(nc.semaphore("f_bld"))
            with nc.Block() as blk:
                @blk.sync
                def _(sy):
                    sy.dma_start(out=x_sb[:, :].rearrange("p (ic t) -> p ic t", ic=NIC),
                                 in_=x1s.rearrange("(ic p) t -> p ic t", p=128)).then_inc(bld, 16)
                    sy.wait_ge(bld, 16)
            emit_blockB(nc, c, ioB, x_sb, zn_sb, consts["ones_col"], consts["ones_row"],
                        consts["ident"], consts["eps_cols"],
                        ones_col_bf=consts["ones_col_bf"])
            with nc.Block() as blk:
                @blk.sync
                def _(sy):
                    sy.dma_start(out=x2s.rearrange("(ic p) t -> p ic t", p=128),
                                 in_=x_sb[:, :].rearrange("p (ic t) -> p ic t", ic=NIC)).then_inc(bld, 16)
                    sy.wait_ge(bld, 32)
        ioC = dict(io); ioC["xin"] = x2s; ioC["xout"] = x3s
        emit_blockC(nc, c, ioC, consts)
        ioD = dict(io); ioD["xin"] = x3s
        emit_blockD(nc, c, ioD, None, consts)
    return nc


_FUSED_NC = None


def kernel_fused(**inputs):
    global _FUSED_NC
    if _FUSED_NC is None:
        _FUSED_NC = build_fused()
    nc = _FUSED_NC
    x0 = np.asarray(inputs["L0_attn_hidden_states"], dtype=np.float32)[0]
    ca = const_arrays()
    da = dA_host_inputs(inputs)
    dc = dC_host_inputs(inputs)
    dd = dD_host_inputs(inputs)
    mks = core_masks()
    base = dict(ca)
    base.update(da); base.update(dc); base.update(dd)
    base["wr"] = f32c(inputs["L0_ffn_router_kernel_DE"])
    base["wg"] = f32c(inputs["L0_ffn_kernel_gating_EDF"])
    base["wu"] = f32c(inputs["L0_ffn_kernel_up_EDF"])
    base["wd"] = f32c(inputs["L0_ffn_kernel_down_EFD"])
    in_maps = []
    for i in range(NCORES):
        m = dict(base)
        m["xin"] = np.ascontiguousarray(x0[i * R:(i + 1) * R, :].T)
        m["maskin"] = mks[i]
        in_maps.append(m)
    res = run_bass_kernel_spmd(nc, in_maps, core_ids=list(range(NCORES)))
    out = np.concatenate([np.asarray(r["xout"], np.float32).T for r in res.results], axis=0)
    return np.ascontiguousarray(out[None, :, :], dtype=np.float32)


def kernel_unfused(**inputs):
    x0 = np.asarray(inputs["L0_attn_hidden_states"], dtype=np.float32)[0]
    shards = [np.ascontiguousarray(x0[i * R:(i + 1) * R, :].T) for i in range(NCORES)]
    shards = run_A(shards, inputs)
    shards = run_B(shards, inputs)
    shards = run_C(shards, inputs)
    shards = run_D(shards, inputs)
    out = np.concatenate([s.T for s in shards], axis=0)[None, :, :]
    return np.ascontiguousarray(out, dtype=np.float32)


def kernel(**inputs):
    return kernel_fused(**inputs)


# version 14 (final)
# speedup vs baseline: 1.0839x; 1.0207x over previous
import numpy as np
import ml_dtypes
import concourse.bass as bass
import concourse.mybir as mybir
from concourse.bass_utils import run_bass_kernel_spmd
from contextlib import ExitStack

F32 = mybir.dt.float32
BF16 = mybir.dt.bfloat16
AF = mybir.ActivationFunctionType
ALU = mybir.AluOpType
AX = mybir.AxisListType

NCORES = 8
R = 2048
D = 1024
NIC = D // 128
G = 512
NG = R // G
NT = R // 128
DBG = {"gu": True, "down": True, "evac": True}


def f32c(a):
    return np.ascontiguousarray(np.asarray(a, dtype=np.float32))


class Plan:
    ENGS = ("pe", "act", "dve", "gp", "sy")

    def __init__(self):
        self.stages = {e: [] for e in self.ENGS}

    def __init2__(self):
        pass

    def add(self, eng, fn, deps=(), inc=1):
        cnt = getattr(self, "_cnt", None)
        if cnt is None:
            self._cnt = {e: 0 for e in self.ENGS}
            cnt = self._cnt
        self.stages[eng].append((fn, [d for d in deps if d is not None], inc))
        self._pending = getattr(self, "_pending", {})
        raise RuntimeError("use add_n")

    def add_n(self, eng, fn, deps=(), inc_total=1, chan=None):
        if chan is None:
            chan = eng
        if not hasattr(self, "_cnt"):
            self._cnt = {}
        self._cnt.setdefault(chan, 0)
        self.stages[eng].append((fn, [d for d in deps if d is not None], inc_total, chan))
        self._cnt[chan] += inc_total
        return (chan, self._cnt[chan])

    def emit(self, nc, c, tag):
        chans = {ch for e in self.ENGS for (_, _, _, ch) in self.stages[e]}
        for e in self.ENGS:
            for (_, deps, _, _) in self.stages[e]:
                for (ch, _) in deps:
                    chans.add(ch)
        sems = {ch: c.enter_context(nc.semaphore(f"{tag}_{ch}")) for ch in chans}
        attr = {"pe": "tensor", "act": "scalar", "dve": "vector",
                "gp": "gpsimd", "sy": "sync"}
        with nc.Block() as blk:
            for e in self.ENGS:
                if not self.stages[e]:
                    continue
                def mk(e):
                    def body(eng):
                        for fn, deps, inc_total, ch in self.stages[e]:
                            for (pc_, pn) in deps:
                                eng.wait_ge(sems[pc_], pn)
                            if inc_total == 1:
                                ins = fn(eng)
                                ins.then_inc(sems[ch], 1)
                            else:
                                fn(eng, sems[ch])
                    return body
                getattr(blk, attr[e])(mk(e))


class VSync:
    def __init__(self, c, nc, name="vsync"):
        self.h = c.enter_context(nc.semaphore(name))
        self.n = 0

    def __call__(self, eng, ins):
        self.n += 1
        ins.then_inc(self.h, 1)
        eng.wait_ge(self.h, self.n)
        return ins



def emit_ln_rows(nc, semc, vsn, x_sb, ones_col, rows_m, rows_s, rows_c, kind,
                 eps1_col=None, eps2_col=None, nic=NIC, xcols=R, name="",
                 rows_s1=None, rows_c1=None, width=None, xoff=0, shared=None):
    with ExitStack() as c:
        p_sx = None if kind == 'rms' else c.enter_context(nc.psum_tensor("p_sx" + name, [128, G], F32))
        p_sq = c.enter_context(nc.psum_tensor("p_sq" + name, [128, G], F32))
        sqs = [c.enter_context(nc.sbuf_tensor(f"lnsq{i}" + name, [128, G], F32)) for i in range(2)]
        rt1 = c.enter_context(nc.sbuf_tensor("lnrt1" + name, [1, G], F32))
        rt2 = c.enter_context(nc.sbuf_tensor("lnrt2" + name, [1, G], F32))
        width = float(nic * 128) if width is None else float(width)
        if shared is None:
            sqd = semc.enter_context(nc.semaphore("lnsqd" + name))
            mmd = semc.enter_context(nc.semaphore("lnmmd" + name))
            b_sq = b_mm = 0
        else:
            sqd, mmd = shared["sqd"], shared["mmd"]
            b_sq, b_mm = shared["n_sq"], shared["n_mm"]
        for g in range(NG):
            gs = slice(g * G, (g + 1) * G)
            if True:
                b0 = b_sq + g * nic
                b1 = b_mm + g * nic
                with nc.Block() as blk:
                    @blk.scalar
                    def _(s):
                        for ic in range(nic):
                            s.activation(sqs[ic % 2][:, :],
                                         x_sb[:, xoff + ic * xcols + g * G: xoff + ic * xcols + (g + 1) * G],
                                         AF.Square).then_inc(sqd, 1)
                            if ic >= 1:
                                s.wait_ge(mmd, b1 + ic)
                    @blk.tensor
                    def _(t):
                        if kind != 'rms':
                            for ic in range(nic):
                                t.matmul(p_sx[0:1, :], ones_col[:, :],
                                         x_sb[:, xoff + ic * xcols + g * G: xoff + ic * xcols + (g + 1) * G],
                                         start=(ic == 0), stop=(ic == nic - 1))
                        for ic in range(nic):
                            t.wait_ge(sqd, b0 + ic + 1)
                            t.matmul(p_sq[0:1, :], ones_col[:, :], sqs[ic % 2][:, :],
                                     start=(ic == 0), stop=(ic == nic - 1)).then_inc(mmd, 1)
            with nc.Block() as blk:
                @blk.vector
                def _(v):
                    if kind == 'rms':
                        v.tensor_scalar(rt1[:, :], p_sq[0:1, :], 1.0 / width, None, ALU.mult)
                    else:
                        v.tensor_scalar(rows_m[:, gs], p_sx[0:1, :], 1.0 / width, None, ALU.mult)
                        v.tensor_tensor(rt2[:, :], rows_m[:, gs], rows_m[:, gs], ALU.mult)
                        v.scalar_tensor_tensor(rt1[:, :], p_sq[0:1, :], 1.0 / width, rt2[:, :],
                                               ALU.mult, ALU.subtract)
            with nc.Block() as blk:
                @blk.scalar
                def _(s):
                    s.activation(rt2[:, :], rt1[:, :], AF.Sqrt, bias=eps1_col[0:1, 0:1])
            with nc.Block() as blk:
                @blk.vector
                def _(v):
                    vsn(v, v.reciprocal(rows_s[:, gs], rt2[:, :]))
                    if kind == 'lnrms':
                        if rows_s1 is not None:
                            v.tensor_copy(rows_s1[:, gs], rows_s[:, gs])
                            v.scalar_tensor_tensor(rows_c1[:, gs], rows_m[:, gs], -1.0,
                                                   rows_s[:, gs], ALU.mult, ALU.mult)
                        v.tensor_tensor(rt2[:, :], rows_s[:, gs], rows_s[:, gs], ALU.mult)
                        v.tensor_tensor(rt2[:, :], rt1[:, :], rt2[:, :], ALU.mult)
            if kind == 'lnrms':
                with nc.Block() as blk:
                    @blk.scalar
                    def _(s):
                        s.activation(rt2[:, :], rt2[:, :], AF.Sqrt, bias=eps2_col[0:1, 0:1])
                with nc.Block() as blk:
                    @blk.vector
                    def _(v):
                        v.reciprocal(rt1[:, :], rt2[:, :])
                        v.tensor_tensor(rows_s[:, gs], rows_s[:, gs], rt1[:, :], ALU.mult)
            if kind != 'rms':
                with nc.Block() as blk:
                    @blk.vector
                    def _(v):
                        v.scalar_tensor_tensor(rows_c[:, gs], rows_m[:, gs], -1.0, rows_s[:, gs],
                                               ALU.mult, ALU.mult)
        if shared is not None:
            shared["n_sq"] = b_sq + NG * nic
            shared["n_mm"] = b_mm + NG * nic


def emit_ln_zn(nc, c, vsn, x_sb, zn_sb, ones_col, ones_row, kind,
               eps1_col=None, eps2_col=None, nic=NIC, xcols=R, zcols=R,
               gain_rowtile=None, gain_coltile=None, rms_gain=False,
               rows_s1=None, rows_c1=None, width=None, name="", ones_col_bf=None):
    width = float(nic * 128) if width is None else float(width)
    with ExitStack() as sbx:
        p_sx = None if kind == 'rms' else sbx.enter_context(nc.psum_tensor("lz_sx" + name, [128, G], F32))
        p_sq = sbx.enter_context(nc.psum_tensor("lz_sq" + name, [128, G], F32))
        pR = sbx.enter_context(nc.psum_tensor("lz_pR" + name, [128, G], F32))
        pC = None
        if not rms_gain:
            pC = [sbx.enter_context(nc.psum_tensor(f"lz_pC{i}" + name, [128, G], F32)) for i in range(2)]
        sqb = [sbx.enter_context(nc.sbuf_tensor(f"lz_sqb{i}" + name, [128, nic * G], BF16)) for i in range(2)]
        xbb = [sbx.enter_context(nc.sbuf_tensor(f"lz_xbb{i}" + name, [128, nic * G], BF16)) for i in range(2)]
        rt1 = sbx.enter_context(nc.sbuf_tensor("lz_rt1" + name, [1, NG * G], F32))
        rt2 = sbx.enter_context(nc.sbuf_tensor("lz_rt2" + name, [1, NG * G], F32))
        rS = sbx.enter_context(nc.sbuf_tensor("lz_rS" + name, [1, NG * G], F32))
        rC = sbx.enter_context(nc.sbuf_tensor("lz_rC" + name, [1, NG * G], F32))
        scr = sbx.enter_context(nc.sbuf_tensor("lz_scr" + name, [128, G], F32))
        plan = Plan()
        prev_summ = [None, None]
        prev_apply_last = None
        aplist = [None, None]
        for g in range(NG):
            gs = slice(g * G, (g + 1) * G)
            def mk_sqall(g=g):
                def f(s):
                    for ic in range(nic):
                        ins = s.activation(sqb[g % 2][:, ic * G:(ic + 1) * G],
                                           x_sb[:, ic * xcols + g * G: ic * xcols + (g + 1) * G],
                                           AF.Square)
                        if kind != 'rms':
                            ins = s.activation(xbb[g % 2][:, ic * G:(ic + 1) * G],
                                               x_sb[:, ic * xcols + g * G: ic * xcols + (g + 1) * G],
                                               AF.Copy)
                    return ins
                return f
            sq = plan.add_n("act", mk_sqall(), [prev_summ[g % 2]])
            def mk_summ(g=g):
                def f(t):
                    if kind != 'rms':
                        for ic in range(nic):
                            t.matmul(p_sx[0:1, :], ones_col_bf[:, :],
                                     xbb[g % 2][:, ic * G:(ic + 1) * G],
                                     start=(ic == 0), stop=(ic == nic - 1))
                    for ic in range(nic):
                        ins = t.matmul(p_sq[0:1, :], ones_col_bf[:, :], sqb[g % 2][:, ic * G:(ic + 1) * G],
                                       start=(ic == 0), stop=(ic == nic - 1))
                    return ins
                return f
            sdeps = [sq]
            if g >= 1:
                sdeps.append(rows1_ref)
            summ = plan.add_n("pe", mk_summ(), sdeps)
            prev_summ[g % 2] = summ
            def mk_rows1(g=g, gs=gs):
                def f(v):
                    if kind == 'rms':
                        return v.tensor_scalar(rt1[0:1, gs], p_sq[0:1, :], 1.0 / width, None, ALU.mult)
                    v.tensor_scalar(rt2[0:1, gs], p_sx[0:1, :], 1.0 / width, None, ALU.mult)
                    v.tensor_tensor(scr[0:1, 0:G], rt2[0:1, gs], rt2[0:1, gs], ALU.mult)
                    return v.scalar_tensor_tensor(rt1[0:1, gs], p_sq[0:1, :], 1.0 / width,
                                                  scr[0:1, 0:G], ALU.mult, ALU.subtract)
                return f
            rows1_ref = plan.add_n("dve", mk_rows1(), [summ])
            def mk_sqrt1(g=g, gs=gs):
                def f(s):
                    return s.activation(rS[0:1, gs], rt1[0:1, gs], AF.Sqrt, bias=eps1_col[0:1, 0:1])
                return f
            sq1 = plan.add_n("act", mk_sqrt1(), [rows1_ref])
            def mk_rows2(g=g, gs=gs):
                def f(v):
                    vsn(v, v.reciprocal(rS[0:1, gs], rS[0:1, gs]))
                    ins = None
                    if kind == 'lnrms':
                        if rows_s1 is not None:
                            v.tensor_copy(rows_s1[0:1, gs], rS[0:1, gs])
                            v.scalar_tensor_tensor(rows_c1[0:1, gs], rt2[0:1, gs], -1.0,
                                                   rS[0:1, gs], ALU.mult, ALU.mult)
                        v.tensor_tensor(scr[0:1, 0:G], rS[0:1, gs], rS[0:1, gs], ALU.mult)
                        ins = v.tensor_tensor(scr[0:1, 0:G], rt1[0:1, gs], scr[0:1, 0:G], ALU.mult)
                    if ins is None:
                        ins = v.tensor_copy(scr[0:1, 0:G], rS[0:1, gs])
                    return ins
                return f
            rows2_ref = plan.add_n("dve", mk_rows2(), [sq1])
            if kind == 'lnrms':
                def mk_sqrt2(g=g, gs=gs):
                    def f(s):
                        return s.activation(scr[0:1, 0:G], scr[0:1, 0:G], AF.Sqrt,
                                            bias=eps2_col[0:1, 0:1])
                    return f
                sq2 = plan.add_n("act", mk_sqrt2(), [rows2_ref])
                def mk_rows3(g=g, gs=gs):
                    def f(v):
                        vsn(v, v.reciprocal(scr[0:1, 0:G], scr[0:1, 0:G]))
                        ins = v.tensor_tensor(rS[0:1, gs], rS[0:1, gs], scr[0:1, 0:G], ALU.mult)
                        if True:
                            vsn(v, ins)
                            ins = v.scalar_tensor_tensor(rC[0:1, gs], rt2[0:1, gs], -1.0,
                                                         rS[0:1, gs], ALU.mult, ALU.mult)
                        return ins
                    return f
                rows_final = plan.add_n("dve", mk_rows3(), [sq2])
            elif kind == 'ln':
                def mk_rows3(g=g, gs=gs):
                    def f(v):
                        return v.scalar_tensor_tensor(rC[0:1, gs], rt2[0:1, gs], -1.0,
                                                      rS[0:1, gs], ALU.mult, ALU.mult)
                    return f
                rows_final = plan.add_n("dve", mk_rows3(), [rows2_ref])
            else:
                rows_final = rows2_ref
            def mk_bR(g=g, gs=gs):
                def f(t):
                    return t.matmul(pR[:, :], ones_row[:, :], rS[0:1, gs], start=True, stop=True)
                return f
            bR = plan.add_n("pe", mk_bR(), [rows_final, prev_apply_last])
            if rms_gain:
                for ic in range(nic):
                    def mk_ap(g=g, ic=ic):
                        def f(v):
                            v.tensor_tensor(scr[:, :], x_sb[:, ic * xcols + g * G: ic * xcols + (g + 1) * G],
                                            pR[:, :], ALU.mult)
                            return v.tensor_scalar(
                                zn_sb[:, ic * zcols + g * G: ic * zcols + (g + 1) * G],
                                scr[:, :], gain_coltile[:, ic:ic + 1], None, ALU.mult)
                        return f
                    prev_apply_last = plan.add_n("dve", mk_ap(), [bR])
            elif gain_rowtile is not None:
                def mk_bC0g(g=g, gs=gs):
                    def f(t):
                        return t.matmul(pC[0][:, :], ones_row[:, :], rC[0:1, gs],
                                        start=True, stop=True)
                    return f
                bC0 = plan.add_n("pe", mk_bC0g(), [rows_final, prev_apply_last])
                def mk_apg(g=g):
                    def f(v):
                        for ic in range(nic):
                            v.tensor_tensor(scr[:, :],
                                            x_sb[:, ic * xcols + g * G: ic * xcols + (g + 1) * G],
                                            pR[:, :], ALU.mult)
                            v.tensor_tensor(scr[:, :], scr[:, :], pC[0][:, :], ALU.add)
                            ins = v.tensor_scalar(
                                zn_sb[:, ic * zcols + g * G: ic * zcols + (g + 1) * G],
                                scr[:, :], gain_coltile[:, ic:ic + 1], None, ALU.mult)
                        return ins
                    return f
                prev_apply_last = plan.add_n("dve", mk_apg(), [bC0, bR])
            else:
                def mk_bC0(g=g, gs=gs):
                    def f(t):
                        return t.matmul(pC[0][:, :], ones_row[:, :], rC[0:1, gs],
                                        start=True, stop=True)
                    return f
                bC0 = plan.add_n("pe", mk_bC0(), [rows_final, prev_apply_last])
                def mk_ap(g=g):
                    def f(v):
                        for ic in range(nic):
                            v.tensor_tensor(scr[:, :],
                                            x_sb[:, ic * xcols + g * G: ic * xcols + (g + 1) * G],
                                            pR[:, :], ALU.mult)
                            ins = v.tensor_tensor(
                                zn_sb[:, ic * zcols + g * G: ic * zcols + (g + 1) * G],
                                scr[:, :], pC[0][:, :], ALU.add)
                        return ins
                    return f
                prev_apply_last = plan.add_n("dve", mk_ap(), [bC0, bR])
        plan.emit(nc, c, "lz" + name)


def emit_zn(nc, x_sb, zn_sb, rows_s, rows_c, ones_row, gain_rowtile=None, gain_coltile=None,
            nic=NIC, xin_cols=R, zout_cols=R, name="", rms_gain=False, zoff=0, xoff=0):
    with ExitStack() as c:
        pR = c.enter_context(nc.psum_tensor("pR" + name, [128, G], F32))
        pC = None if rms_gain else c.enter_context(nc.psum_tensor("pC" + name, [128, G], F32))
        scr = c.enter_context(nc.sbuf_tensor("znscr" + name, [128, G], F32))
        for g in range(NG):
            if rms_gain:
                with nc.Block() as blk:
                    @blk.tensor
                    def _(t):
                        t.matmul(pR[:, :], ones_row[:, :], rows_s[0:1, g * G:(g + 1) * G],
                                 start=True, stop=True)
                with nc.Block() as blk:
                    @blk.vector
                    def _(v):
                        for ic in range(nic):
                            xs = x_sb[:, xoff + ic * xin_cols + g * G: xoff + ic * xin_cols + (g + 1) * G]
                            zs = zn_sb[:, zoff + ic * zout_cols + g * G: zoff + ic * zout_cols + (g + 1) * G]
                            v.tensor_tensor(scr[:, :], xs, pR[:, :], ALU.mult)
                            v.tensor_scalar(zs, scr[:, :], gain_coltile[:, ic:ic + 1], None, ALU.mult)
                continue
            if gain_rowtile is None:
                with nc.Block() as blk:
                    @blk.tensor
                    def _(t):
                        t.matmul(pR[:, :], ones_row[:, :], rows_s[0:1, g * G:(g + 1) * G],
                                 start=True, stop=True)
                        t.matmul(pC[:, :], ones_row[:, :], rows_c[0:1, g * G:(g + 1) * G],
                                 start=True, stop=True)
                with nc.Block() as blk:
                    @blk.vector
                    def _(v):
                        for ic in range(nic):
                            xs = x_sb[:, ic * xin_cols + g * G: ic * xin_cols + (g + 1) * G]
                            zs = zn_sb[:, ic * zout_cols + g * G: ic * zout_cols + (g + 1) * G]
                            v.tensor_tensor(scr[:, :], xs, pR[:, :], ALU.mult)
                            v.tensor_tensor(zs, scr[:, :], pC[:, :], ALU.add)
            else:
                for ic in range(nic):
                    with nc.Block() as blk:
                        @blk.tensor
                        def _(t):
                            if ic == 0:
                                t.matmul(pR[:, :], ones_row[:, :], rows_s[0:1, g * G:(g + 1) * G],
                                         start=True, stop=True)
                            t.matmul(pC[:, :], gain_rowtile[0:1, ic * 128:(ic + 1) * 128],
                                     rows_c[0:1, g * G:(g + 1) * G], start=True, stop=True)
                    with nc.Block() as blk:
                        @blk.vector
                        def _(v):
                            xs = x_sb[:, ic * xin_cols + g * G: ic * xin_cols + (g + 1) * G]
                            zs = zn_sb[:, ic * zout_cols + g * G: ic * zout_cols + (g + 1) * G]
                            v.tensor_tensor(scr[:, :], xs, pR[:, :], ALU.mult)
                            v.scalar_tensor_tensor(zs, scr[:, :], gain_coltile[:, ic:ic + 1],
                                                   pC[:, :], ALU.mult, ALU.add)


def declare_consts(nc):
    ios = {}
    ios["ones_col"] = nc.dram_tensor("ones_col", [128, 1], F32, kind="ExternalInput").ap()
    ios["ones_row"] = nc.dram_tensor("ones_row", [1, 128], F32, kind="ExternalInput").ap()
    ios["ident_f32"] = nc.dram_tensor("ident_f32", [128, 128], F32, kind="ExternalInput").ap()
    ios["eps_cols"] = nc.dram_tensor("eps_cols", [128, 4], F32, kind="ExternalInput").ap()
    ios["ident_bf"] = nc.dram_tensor("ident_bf", [128, 128], BF16, kind="ExternalInput").ap()
    ios["ones_col_bf"] = nc.dram_tensor("ones_col_bf", [128, 1], BF16, kind="ExternalInput").ap()
    ios["triu"] = nc.dram_tensor("triu", [128, 128], F32, kind="ExternalInput").ap()
    return ios


def const_arrays():
    return {
        "ones_col": np.ones((128, 1), np.float32),
        "ones_row": np.ones((1, 128), np.float32),
        "ident_f32": np.eye(128, dtype=np.float32),
        "eps_cols": np.tile(np.array([[1e-5, 1e-6, 0.0, 0.0]], np.float32), (128, 1)),
        "ident_bf": np.eye(128, dtype=ml_dtypes.bfloat16),
        "ones_col_bf": np.ones((128, 1), ml_dtypes.bfloat16),
        "triu": np.triu(np.ones((128, 128), np.float32)),
    }


def load_consts(nc, c, io):
    ones_col = c.enter_context(nc.sbuf_tensor("c_onescol", [128, 1], F32))
    ones_row = c.enter_context(nc.sbuf_tensor("c_onesrow", [1, 128], F32))
    ident = c.enter_context(nc.sbuf_tensor("c_ident", [128, 128], F32))
    eps_cols = c.enter_context(nc.sbuf_tensor("c_eps", [128, 4], F32))
    ident_bf = c.enter_context(nc.sbuf_tensor("c_identbf", [128, 128], BF16))
    ones_col_bf = c.enter_context(nc.sbuf_tensor("c_onescolbf", [128, 1], BF16))
    triu = c.enter_context(nc.sbuf_tensor("c_triu", [128, 128], F32))
    ld = c.enter_context(nc.semaphore("c_ld"))
    with nc.Block() as blk:
        @blk.sync
        def _(sy):
            sy.dma_start(out=ones_col[:, :], in_=io["ones_col"]).then_inc(ld, 16)
            sy.dma_start(out=ones_row[:, :], in_=io["ones_row"]).then_inc(ld, 16)
            sy.dma_start(out=ident[:, :], in_=io["ident_f32"]).then_inc(ld, 16)
            sy.dma_start(out=eps_cols[:, :], in_=io["eps_cols"]).then_inc(ld, 16)
            sy.dma_start(out=ident_bf[:, :], in_=io["ident_bf"]).then_inc(ld, 16)
            sy.dma_start(out=ones_col_bf[:, :], in_=io["ones_col_bf"]).then_inc(ld, 16)
            sy.dma_start(out=triu[:, :], in_=io["triu"]).then_inc(ld, 16)
            sy.wait_ge(ld, 112)
    return {"ones_col": ones_col, "ones_row": ones_row, "ident": ident,
            "eps_cols": eps_cols, "ident_bf": ident_bf, "ones_col_bf": ones_col_bf,
            "triu": triu}



def emit_blockB(nc, c, io, x_sb, zn_sb, ones_col, ones_row, ident, eps_cols, stage="full", ones_col_bf=None):
    sb = ExitStack()
    NE, F = 8, 2048
    if stage.startswith("exp"):
        NE = int(stage[3:])
    NFC = F // 128
    NDC = 8
    vsn = VSync(c, nc, "b_vs")
    if True:
        emit_ln_zn(nc, c, vsn, x_sb, zn_sb, ones_col, ones_row, 'ln',
                   eps1_col=eps_cols[:, 0:1], name="B", ones_col_bf=ones_col_bf)
    if stage == "ln":
        return

    fw_tm = sb.enter_context(nc.sbuf_tensor("b_fwtm", [128, NT * 8], F32))
    with ExitStack() as cc:
        wrs = cc.enter_context(nc.sbuf_tensor("b_wr", [128, NIC * 8], BF16))
        rld = c.enter_context(nc.semaphore("b_rld"))
        with nc.Block() as blk:
            @blk.gpsimd
            def _(g):
                g.dma_start(out=wrs[:, :].rearrange("p (ic e) -> p ic e", ic=NIC),
                            in_=io["wr"].rearrange("(ic p) e -> p ic e", p=128)).then_inc(rld, 16)
                g.wait_ge(rld, 16)
        pl = cc.enter_context(nc.psum_tensor("b_pl", [128, 2 * G], F32))
        ltm = cc.enter_context(nc.sbuf_tensor("b_ltm", [128, NT * 8], F32))
        prb = cc.enter_context(nc.sbuf_tensor("b_prb", [128, 8], F32))
        col1 = cc.enter_context(nc.sbuf_tensor("b_col1", [128, 1], F32))
        col2 = cc.enter_context(nc.sbuf_tensor("b_col2", [128, 1], F32))
        col3 = cc.enter_context(nc.sbuf_tensor("b_col3", [128, 1], F32))
        msk = cc.enter_context(nc.sbuf_tensor("b_msk", [128, 8], F32))
        rp = Plan()
        d3_prev = None
        e1_prev = None
        d1list = [None] * NT
        for tt in range(NT):
            ts8 = slice((tt % 2) * G + (tt // 2) * 8, (tt % 2) * G + (tt // 2) * 8 + 8)
            def mk_lmm(tt=tt, ts8=ts8):
                def f(t):
                    for ic in range(NIC):
                        ins = t.matmul(pl[:, ts8],
                                       zn_sb[:, ic * R + tt * 128:ic * R + (tt + 1) * 128],
                                       wrs[:, ic * 8:(ic + 1) * 8],
                                       start=(tt < 2 and ic == 0), stop=(ic == NIC - 1),
                                       skip_group_check=True)
                    return ins
                return f
            lmm = rp.add_n("pe", mk_lmm(), [d1list[tt - 2] if tt >= 2 else None])
            def mk_d1(tt=tt, ts8=ts8):
                def f(v):
                    return v.tensor_reduce(col1[:, :], pl[:, ts8], AX.X, ALU.max, negate=True)
                return f
            d1 = rp.add_n("dve", mk_d1(), [lmm, e1_prev])
            d1list[tt] = d1
            def mk_e1(tt=tt, ts8=ts8):
                def f(s):
                    return s.activation(prb[:, :], pl[:, ts8], AF.Exp, bias=col1[:, :])
                return f
            e1 = rp.add_n("act", mk_e1(), [d1, d3_prev])
            d1list[tt] = e1
            ls8 = slice(tt * 8, (tt + 1) * 8)
            def mk_d2(tt=tt, ls8=ls8):
                def f(v):
                    vsn(v, v.tensor_reduce(col2[:, :], prb[:, :], AX.X, ALU.add))
                    vsn(v, v.reciprocal(col2[:, :], col2[:, :]))
                    vsn(v, v.tensor_scalar(prb[:, :], prb[:, :], col2[:, :], None, ALU.mult))
                    vsn(v, v.tensor_reduce(col1[:, :], prb[:, :], AX.X, ALU.max))
                    vsn(v, v.tensor_scalar(msk[:, :], prb[:, :], col1[:, :], None, ALU.is_ge))
                    vsn(v, v.scalar_tensor_tensor(ltm[:, ls8], msk[:, :], -3.0, prb[:, :],
                                                  ALU.mult, ALU.add))
                    vsn(v, v.tensor_reduce(col3[:, :], ltm[:, ls8], AX.X, ALU.max))
                    return v.tensor_scalar(msk[:, :], prb[:, :], col3[:, :], None, ALU.is_ge)
                return f
            d2 = rp.add_n("dve", mk_d2(), [e1])
            def mk_e2(tt=tt):
                def f(s):
                    return s.activation(prb[:, :], prb[:, :], AF.Exp)
                return f
            e2 = rp.add_n("act", mk_e2(), [d2])
            def mk_d3(tt=tt, ls8=ls8):
                def f(v):
                    vsn(v, v.tensor_tensor(prb[:, :], prb[:, :], msk[:, :], ALU.mult))
                    vsn(v, v.tensor_reduce(col2[:, :], prb[:, :], AX.X, ALU.add))
                    vsn(v, v.reciprocal(col2[:, :], col2[:, :]))
                    return v.tensor_scalar(fw_tm[:, ls8], prb[:, :], col2[:, :], None, ALU.mult)
                return f
            d3 = rp.add_n("dve", mk_d3(), [e2])
            d3_prev, e1_prev = d3, e2
        rp.emit(nc, c, "brp")
    f_sb = sb.enter_context(nc.sbuf_tensor("b_f", [128, NFC * R], BF16))
    wb = sb.enter_context(nc.sbuf_tensor("b_wb", [128, R], F32))
    row_all = sb.enter_context(nc.sbuf_tensor("b_rowall", [1, R], F32))
    wgb = [sb.enter_context(nc.sbuf_tensor(f"b_wg{i}", [128, NIC * 128], BF16)) for i in range(2)]
    wub = [sb.enter_context(nc.sbuf_tensor(f"b_wu{i}", [128, NIC * 128], BF16)) for i in range(2)]
    wdb = [sb.enter_context(nc.sbuf_tensor(f"b_wd{i}", [128, NFC * 128], BF16)) for i in range(2)]
    stmp = sb.enter_context(nc.sbuf_tensor("b_stmp", [128, 4 * G], F32))
    ttmp = sb.enter_context(nc.sbuf_tensor("b_ttmp", [128, G], F32))
    pg = sb.enter_context(nc.psum_tensor("b_pg", [128, 4 * G], F32))
    pu = sb.enter_context(nc.psum_tensor("b_pu", [128, 4 * G], F32))
    wsem = c.enter_context(nc.semaphore("b_wsem"))
    fsem = c.enter_context(nc.semaphore("b_fsem"))
    asem = c.enter_context(nc.semaphore("b_asem"))
    esem = c.enter_context(nc.semaphore("b_esem"))

    W_E = 32 * NFC + 16 * NDC
    F_E = 2 + 2 * NFC + NDC * 2
    A_E = 1 + 4 * NFC
    E_E = 1 + 2 * NFC + 2 * NDC

    with nc.Block() as blk:
        @blk.gpsimd
        def _(gg):
            for e in range(NE):
                for fc in range(NFC):
                    if fc >= 2:
                        gg.wait_ge(fsem, 50 * e + 2 + 2 * (fc - 1))
                    elif e >= 1:
                        gg.wait_ge(fsem, 50 * e - 16 + 2 * fc)
                    gg.dma_start(
                        out=wgb[fc % 2][:, :].rearrange("p (ic f) -> p ic f", ic=NIC),
                        in_=io["wg"][e, :, fc * 128:(fc + 1) * 128].rearrange("(ic p) f -> p ic f", p=128)
                    ).then_inc(wsem, 16)
                    gg.dma_start(
                        out=wub[fc % 2][:, :].rearrange("p (ic f) -> p ic f", ic=NIC),
                        in_=io["wu"][e, :, fc * 128:(fc + 1) * 128].rearrange("(ic p) f -> p ic f", p=128)
                    ).then_inc(wsem, 16)
                for dc in (range(NDC) if DBG["down"] else []):
                    if dc >= 2:
                        gg.wait_ge(fsem, 50 * e + 34 + 2 * (dc - 1))
                    elif e >= 1:
                        gg.wait_ge(fsem, 50 * e - 2 + 2 * dc)
                    gg.dma_start(
                        out=wdb[dc % 2][:, :].rearrange("p (fc d) -> p fc d", fc=NFC),
                        in_=io["wd"][e, :, dc * 128:(dc + 1) * 128].rearrange("(fc p) d -> p fc d", p=128)
                    ).then_inc(wsem, 16)

        @blk.tensor
        def _(t):
            for e in range(NE):
                W0 = e * W_E
                if e >= 1:
                    t.wait_ge(esem, 49 * e)
                for tt in range(NT):
                    ins = t.transpose(pg[0:1, tt * 128:(tt + 1) * 128],
                                      fw_tm[:, tt * 8 + e: tt * 8 + e + 1], ident[:, :])
                ins.then_inc(fsem, 1)
                t.wait_ge(asem, 65 * e + 1)
                for g in range(NG):
                    ins = t.matmul(pu[:, g * G:(g + 1) * G], ones_row[:, :],
                                   row_all[0:1, g * G:(g + 1) * G], start=True, stop=True)
                ins.then_inc(fsem, 1)
                for fc in range(NFC):
                    t.wait_ge(wsem, W0 + 32 * (fc + 1))
                    for half in range(2):
                        u = 2 * fc + half
                        if not DBG["evac"]:
                            pass
                        elif u >= 2:
                            t.wait_ge(esem, 49 * e + u)
                        else:
                            t.wait_ge(esem, 49 * e + 1)
                        bs = (u % 2) * 2 * G
                        for ic in range(NIC):
                            for g2 in range(2):
                                g = half * 2 + g2
                                t.matmul(pg[:, bs + g2 * G: bs + (g2 + 1) * G],
                                         wgb[fc % 2][:, ic * 128:(ic + 1) * 128],
                                         zn_sb[:, ic * R + g * G:ic * R + (g + 1) * G],
                                         start=(ic == 0), stop=(ic == NIC - 1))
                        for ic in range(NIC):
                            for g2 in range(2):
                                g = half * 2 + g2
                                ins = t.matmul(pu[:, bs + g2 * G: bs + (g2 + 1) * G],
                                               wub[fc % 2][:, ic * 128:(ic + 1) * 128],
                                               zn_sb[:, ic * R + g * G:ic * R + (g + 1) * G],
                                               start=(ic == 0), stop=(ic == NIC - 1))
                        ins.then_inc(fsem, 1)
                W1 = W0 + 32 * NFC
                for dc in (range(NDC) if DBG["down"] else []):
                    t.wait_ge(wsem, W1 + 16 * (dc + 1))
                    for half in range(2):
                        u = 2 * dc + half
                        if u >= 2:
                            t.wait_ge(esem, 49 * e + 33 + u - 1)
                        else:
                            t.wait_ge(esem, 49 * e + 33)
                        bs = (u % 2) * 2 * G
                        for fc in range(NFC):
                            for g2 in range(2):
                                g = half * 2 + g2
                                ins = t.matmul(pg[:, bs + g2 * G: bs + (g2 + 1) * G],
                                               wdb[dc % 2][:, fc * 128:(fc + 1) * 128],
                                               f_sb[:, fc * R + g * G:fc * R + (g + 1) * G],
                                               start=(fc == 0), stop=(fc == NFC - 1))
                        ins.then_inc(fsem, 1)

        @blk.scalar
        def _(s):
            for e in range(NE):
                s.wait_ge(fsem, 50 * e + 1)
                s.activation(row_all[0:1, :], pg[0:1, 0:R], AF.Copy).then_inc(asem, 1)
                s.wait_ge(fsem, 50 * e + 2)
                for g in range(NG):
                    ins = s.activation(wb[:, g * G:(g + 1) * G], pu[:, g * G:(g + 1) * G],
                                       AF.Copy)
                ins.then_inc(esem, 1)
                for fc in (range(NFC) if DBG["evac"] else []):
                    for half in range(2):
                        u = 2 * fc + half
                        s.wait_ge(fsem, 50 * e + 2 + u + 1)
                        bs = (u % 2) * 2 * G
                        for g2 in range(2):
                            s.activation(stmp[:, ((u % 2) * 2 + g2) * G: ((u % 2) * 2 + g2 + 1) * G],
                                         pg[:, bs + g2 * G: bs + (g2 + 1) * G],
                                         AF.Silu).then_inc(asem, 1)

        @blk.vector
        def _(v):
            for e in range(NE):
                for fc in (range(NFC) if DBG["evac"] else []):
                    for half in range(2):
                        u = 2 * fc + half
                        bs = (u % 2) * 2 * G
                        for g2 in range(2):
                            g = half * 2 + g2
                            v.wait_ge(asem, 65 * e + 1 + 2 * u + g2 + 1)
                            ins = v.tensor_tensor(f_sb[:, fc * R + g * G:fc * R + (g + 1) * G],
                                                  stmp[:, ((u % 2) * 2 + g2) * G: ((u % 2) * 2 + g2 + 1) * G],
                                                  pu[:, bs + g2 * G: bs + (g2 + 1) * G], ALU.mult)
                        ins.then_inc(esem, 1)
                for dc in (range(NDC) if (DBG["down"] and DBG["evac"]) else []):
                    for half in range(2):
                        u = 2 * dc + half
                        bs = (u % 2) * 2 * G
                        v.wait_ge(fsem, 50 * e + 34 + u + 1)
                        for g2 in range(2):
                            g = half * 2 + g2
                            v.tensor_tensor(ttmp[:, :], pg[:, bs + g2 * G: bs + (g2 + 1) * G],
                                            wb[:, g * G:(g + 1) * G], ALU.mult)
                            xs = x_sb[:, dc * R + g * G:dc * R + (g + 1) * G]
                            ins = v.tensor_tensor(xs, xs, ttmp[:, :], ALU.add)
                        ins.then_inc(esem, 1)
    sb.close()



def emit_blockC(nc, c, io, consts):
    sb = ExitStack()
    H, NCH, M = 16, 16, 64
    ones_col, ones_row = consts["ones_col"], consts["ones_row"]
    eps_cols, ident_bf, triu = consts["eps_cols"], consts["ident_bf"], consts["triu"]
    vsn = consts["vsn"]

    zn = sb.enter_context(nc.sbuf_tensor("c_zn", [128, NIC * R], BF16))
    bcols = sb.enter_context(nc.sbuf_tensor("c_bcols", [64, 48], F32))
    bcols_bf = sb.enter_context(nc.sbuf_tensor("c_bcolsbf", [64, 48], BF16))
    boutc = sb.enter_context(nc.sbuf_tensor("c_bout", [128, 8], F32))
    mcols = sb.enter_context(nc.sbuf_tensor("c_mcols", [128, 8], F32))
    ld = c.enter_context(nc.semaphore("c_ld2"))
    ldn = [0]
    gld = c.enter_context(nc.semaphore("c_gld2"))
    gldn = [0]

    def LD(eng, *dmas):
        for d_ in dmas:
            d_.then_inc(ld, 16)
            ldn[0] += 16
        eng.wait_ge(ld, ldn[0])

    def GLD(eng, *dmas):
        for d_ in dmas:
            d_.then_inc(gld, 16)
            gldn[0] += 16
        eng.wait_ge(gld, gldn[0])

    with nc.Block() as blk:
        @blk.sync
        def _(sy):
            LD(sy, sy.dma_start(out=bcols[:, :], in_=io["bqkvC"]),
               sy.dma_start(out=boutc[:, :], in_=io["boutC"]),
               sy.dma_start(out=mcols[:, :], in_=io["maskin"]))
        @blk.gpsimd
        def _(gg):
            GLD(gg, gg.dma_start(out=bcols_bf[:, :], in_=io["bqkvC"]))

    with ExitStack() as lnx:
        x_sb = lnx.enter_context(nc.sbuf_tensor("c_x", [128, NIC * R], F32))
        with nc.Block() as blk:
            @blk.sync
            def _(sy):
                LD(sy, sy.dma_start(out=x_sb[:, :].rearrange("p (ic t) -> p ic t", ic=NIC),
                                    in_=io["xin"].rearrange("(ic p) t -> p ic t", p=128)))
        emit_ln_zn(nc, c, vsn, x_sb, zn, ones_col, ones_row, 'ln',
                   eps1_col=eps_cols[:, 0:1], name="C", ones_col_bf=consts["ident_bf"][:, 0:1] if False else consts["ones_col_bf"])

    o_all = sb.enter_context(nc.sbuf_tensor("c_oall", [128, 8 * R], BF16))
    mid = ExitStack()
    Wq_all = mid.enter_context(nc.sbuf_tensor("c_wq", [128, H * NIC * 64], BF16))
    Wk_all = mid.enter_context(nc.sbuf_tensor("c_wk", [128, H * NIC * 64], BF16))
    b8 = mid.enter_context(nc.sbuf_tensor("c_b8", [64, 2 * H], F32))
    with ExitStack() as c2s:
        wrow = [c2s.enter_context(nc.sbuf_tensor(f"c_wrow{i}", [64, NIC * 128], BF16)) for i in range(2)]
        prjall = c2s.enter_context(nc.sbuf_tensor("c_prjall", [64, H * 64], BF16))
        pcmp = c2s.enter_context(nc.psum_tensor("c_pcmp", [128, 1024], F32))
        pbb = [c2s.enter_context(nc.psum_tensor(f"c_pbb{i}", [64, 64], F32)) for i in range(2)]
        ldp = c.enter_context(nc.semaphore("c_ldp"))
        with nc.Block() as blk:
            @blk.gpsimd
            def _(gg):
                gg.dma_start(out=prjall[:, :].rearrange("p (h m) -> p h m", h=H),
                             in_=io["proj"].rearrange("h d m -> d h m")).then_inc(ldp, 16)
                gg.wait_ge(ldp, 16)
        c2p = Plan()
        prev = {0: None, 1: None}
        prev_mm = {0: None, 1: None}
        for h in range(H):
            for qk in range(2):
                i2 = (2 * h + qk) % 2
                dst = Wq_all if qk == 0 else Wk_all
                rows0 = qk * 1024 + h * 64
                def mk_ldw(h=h, qk=qk, rows0=rows0, i2=i2):
                    def f(gg, sem):
                        gg.dma_start(out=wrow[i2][:, :],
                                     in_=io["wqkvT"][rows0:rows0 + 64, :]).then_inc(sem, 16)
                    return f
                ldw = c2p.add_n("gp", mk_ldw(), [prev_mm[i2]], inc_total=16)
                def mk_cmm(h=h, qk=qk, i2=i2):
                    def f(t):
                        for ic in range(NIC):
                            t.matmul(pcmp[:, i2 * 512 + ic * 64: i2 * 512 + (ic + 1) * 64],
                                     wrow[i2][:, ic * 128:(ic + 1) * 128],
                                     prjall[:, h * 64:(h + 1) * 64],
                                     start=True, stop=True, skip_group_check=True)
                        return t.matmul(pbb[i2][:, 0:1], prjall[:, h * 64:(h + 1) * 64],
                                        bcols_bf[:, qk * 16 + h: qk * 16 + h + 1],
                                        start=True, stop=True)
                    return f
                cmm = c2p.add_n("pe", mk_cmm(), [ldw, prev[i2]])
                prev_mm[i2] = cmm
                def mk_cevac(h=h, qk=qk, dst=dst, i2=i2):
                    def f(v):
                        for ic in range(NIC):
                            v.tensor_scalar(dst[:, (h * NIC + ic) * 64:(h * NIC + ic + 1) * 64],
                                            pcmp[:, i2 * 512 + ic * 64: i2 * 512 + (ic + 1) * 64],
                                            0.125, None, ALU.mult)
                        return v.tensor_scalar(b8[:, qk * H + h: qk * H + h + 1], pbb[i2][:, 0:1],
                                               0.125, None, ALU.mult)
                    return f
                prev[i2] = c2p.add_n("dve", mk_cevac(), [cmm])
        c2p.emit(nc, c, "c2p")
    ktm = mid.enter_context(nc.sbuf_tensor("c_ktm", [128, H * NCH * 64], BF16))
    vext = mid.enter_context(nc.sbuf_tensor("c_vext", [128, H * NCH * 65], BF16))
    contrib = mid.enter_context(nc.sbuf_tensor("c_contrib", [64, H * 65], F32))
    Sg_all = mid.enter_context(nc.sbuf_tensor("c_sg", [64, H * 65], F32))
    with ExitStack() as c3s:
        wvs = c3s.enter_context(nc.sbuf_tensor("c_wvs", [128, NIC * 64], BF16))
        fscr = c3s.enter_context(nc.sbuf_tensor("c_fscr", [64, R], BF16))
        pk3 = c3s.enter_context(nc.psum_tensor("c_pk3", [64, NG * G], F32))
        ptp3 = c3s.enter_context(nc.psum_tensor("c_ptp3", [128, NCH * 64], BF16))
        pst3 = c3s.enter_context(nc.psum_tensor("c_pst3", [64, 65], F32))
        c3p = Plan()
        p_vmm = p_vact = p_vtr = p_vcopy = p_kact = p_kcopy = p_scon = None
        for h in range(H):
            def mk_ldv(h=h):
                def f(gg, sem):
                    gg.dma_start(
                        out=wvs[:, :].rearrange("p (ic f) -> p ic f", ic=NIC),
                        in_=io["wqkv"][:, 2048 + h * 64: 2048 + (h + 1) * 64].rearrange(
                            "(ic p) f -> p ic f", p=128)).then_inc(sem, 16)
                return f
            ldv = c3p.add_n("gp", mk_ldv(), [p_vmm], inc_total=16)
            def mk_vmm(h=h):
                def f(t):
                    for g in range(NG):
                        for ic in range(NIC):
                            ins = t.matmul(pk3[:, g * G:(g + 1) * G],
                                           wvs[:, ic * 64:(ic + 1) * 64],
                                           zn[:, ic * R + g * G: ic * R + (g + 1) * G],
                                           start=(ic == 0), stop=(ic == NIC - 1))
                    return ins
                return f
            vmm = c3p.add_n("pe", mk_vmm(), [ldv, p_kact])
            def mk_vact(h=h):
                def f(s):
                    for g in range(NG):
                        ins = s.activation(fscr[:, g * G:(g + 1) * G], pk3[0:64, g * G:(g + 1) * G],
                                           AF.Identity, bias=bcols[:, 32 + h: 33 + h])
                    return ins
                return f
            vact = c3p.add_n("act", mk_vact(), [vmm])
            def mk_vtr(h=h):
                def f(t):
                    for c2 in range(NCH):
                        ins = t.transpose(ptp3[:, c2 * 64:(c2 + 1) * 64],
                                          fscr[:, c2 * 128:(c2 + 1) * 128], ident_bf[0:64, 0:64])
                    return ins
                return f
            vtr = c3p.add_n("pe", mk_vtr(), [vact, p_kcopy])
            def mk_vcopy(h=h):
                def f(v):
                    v.memset(vext[:, h * NCH * 65:(h + 1) * NCH * 65], 1.0)
                    for c2 in range(NCH):
                        ins = v.tensor_copy(vext[:, (h * NCH + c2) * 65:(h * NCH + c2) * 65 + 64],
                                            ptp3[:, c2 * 64:(c2 + 1) * 64])
                    return ins
                return f
            vcopy = c3p.add_n("dve", mk_vcopy(), [vtr])
            def mk_kmm(h=h):
                def f(t):
                    for g in range(NG):
                        for ic in range(NIC):
                            ins = t.matmul(pk3[:, g * G:(g + 1) * G],
                                           Wk_all[:, (h * NIC + ic) * 64:(h * NIC + ic + 1) * 64],
                                           zn[:, ic * R + g * G: ic * R + (g + 1) * G],
                                           start=(ic == 0), stop=(ic == NIC - 1))
                    return ins
                return f
            kmm = c3p.add_n("pe", mk_kmm(), [vact])
            def mk_kact(h=h):
                def f(s):
                    for g in range(NG):
                        ins = s.activation(fscr[:, g * G:(g + 1) * G], pk3[0:64, g * G:(g + 1) * G],
                                           AF.Relu, bias=b8[:, H + h: H + h + 1])
                    return ins
                return f
            p_kact = c3p.add_n("act", mk_kact(), [kmm, vtr])
            def mk_ktr(h=h):
                def f(t):
                    for c2 in range(NCH):
                        ins = t.transpose(ptp3[:, c2 * 64:(c2 + 1) * 64],
                                          fscr[:, c2 * 128:(c2 + 1) * 128], ident_bf[0:64, 0:64])
                    return ins
                return f
            ktr = c3p.add_n("pe", mk_ktr(), [p_kact, vcopy])
            def mk_kcopy(h=h):
                def f(v):
                    for c2 in range(NCH):
                        ins = v.tensor_copy(ktm[:, (h * NCH + c2) * 64:(h * NCH + c2 + 1) * 64],
                                            ptp3[:, c2 * 64:(c2 + 1) * 64])
                    return ins
                return f
            p_kcopy = c3p.add_n("dve", mk_kcopy(), [ktr])
            def mk_smm(h=h):
                def f(t):
                    for c2 in range(NCH):
                        ins = t.matmul(pst3[:, :],
                                       ktm[:, (h * NCH + c2) * 64:(h * NCH + c2 + 1) * 64],
                                       vext[:, (h * NCH + c2) * 65:(h * NCH + c2 + 1) * 65],
                                       start=(c2 == 0), stop=(c2 == NCH - 1))
                    return ins
                return f
            smm = c3p.add_n("pe", mk_smm(), [p_kcopy, vcopy, p_scon])
            def mk_scon(h=h):
                def f(v):
                    return v.tensor_copy(contrib[:, h * 65:(h + 1) * 65], pst3[:, :])
                return f
            p_scon = c3p.add_n("dve", mk_scon(), [smm])
            p_vmm, p_vtr, p_vcopy = vmm, vtr, vcopy
        c3p.emit(nc, c, "c3p")
    ccs = c.enter_context(nc.semaphore("c_ccs2"))
    with nc.Block() as blk:
        @blk.sync
        def _(sy):
            LD(sy, sy.dma_start(out=io["cc_inC"].rearrange("(h p) d -> p h d", p=64),
                                in_=contrib[:, :].rearrange("p (h d) -> p h d", h=H)))
    with nc.Block() as blk:
        @blk.gpsimd
        def _(gg):
            gg.collective_compute("AllGather", ALU.bypass,
                                  replica_groups=[list(range(8))],
                                  ins=[io["cc_inC"].opt()],
                                  outs=[io["cc_outC"].opt()]).then_inc(ccs, 1)
            gg.wait_ge(ccs, 1)
    with ExitStack() as gsc:
        gath = gsc.enter_context(nc.sbuf_tensor("c_gath", [64, 128 * 65], F32))
        with nc.Block() as blk:
            @blk.gpsimd
            def _(gg):
                GLD(gg, gg.dma_start(out=gath[:, :].rearrange("p (q d) -> p q d", q=128),
                                     in_=io["cc_outC"].rearrange("(q p) d -> p q d", p=64)))
        with nc.Block() as blk:
            @blk.vector
            def _(v):
                v.memset(Sg_all[:, :], 0.0)
                for r in range(8):
                    v.tensor_scalar(gath[:, r * H * 65:(r + 1) * H * 65],
                                    gath[:, r * H * 65:(r + 1) * H * 65],
                                    mcols[0:64, r:r + 1], None, ALU.mult)
                    v.tensor_tensor(Sg_all[:, :], Sg_all[:, :],
                                    gath[:, r * H * 65:(r + 1) * H * 65], ALU.add)

    pe_a = c.enter_context(nc.semaphore("c_pea"))
    pe_os = c.enter_context(nc.semaphore("c_peos"))
    act_oc = c.enter_context(nc.semaphore("c_actoc"))
    dve_am = c.enter_context(nc.semaphore("c_dveam"))
    dve_s = c.enter_context(nc.semaphore("c_dves"))
    with ExitStack() as c4s:
        q_pad = c4s.enter_context(nc.sbuf_tensor("c_qpad", [128, R], BF16))
        kfm = c4s.enter_context(nc.sbuf_tensor("c_kfm", [64, R], BF16))
        of65 = c4s.enter_context(nc.sbuf_tensor("c_of65", [128, R], F32))
        S_sb = c4s.enter_context(nc.sbuf_tensor("c_ssb", [128, 65], F32))
        S_bf = c4s.enter_context(nc.sbuf_tensor("c_sbf", [128, NCH * 65], BF16))
        am_sb = c4s.enter_context(nc.sbuf_tensor("c_am", [128, 2 * 128], BF16))
        nrow = c4s.enter_context(nc.sbuf_tensor("c_nrow", [1, R], F32))
        rb_sb = c4s.enter_context(nc.sbuf_tensor("c_rb", [64, 2 * G], F32))
        pa = c4s.enter_context(nc.psum_tensor("c_pa", [128, 2 * G], F32))
        po = c4s.enter_context(nc.psum_tensor("c_po", [128, 2 * G], F32))
        ps = c4s.enter_context(nc.psum_tensor("c_ps", [128, 2 * G], F32))
        pnb = c4s.enter_context(nc.psum_tensor("c_pnb", [128, G], F32))
        ptf = c4s.enter_context(nc.psum_tensor("c_ptf", [128, 1024], BF16))
        with nc.Block() as blk:
            @blk.vector
            def _(v):
                v.memset(q_pad[64:128, :], 0.0)
                v.memset(S_sb[64:128, :], 0.0)
        plan = Plan()
        last_pa = last_po = prev_omm = prev_mmA = prev_tc1 = None
        prev_sacc_c = prev_oadd_c = prev_inter_c = None
        war_nb = war_rb3 = None
        war_rcp = [None, None]
        for h in range(H):
            qe = None
            qm0 = qe1 = None
            for gp in range(2):
                def mk_qmm(h=h, gp=gp):
                    def f(t):
                        for g2 in range(2):
                            g = gp * 2 + g2
                            ptgt = pa if g2 == 0 else po
                            for ic in range(NIC):
                                ins = t.matmul(ptgt[0:64, 0:G],
                                               Wq_all[:, (h * NIC + ic) * 64:(h * NIC + ic + 1) * 64],
                                               zn[:, ic * R + g * G: ic * R + (g + 1) * G],
                                               start=(ic == 0), stop=(ic == NIC - 1))
                        return ins
                    return f
                deps = [last_pa, last_po] if gp == 0 else [qe]
                qm = plan.add_n("pe", mk_qmm(), deps)
                def mk_qe(h=h, gp=gp):
                    def f(s):
                        for g2 in range(2):
                            g = gp * 2 + g2
                            ptgt = pa if g2 == 0 else po
                            ins = s.activation(q_pad[0:64, g * G:(g + 1) * G], ptgt[0:64, 0:G],
                                               AF.Relu, bias=b8[:, h: h + 1])
                        return ins
                    return f
                qe = plan.add_n("act", mk_qe(), [qm, prev_omm] if gp == 0 else [qm])
            qe1 = qe
            tcs = []
            tprev = None
            for ph in range(2):
                def mk_tf(h=h, ph=ph):
                    def f(t):
                        for c2 in range(8):
                            cc2 = ph * 8 + c2
                            ins = t.transpose(ptf[0:64, c2 * 128:(c2 + 1) * 128],
                                              ktm[:, (h * NCH + cc2) * 64:(h * NCH + cc2 + 1) * 64],
                                              ident_bf[:, :])
                        return ins
                    return f
                tf = plan.add_n("pe", mk_tf(), [prev_tc1 if ph == 0 else tprev])
                def mk_tc(h=h, ph=ph):
                    def f(v):
                        return v.tensor_copy(kfm[:, ph * 1024:(ph + 1) * 1024], ptf[0:64, :])
                    return f
                tprev = plan.add_n("dve", mk_tc(), [tf, prev_mmA])
                tcs.append(tprev)
            def mk_init(h=h):
                def f(v):
                    return v.tensor_copy(S_sb[0:64, :], Sg_all[:, h * 65:(h + 1) * 65])
                return f
            init = plan.add_n("dve", mk_init(), [prev_omm])
            mmA = [None] * NCH
            mask = [None] * NCH
            intra = [None] * NCH
            smm = [None] * NCH
            ocopy = [None] * NCH
            sacc = [None] * NCH
            inter = [None] * NCH
            oadd = [None] * NCH
            for c2 in range(NCH):
                pao = (c2 % 2) * G
                def mk_mmA(h=h, c2=c2, pao=pao):
                    def f(t):
                        return t.matmul(pa[:, pao:pao + 128],
                                        kfm[0:64, c2 * 128:(c2 + 1) * 128],
                                        q_pad[0:64, c2 * 128:(c2 + 1) * 128],
                                        start=True, stop=True)
                    return f
                deps = [qe1, tcs[c2 // 8]]
                if c2 >= 2:
                    deps.append(mask[c2 - 2])
                mmA[c2] = plan.add_n("pe", mk_mmA(), deps)
                def mk_mask(h=h, c2=c2, pao=pao):
                    def f(v):
                        return v.tensor_tensor(am_sb[:, (c2 % 2) * 128:(c2 % 2) * 128 + 128],
                                               pa[:, pao:pao + 128], triu[:, :], ALU.mult)
                    return f
                mdeps = [mmA[c2]]
                mdeps.append(intra[c2 - 2] if c2 >= 2 else prev_omm)
                mask[c2] = plan.add_n("dve", mk_mask(), mdeps)
                def mk_intra(h=h, c2=c2, pao=pao):
                    def f(t):
                        return t.matmul(po[0:65, pao:pao + 128],
                                        vext[:, (h * NCH + c2) * 65:(h * NCH + c2 + 1) * 65],
                                        am_sb[:, (c2 % 2) * 128:(c2 % 2) * 128 + 128],
                                        start=True, stop=True)
                    return f
                ideps = [mask[c2]]
                if c2 >= 2:
                    ideps.append(ocopy[c2 - 2])
                else:
                    ideps += [qe1, prev_oadd_c]
                intra[c2] = plan.add_n("pe", mk_intra(), ideps)
                def mk_smm(h=h, c2=c2, pao=pao):
                    def f(t):
                        return t.matmul(ps[0:64, pao:pao + 65],
                                        ktm[:, (h * NCH + c2) * 64:(h * NCH + c2 + 1) * 64],
                                        vext[:, (h * NCH + c2) * 65:(h * NCH + c2 + 1) * 65],
                                        start=True, stop=True)
                    return f
                smm[c2] = plan.add_n("pe", mk_smm(), [sacc[c2 - 2] if c2 >= 2 else prev_sacc_c])
                def mk_ocopy(h=h, c2=c2, pao=pao):
                    def f(s):
                        return s.activation(of65[0:65, c2 * 128:(c2 + 1) * 128],
                                            po[0:65, pao:pao + 128], AF.Copy)
                    return f
                ocopy[c2] = plan.add_n("act", mk_ocopy(), [intra[c2]])
                def mk_sacc(h=h, c2=c2, pao=pao):
                    def f(v):
                        v.tensor_copy(S_bf[:, c2 * 65:(c2 + 1) * 65], S_sb[:, :])
                        return v.tensor_tensor(S_sb[0:64, :], S_sb[0:64, :],
                                               ps[0:64, pao:pao + 65], ALU.add)
                    return f
                sadeps = [smm[c2], init if c2 == 0 else None,
                          prev_inter_c if c2 <= 1 else None]
                sacc[c2] = plan.add_n("dve", mk_sacc(), sadeps)
            for c2 in range(NCH):
                pao = (c2 % 2) * G
                def mk_inter(h=h, c2=c2, pao=pao):
                    def f(t):
                        return t.matmul(po[0:65, pao:pao + 128],
                                        S_bf[:, c2 * 65:(c2 + 1) * 65],
                                        q_pad[:, c2 * 128:(c2 + 1) * 128],
                                        start=True, stop=True)
                    return f
                rdeps = [sacc[c2], ocopy[NCH - 2 + (c2 % 2)] if c2 < 2 else oadd[c2 - 2]]
                inter[c2] = plan.add_n("pe", mk_inter(), rdeps)
                def mk_oadd(h=h, c2=c2, pao=pao):
                    def f(v):
                        return v.tensor_tensor(of65[0:65, c2 * 128:(c2 + 1) * 128],
                                               of65[0:65, c2 * 128:(c2 + 1) * 128],
                                               po[0:65, pao:pao + 128], ALU.add)
                    return f
                oadd[c2] = plan.add_n("dve", mk_oadd(), [inter[c2], ocopy[c2]])
            def mk_nrow(h=h):
                def f(v):
                    return v.tensor_scalar(nrow[0:1, :], of65[64:65, :], eps_cols[0:1, 1:2],
                                           None, ALU.add)
                return f
            nr = plan.add_n("dve", mk_nrow(), [oadd[NCH - 1], war_nb])
            rb_refs = [None] * NG
            rcp_refs = [None] * NG
            for g in range(NG):
                def mk_nb(h=h, g=g):
                    def f(t):
                        return t.matmul(pnb[0:64, 0:G], ones_row[:, 0:64],
                                        nrow[0:1, g * G:(g + 1) * G], start=True, stop=True)
                    return f
                nb = plan.add_n("pe", mk_nb(),
                                [nr, rb_refs[g - 1] if g > 0 else war_rb3])
                def mk_rb(h=h, g=g):
                    def f(s):
                        return s.activation(rb_sb[:, (g % 2) * G:(g % 2) * G + G],
                                            pnb[0:64, 0:G], AF.Copy)
                    return f
                rb_refs[g] = plan.add_n("act", mk_rb(),
                                        [nb, rcp_refs[g - 2] if g >= 2 else war_rcp[g]])
                def mk_rcpon(h=h, g=g):
                    def f(v):
                        vsn(v, v.reciprocal(rb_sb[:, (g % 2) * G:(g % 2) * G + G],
                                            rb_sb[:, (g % 2) * G:(g % 2) * G + G]))
                        base = (h % 2) * 64
                        return v.tensor_tensor(
                            o_all[base:base + 64, (h // 2) * R + g * G:(h // 2) * R + (g + 1) * G],
                            of65[0:64, g * G:(g + 1) * G],
                            rb_sb[:, (g % 2) * G:(g % 2) * G + G], ALU.mult)
                    return f
                rcp_refs[g] = plan.add_n("dve", mk_rcpon(), [rb_refs[g]])
            war_nb = nb
            war_rb3 = rb_refs[NG - 1]
            war_rcp = [rcp_refs[2], rcp_refs[3]]
            last_pa = ocopy[NCH - 1]
            last_po = oadd[NCH - 1]
            prev_omm = inter[NCH - 1]
            prev_mmA = mmA[NCH - 1]
            prev_tc1 = tcs[1]
            prev_sacc_c = sacc[NCH - 1]
            prev_oadd_c = oadd[NCH - 1]
            prev_inter_c = inter[NCH - 1]
        plan.emit(nc, c, "c4p")
    mid.close()
    wosem = c.enter_context(nc.semaphore("c_wosem"))
    woe = c.enter_context(nc.semaphore("c_woe"))
    with ExitStack() as c5s:
        wo_sb = c5s.enter_context(nc.sbuf_tensor("c_wo2", [128, NIC * 1024], BF16))
        xg = c5s.enter_context(nc.sbuf_tensor("c_xg", [128, 8 * G], F32))
        osb = c5s.enter_context(nc.sbuf_tensor("c_osb", [128, 8 * G], F32))
        scr5 = c5s.enter_context(nc.sbuf_tensor("c_scr5", [128, 2 * G], F32))
        pwo = c5s.enter_context(nc.psum_tensor("c_pwo", [128, 2 * G], F32))
        with nc.Block() as blk:
            @blk.gpsimd
            def _(gg):
                GLD(gg, gg.dma_start(out=wo_sb[:, :].rearrange("p (ic f) -> p ic f", ic=NIC),
                                     in_=io["wout"].rearrange("(ic p) f -> p ic f", p=128)))
        won = [0, 0]
        for g in range(NG):
            with nc.Block() as blk:
                @blk.sync
                def _(sy):
                    LD(sy, sy.dma_start(out=xg[:, :].rearrange("p (ic t) -> p ic t", ic=8),
                                        in_=io["xin"][:, g * G:(g + 1) * G].rearrange(
                                            "(ic p) t -> p ic t", p=128)))
            with nc.Block() as blk:
                @blk.tensor
                def _(t):
                    for oc in range(NIC):
                        if oc >= 2:
                            t.wait_ge(woe, won[1] + oc - 1)
                        for kc in range(8):
                            ins = t.matmul(pwo[:, (oc % 2) * G:(oc % 2 + 1) * G],
                                           wo_sb[:, kc * 1024 + oc * 128: kc * 1024 + (oc + 1) * 128],
                                           o_all[:, kc * R + g * G:kc * R + (g + 1) * G],
                                           start=(kc == 0), stop=(kc == 7))
                        ins.then_inc(wosem, 1)
                @blk.scalar
                def _(s):
                    for oc in range(NIC):
                        s.wait_ge(wosem, won[0] + 2 * oc + 1)
                        if oc >= 2:
                            s.wait_ge(woe, won[1] + oc - 1)
                        s.activation(scr5[:, (oc % 2) * G:(oc % 2 + 1) * G],
                                     pwo[:, (oc % 2) * G:(oc % 2 + 1) * G],
                                     AF.Identity, bias=boutc[:, oc:oc + 1]).then_inc(wosem, 1)
                @blk.vector
                def _(v):
                    for oc in range(NIC):
                        v.wait_ge(wosem, won[0] + 2 * oc + 2)
                        v.tensor_tensor(osb[:, oc * G:(oc + 1) * G],
                                        scr5[:, (oc % 2) * G:(oc % 2 + 1) * G],
                                        xg[:, oc * G:(oc + 1) * G], ALU.add).then_inc(woe, 1)
            won[0] += 2 * NIC
            won[1] += NIC
            with nc.Block() as blk:
                @blk.sync
                def _(sy):
                    LD(sy, sy.dma_start(out=io["xout"][:, g * G:(g + 1) * G].rearrange(
                                            "(ic p) t -> p ic t", p=128),
                                        in_=osb[:, :].rearrange("p (ic t) -> p ic t", ic=8)))
    sb.close()


def build_C(_unused=None):
    nc = bass.Bass("TRN2", target_bir_lowering=False, detect_race_conditions=False)
    io = declare_consts(nc)
    io["xin"] = nc.dram_tensor("xin", [D, R], F32, kind="ExternalInput").ap()
    io["wqkv"] = nc.dram_tensor("wqkv", [D, 3072], F32, kind="ExternalInput").ap()
    io["wqkvT"] = nc.dram_tensor("wqkvT", [3072, D], F32, kind="ExternalInput").ap()
    io["proj"] = nc.dram_tensor("proj", [16, 64, 64], F32, kind="ExternalInput").ap()
    io["wout"] = nc.dram_tensor("wout", [D, D], F32, kind="ExternalInput").ap()
    io["bqkvC"] = nc.dram_tensor("bqkvC", [64, 48], F32, kind="ExternalInput").ap()
    io["boutC"] = nc.dram_tensor("boutC", [128, 8], F32, kind="ExternalInput").ap()
    io["maskin"] = nc.dram_tensor("maskin", [128, 8], F32, kind="ExternalInput").ap()
    io["xout"] = nc.dram_tensor("xout", [D, R], F32, kind="ExternalOutput").ap()
    io["cc_inC"] = nc.dram_tensor("cc_inC", [1024, 65], F32).ap()
    io["cc_outC"] = nc.dram_tensor("cc_outC", [8192, 65], F32).ap()
    with ExitStack() as c:
        consts = load_consts(nc, c, io)
        consts["vsn"] = VSync(c, nc, "c_vs")
        emit_blockC(nc, c, io, consts)
    return nc


def dC_host_inputs(inputs):
    m = {}
    m["wqkv"] = f32c(inputs["L1_attn_w_qkv"])
    m["wqkvT"] = f32c(np.asarray(inputs["L1_attn_w_qkv"]).T)
    m["proj"] = f32c(inputs["L1_attn_proj"])
    m["wout"] = f32c(inputs["L1_attn_w_out"])
    m["bqkvC"] = f32c(np.asarray(inputs["L1_attn_b_qkv"]).reshape(48, 64).T)
    m["boutC"] = f32c(np.asarray(inputs["L1_attn_b_out"]).reshape(8, 128).T)
    return m


def run_C(x2_fm_shards, inputs):
    nc = build_C()
    ca = const_arrays()
    dc = dC_host_inputs(inputs)
    mks = core_masks()
    in_maps = []
    for i in range(NCORES):
        m = dict(ca)
        m.update(dc)
        m["xin"] = f32c(x2_fm_shards[i])
        m["maskin"] = mks[i]
        in_maps.append(m)
    res = run_bass_kernel_spmd(nc, in_maps, core_ids=list(range(NCORES)))
    return [np.asarray(r["xout"], dtype=np.float32) for r in res.results]



def emit_ffn(nc, c, sems, name, kind, wins, src_sb, src_cols, src_goffs,
             noc, nic, dst_sb=None, dst_cols=None, dst_goffs=None,
             scol1=None, scol2=None, xin_ap=None, out_ap=None, out_rows0=0,
             tok_offs=None, x_sb=None, plan=None, src_ready=None,
             bufc=None, shared_psum=None):
    ng = len(src_goffs)
    if dst_goffs is None:
        dst_goffs = src_goffs
    nw = len(wins)
    own = plan is None
    if own:
        plan = Plan()
    local = ExitStack()
    cc = bufc if bufc is not None else local
    if True:
        wb_ = [[cc.enter_context(nc.sbuf_tensor(f"{name}_w{j}_{i}", [128, nic * 128], BF16))
                for i in range(2)] for j in range(nw)]
        if shared_psum is None:
            pg = cc.enter_context(nc.psum_tensor(name + "_pg", [128, 2 * ng * G], F32))
            pu = cc.enter_context(nc.psum_tensor(name + "_pu", [128, 2 * ng * G], F32)) if kind == 'silu2' else None
            bank_user = {0: None, 1: None}
        else:
            pg, pu, bank_user = shared_psum
        stmp = cc.enter_context(nc.sbuf_tensor(name + "_st", [128, 2 * ng * G], BF16)) if kind == 'silu2' else None
        if kind == 'w3add':
            xbuf_ = cc.enter_context(nc.sbuf_tensor(name + "_xb", [128, ng * G], F32))
            xbuf = [xbuf_, xbuf_]
            osb = [cc.enter_context(nc.sbuf_tensor(f"{name}_ob{i}", [128, ng * G], F32)) for i in range(2)]
        mm_r = [None] * noc
        ev_r = [None] * noc
        st_r = [None] * noc
        lx_r = [None] * noc
        for u in range(noc):
            def mk_ldw(u=u):
                def f(gg, sem):
                    for j in range(nw):
                        gg.dma_start(
                            out=wb_[j][u % 2][:, :].rearrange("p (ic f) -> p ic f", ic=nic),
                            in_=wins[j][:, u * 128:(u + 1) * 128].rearrange("(ic p) f -> p ic f", p=128)
                        ).then_inc(sem, 16)
                return f
            ldw = plan.add_n("gp", mk_ldw(), [mm_r[u - 2] if u >= 2 else None],
                             inc_total=16 * nw)
            def mk_mm(u=u):
                def f(t):
                    bs = (u % 2) * ng * G
                    for j in range(nw):
                        ptgt = pg if j == 0 else pu
                        for ic in range(nic):
                            for g in range(ng):
                                ins = t.matmul(ptgt[:, bs + g * G: bs + (g + 1) * G],
                                               wb_[j][u % 2][:, ic * 128:(ic + 1) * 128],
                                               src_sb[:, ic * src_cols + src_goffs[g]: ic * src_cols + src_goffs[g] + G],
                                               start=(ic == 0), stop=(ic == nic - 1))
                    return ins
                return f
            mdeps = [ldw, bank_user[u % 2]]
            if u == 0 and src_ready is not None:
                mdeps.append(src_ready)
            mm_r[u] = plan.add_n("pe", mk_mm(), mdeps)
            bs = (u % 2) * ng * G
            if kind == 'silu2':
                def mk_ea(u=u, bs=bs):
                    def f(s):
                        for g in range(ng):
                            kw = {} if scol1 is None else {"scale": scol1[:, u:u + 1]}
                            ins = s.activation(stmp[:, bs + g * G: bs + (g + 1) * G],
                                               pg[:, bs + g * G: bs + (g + 1) * G],
                                               AF.Silu, **kw)
                        return ins
                    return f
                ea = plan.add_n("act", mk_ea(), [mm_r[u], ev_r[u - 2] if u >= 2 else None])
                def mk_ed(u=u, bs=bs):
                    def f(v):
                        for g in range(ng):
                            sc = 1.0 if scol2 is None else scol2[:, u:u + 1]
                            ins = v.scalar_tensor_tensor(
                                dst_sb[:, u * dst_cols + dst_goffs[g]: u * dst_cols + dst_goffs[g] + G],
                                pu[:, bs + g * G: bs + (g + 1) * G], sc,
                                stmp[:, bs + g * G: bs + (g + 1) * G],
                                ALU.mult, ALU.mult)
                        return ins
                    return f
                ev_r[u] = plan.add_n("dve", mk_ed(), [ea])
                bank_user[u % 2] = ev_r[u]
            elif kind == 'gelu':
                def mk_ea(u=u, bs=bs):
                    def f(s):
                        for g in range(ng):
                            ins = s.activation(
                                dst_sb[:, u * dst_cols + dst_goffs[g]: u * dst_cols + dst_goffs[g] + G],
                                pg[:, bs + g * G: bs + (g + 1) * G],
                                AF.Gelu, scale=scol1[:, u:u + 1])
                        return ins
                    return f
                ev_r[u] = plan.add_n("act", mk_ea(), [mm_r[u]])
                bank_user[u % 2] = ev_r[u]
            elif kind == 'addx':
                def mk_ed(u=u, bs=bs):
                    def f(v):
                        for g in range(ng):
                            xsl = x_sb[:, u * R + dst_goffs[g]: u * R + dst_goffs[g] + G]
                            ins = v.tensor_tensor(xsl, xsl, pg[:, bs + g * G: bs + (g + 1) * G],
                                                  ALU.add)
                        return ins
                    return f
                ev_r[u] = plan.add_n("dve", mk_ed(), [mm_r[u]])
                bank_user[u % 2] = ev_r[u]
            elif kind == 'w3add':
                def mk_lx(u=u):
                    def f(sy, sem):
                        sy.dma_start(out=xbuf[u % 2][:, :],
                                     in_=xin_ap[u * 128:(u + 1) * 128,
                                                tok_offs:tok_offs + ng * G]).then_inc(sem, 16)
                    return f
                lx_r[u] = plan.add_n("sy", mk_lx(), [ev_r[u - 1] if u >= 1 else None],
                                     inc_total=16, chan="syl")
                def mk_ed(u=u, bs=bs):
                    def f(v):
                        for g in range(ng):
                            v.tensor_scalar(osb[u % 2][:, g * G:(g + 1) * G],
                                            pg[:, bs + g * G: bs + (g + 1) * G],
                                            scol1[:, u:u + 1], None, ALU.mult)
                            ins = v.tensor_tensor(osb[u % 2][:, g * G:(g + 1) * G],
                                                  osb[u % 2][:, g * G:(g + 1) * G],
                                                  xbuf[u % 2][:, g * G:(g + 1) * G], ALU.add)
                        return ins
                    return f
                ev_r[u] = plan.add_n("dve", mk_ed(), [mm_r[u], lx_r[u],
                                                      st_r[u - 2] if u >= 2 else None])
                def mk_st(u=u):
                    def f(sy, sem):
                        sy.dma_start(out=out_ap[u * 128:(u + 1) * 128,
                                                tok_offs:tok_offs + ng * G],
                                     in_=osb[u % 2][:, :]).then_inc(sem, 16)
                    return f
                st_r[u] = plan.add_n("sy", mk_st(), [ev_r[u]], inc_total=16, chan="sys")
                bank_user[u % 2] = ev_r[u]
        last = st_r[noc - 1] if kind == 'w3add' else ev_r[noc - 1]
        if own:
            plan.emit(nc, c, name)
    local.close()
    return last



def emit_blockA(nc, c, io, consts):
    sb = ExitStack()
    H, NCH = 8, 16
    ones_col, ones_row = consts["ones_col"], consts["ones_row"]
    eps_cols, ident_bf, triu = consts["eps_cols"], consts["ident_bf"], consts["triu"]
    vsn = consts["vsn"]

    rows_s1 = sb.enter_context(nc.sbuf_tensor("a_rs1", [1, R], F32))
    rows_c1 = sb.enter_context(nc.sbuf_tensor("a_rc1", [1, R], F32))
    gw = sb.enter_context(nc.sbuf_tensor("a_gw", [128, 17], F32))
    mcols = sb.enter_context(nc.sbuf_tensor("a_mcols", [128, 8], F32))
    ld = c.enter_context(nc.semaphore("a_ld"))
    ldn = [0]
    gld = c.enter_context(nc.semaphore("a_gld"))
    gldn = [0]

    def LD(eng, *dmas):
        for d_ in dmas:
            d_.then_inc(ld, 16)
            ldn[0] += 16
        eng.wait_ge(ld, ldn[0])

    def GLD(eng, *dmas):
        for d_ in dmas:
            d_.then_inc(gld, 16)
            gldn[0] += 16
        eng.wait_ge(gld, gldn[0])

    with nc.Block() as blk:
        @blk.sync
        def _(sy):
            LD(sy, sy.dma_start(out=gw[:, :], in_=io["gwA"]),
               sy.dma_start(out=mcols[:, :], in_=io["maskin"]))
    wcolA, wcolM, w128 = gw[:, 0:8], gw[:, 8:16], gw[:, 16:17]

    with ExitStack() as att:
        h_attn = att.enter_context(nc.sbuf_tensor("a_h", [128, NIC * R], BF16))
        with ExitStack() as lnx:
            x_sb = lnx.enter_context(nc.sbuf_tensor("a_x", [128, NIC * R], F32))
            gwrow = lnx.enter_context(nc.sbuf_tensor("a_gwrow", [1, D], F32))
            with nc.Block() as blk:
                @blk.sync
                def _(sy):
                    LD(sy, sy.dma_start(out=x_sb[:, :].rearrange("p (ic t) -> p ic t", ic=NIC),
                                        in_=io["xin"].rearrange("(ic p) t -> p ic t", p=128)),
                       sy.dma_start(out=gwrow[:, :], in_=io["wrowA"]))
            emit_ln_zn(nc, c, vsn, x_sb, h_attn, ones_col, ones_row, 'lnrms',
                       eps1_col=eps_cols[:, 0:1], eps2_col=eps_cols[:, 0:1], name="A",
                       gain_rowtile=gwrow, gain_coltile=wcolA,
                       rows_s1=rows_s1, rows_c1=rows_c1,
                       ones_col_bf=consts["ones_col_bf"])

        ktm = att.enter_context(nc.sbuf_tensor("a_ktm", [128, H * R], BF16))
        vtm = att.enter_context(nc.sbuf_tensor("a_vtm", [128, H * R], BF16))
        contrib = att.enter_context(nc.sbuf_tensor("a_contrib", [128, H * 128], F32))
        S_g = att.enter_context(nc.sbuf_tensor("a_sg", [128, H * 128], F32))
        with ExitStack() as a2c:
            wsl = a2c.enter_context(nc.sbuf_tensor("a_wsl", [128, 2 * NIC * 128], BF16))
            fm_scr = a2c.enter_context(nc.sbuf_tensor("a_fmscr", [128, R], BF16))
            pk = a2c.enter_context(nc.psum_tensor("a_pk", [128, NG * G], F32))
            ptp = a2c.enter_context(nc.psum_tensor("a_ptp", [128, NCH * 128], BF16))
            pst = a2c.enter_context(nc.psum_tensor("a_pst", [128, 128], F32))
            a2p = Plan()
            prev_vmm = prev_vevac = prev_ktr = prev_vtr = None
            prev_kcopy = prev_vcopy = prev_scon = None
            for h in range(H):
                def mk_ld(h=h):
                    def f(gg, sem):
                        gg.dma_start(
                            out=wsl[:, 0:NIC * 128].rearrange("p (ic f) -> p ic f", ic=NIC),
                            in_=io["wk"][:, h * 128:(h + 1) * 128].rearrange("(ic p) f -> p ic f", p=128)
                        ).then_inc(sem, 16)
                        gg.dma_start(
                            out=wsl[:, NIC * 128:].rearrange("p (ic f) -> p ic f", ic=NIC),
                            in_=io["wv"][:, h * 128:(h + 1) * 128].rearrange("(ic p) f -> p ic f", p=128)
                        ).then_inc(sem, 16)
                    return f
                ldr = a2p.add_n("gp", mk_ld(), [prev_vmm], inc_total=32)
                def mk_kvmm(h=h, kv=0):
                    def f(t):
                        for g in range(NG):
                            for ic in range(NIC):
                                ins = t.matmul(pk[:, g * G:(g + 1) * G],
                                               wsl[:, kv * NIC * 128 + ic * 128: kv * NIC * 128 + (ic + 1) * 128],
                                               h_attn[:, ic * R + g * G: ic * R + (g + 1) * G],
                                               start=(ic == 0), stop=(ic == NIC - 1))
                        return ins
                    return f
                kmm = a2p.add_n("pe", mk_kvmm(h, 0), [ldr, prev_vevac])
                def mk_evac(h=h):
                    def f(v):
                        for g in range(NG):
                            ins = v.tensor_copy(fm_scr[:, g * G:(g + 1) * G], pk[:, g * G:(g + 1) * G])
                        return ins
                    return f
                kevac = a2p.add_n("dve", mk_evac(), [kmm, prev_ktr])
                def mk_tr(h=h):
                    def f(t):
                        for c2 in range(NCH):
                            ins = t.transpose(ptp[:, c2 * 128:(c2 + 1) * 128],
                                              fm_scr[:, c2 * 128:(c2 + 1) * 128], ident_bf[:, :])
                        return ins
                    return f
                ktr = a2p.add_n("pe", mk_tr(), [kevac, prev_vcopy])
                def mk_copy(h=h, dstsel=0):
                    def f(v):
                        dst = ktm if dstsel == 0 else vtm
                        return v.tensor_copy(dst[:, h * R:(h + 1) * R], ptp[:, :])
                    return f
                kcopy = a2p.add_n("dve", mk_copy(h, 0), [ktr])
                vmm = a2p.add_n("pe", mk_kvmm(h, 1), [kevac])
                vevac = a2p.add_n("dve", mk_evac(), [vmm, ktr])
                vtr = a2p.add_n("pe", mk_tr(), [vevac, kcopy])
                vcopy = a2p.add_n("dve", mk_copy(h, 1), [vtr])
                def mk_smm(h=h):
                    def f(t):
                        for c2 in range(NCH):
                            ins = t.matmul(pst[:, :], ktm[:, h * R + c2 * 128: h * R + (c2 + 1) * 128],
                                           vtm[:, h * R + c2 * 128: h * R + (c2 + 1) * 128],
                                           start=(c2 == 0), stop=(c2 == NCH - 1))
                        return ins
                    return f
                smm = a2p.add_n("pe", mk_smm(), [kcopy, vcopy, prev_scon])
                def mk_scon(h=h):
                    def f(v):
                        return v.tensor_copy(contrib[:, h * 128:(h + 1) * 128], pst[:, :])
                    return f
                scon = a2p.add_n("dve", mk_scon(), [smm])
                prev_vmm, prev_vevac, prev_ktr, prev_vtr = vmm, vevac, ktr, vtr
                prev_kcopy, prev_vcopy, prev_scon = kcopy, vcopy, scon
            a2p.emit(nc, c, "a2p")

        ccs = c.enter_context(nc.semaphore("a_ccs"))
        with nc.Block() as blk:
            @blk.sync
            def _(sy):
                LD(sy, sy.dma_start(out=io["cc_in"].rearrange("(h p) d -> p h d", p=128),
                                    in_=contrib[:, :].rearrange("p (h d) -> p h d", h=H)))
        with nc.Block() as blk:
            @blk.gpsimd
            def _(gg):
                gg.collective_compute("AllGather", ALU.bypass,
                                      replica_groups=[list(range(8))],
                                      ins=[io["cc_in"].opt()],
                                      outs=[io["cc_out"].opt()]).then_inc(ccs, 1)
                gg.wait_ge(ccs, 1)
        with ExitStack() as gsc:
            gath = gsc.enter_context(nc.sbuf_tensor("a_gath", [128, 64 * 128], F32))
            with nc.Block() as blk:
                @blk.gpsimd
                def _(gg):
                    GLD(gg, gg.dma_start(out=gath[:, :].rearrange("p (q d) -> p q d", q=64),
                                         in_=io["cc_out"].rearrange("(q p) d -> p q d", p=128)))
            with nc.Block() as blk:
                @blk.vector
                def _(v):
                    v.memset(S_g[:, :], 0.0)
                    for r in range(8):
                        v.tensor_scalar(gath[:, r * 1024:(r + 1) * 1024],
                                        gath[:, r * 1024:(r + 1) * 1024],
                                        mcols[:, r:r + 1], None, ALU.mult)
                        v.tensor_tensor(S_g[:, :], S_g[:, :], gath[:, r * 1024:(r + 1) * 1024], ALU.add)

        pe_a = c.enter_context(nc.semaphore("a_pea"))
        pe_os = c.enter_context(nc.semaphore("a_peos"))
        act_oc = c.enter_context(nc.semaphore("a_actoc"))
        dve_am = c.enter_context(nc.semaphore("a_dveam"))
        dve_s = c.enter_context(nc.semaphore("a_dves"))
        with ExitStack() as a3c:
            q_bf = a3c.enter_context(nc.sbuf_tensor("a_q", [128, R], BF16))
            kfm = a3c.enter_context(nc.sbuf_tensor("a_kfm", [128, R], BF16))
            o_f32 = a3c.enter_context(nc.sbuf_tensor("a_of", [128, R], F32))
            o_nb = a3c.enter_context(nc.sbuf_tensor("a_onb", [128, R], BF16))
            wsl3 = a3c.enter_context(nc.sbuf_tensor("a_wsl3", [128, NIC * 128], BF16))
            S_sb = a3c.enter_context(nc.sbuf_tensor("a_ssb", [128, 128], F32))
            S_bf = a3c.enter_context(nc.sbuf_tensor("a_sbf", [128, NCH * 128], BF16))
            am_sb = a3c.enter_context(nc.sbuf_tensor("a_am", [128, 2 * 128], BF16))
            nr_m = a3c.enter_context(nc.sbuf_tensor("a_nrm", [1, R], F32))
            nr_s = a3c.enter_context(nc.sbuf_tensor("a_nrs", [1, R], F32))
            pa = a3c.enter_context(nc.psum_tensor("a_pa", [128, 2 * G], F32))
            po = a3c.enter_context(nc.psum_tensor("a_po", [128, 2 * G], F32))
            ps = a3c.enter_context(nc.psum_tensor("a_ps", [128, 2 * G], F32))
            ptf = a3c.enter_context(nc.psum_tensor("a_ptf", [128, 1024], BF16))
            sq_scr = a3c.enter_context(nc.sbuf_tensor("a_sqscr", [128, 2 * G], F32))
            rb_scr = a3c.enter_context(nc.sbuf_tensor("a_rbscr", [128, G], F32))
            pnq = a3c.enter_context(nc.psum_tensor("a_pnq", [128, G], F32))
            a3p = Plan()
            prev_qmm = prev_mask15 = prev_ocopy15 = prev_omm15 = None
            prev_supd15 = prev_ktc1 = prev_spill = prev_rbc3 = None
            prev_sacc15 = prev_oadd15 = prev_inter15 = None
            for h in range(H):
                def mk_ldq(h=h):
                    def f(gg, sem):
                        gg.dma_start(
                            out=wsl3[:, :].rearrange("p (ic f) -> p ic f", ic=NIC),
                            in_=io["wq"][:, h * 128:(h + 1) * 128].rearrange("(ic p) f -> p ic f", p=128)
                        ).then_inc(sem, 16)
                    return f
                ldq = a3p.add_n("gp", mk_ldq(), [prev_qmm], inc_total=16)
                qev = None
                for gp in range(2):
                    def mk_qmm(h=h, gp=gp):
                        def f(t):
                            for g2 in range(2):
                                g = gp * 2 + g2
                                ptgt = pa if g2 == 0 else po
                                for ic in range(NIC):
                                    ins = t.matmul(ptgt[:, 0:G],
                                                   wsl3[:, ic * 128:(ic + 1) * 128],
                                                   h_attn[:, ic * R + g * G: ic * R + (g + 1) * G],
                                                   start=(ic == 0), stop=(ic == NIC - 1))
                            return ins
                        return f
                    deps = [ldq] if gp == 0 else [qev]
                    if gp == 0:
                        deps += [prev_mask15, prev_ocopy15, prev_rbc3]
                    qmm = a3p.add_n("pe", mk_qmm(), deps)
                    def mk_qev(h=h, gp=gp):
                        def f(v):
                            for g2 in range(2):
                                g = gp * 2 + g2
                                ptgt = pa if g2 == 0 else po
                                ins = v.tensor_copy(q_bf[:, g * G:(g + 1) * G], ptgt[:, 0:G])
                            return ins
                        return f
                    qev = a3p.add_n("dve", mk_qev(), [qmm])
                prev_qmm = qmm
                ktcs = []
                tprev = prev_ktc1
                for ph in range(2):
                    def mk_ktf(h=h, ph=ph):
                        def f(t):
                            for c2 in range(8):
                                cc2 = ph * 8 + c2
                                ins = t.transpose(ptf[:, c2 * 128:(c2 + 1) * 128],
                                                  ktm[:, h * R + cc2 * 128: h * R + (cc2 + 1) * 128],
                                                  ident_bf[:, :])
                            return ins
                        return f
                    ktf = a3p.add_n("pe", mk_ktf(), [tprev])
                    def mk_ktc(h=h, ph=ph):
                        def f(v):
                            return v.tensor_copy(kfm[:, ph * 1024:(ph + 1) * 1024], ptf[:, :])
                        return f
                    tprev = a3p.add_n("dve", mk_ktc(), [ktf])
                    ktcs.append(tprev)
                prev_ktc1 = ktcs[1]
                def mk_init(h=h):
                    def f(v):
                        return v.tensor_copy(S_sb[:, :], S_g[:, h * 128:(h + 1) * 128])
                    return f
                init = a3p.add_n("dve", mk_init(), [prev_omm15])
                mmA = [None] * NCH
                mask = [None] * NCH
                intra = [None] * NCH
                smm = [None] * NCH
                ocopy = [None] * NCH
                sacc = [None] * NCH
                inter = [None] * NCH
                oadd = [None] * NCH
                for c2 in range(NCH):
                    pao = (c2 % 2) * G
                    def mk_mmA(h=h, c2=c2, pao=pao):
                        def f(t):
                            return t.matmul(pa[:, pao:pao + 128],
                                            kfm[:, c2 * 128:(c2 + 1) * 128],
                                            q_bf[:, c2 * 128:(c2 + 1) * 128],
                                            start=True, stop=True)
                        return f
                    deps = [qev, ktcs[c2 // 8]]
                    if c2 >= 2:
                        deps.append(mask[c2 - 2])
                    mmA[c2] = a3p.add_n("pe", mk_mmA(), deps)
                    def mk_mask(h=h, c2=c2, pao=pao):
                        def f(v):
                            return v.tensor_tensor(am_sb[:, (c2 % 2) * 128:(c2 % 2) * 128 + 128],
                                                   pa[:, pao:pao + 128], triu[:, :], ALU.mult)
                        return f
                    mask[c2] = a3p.add_n("dve", mk_mask(),
                                         [mmA[c2], intra[c2 - 2] if c2 >= 2 else prev_omm15])
                    def mk_intra(h=h, c2=c2, pao=pao):
                        def f(t):
                            return t.matmul(po[:, pao:pao + 128],
                                            vtm[:, h * R + c2 * 128: h * R + (c2 + 1) * 128],
                                            am_sb[:, (c2 % 2) * 128:(c2 % 2) * 128 + 128],
                                            start=True, stop=True)
                        return f
                    ideps = [mask[c2]]
                    if c2 >= 2:
                        ideps.append(ocopy[c2 - 2])
                    else:
                        ideps += [qev, prev_oadd15]
                    intra[c2] = a3p.add_n("pe", mk_intra(), ideps)
                    def mk_smm(h=h, c2=c2, pao=pao):
                        def f(t):
                            return t.matmul(ps[:, pao:pao + 128],
                                            ktm[:, h * R + c2 * 128: h * R + (c2 + 1) * 128],
                                            vtm[:, h * R + c2 * 128: h * R + (c2 + 1) * 128],
                                            start=True, stop=True)
                        return f
                    sdeps = [sacc[c2 - 2] if c2 >= 2 else prev_sacc15]
                    smm[c2] = a3p.add_n("pe", mk_smm(), sdeps)
                    def mk_ocopy(h=h, c2=c2, pao=pao):
                        def f(s):
                            return s.activation(o_f32[:, c2 * 128:(c2 + 1) * 128],
                                                po[:, pao:pao + 128], AF.Copy)
                        return f
                    ocopy[c2] = a3p.add_n("act", mk_ocopy(), [intra[c2]])
                    def mk_sacc(h=h, c2=c2, pao=pao):
                        def f(v):
                            v.tensor_copy(S_bf[:, c2 * 128:(c2 + 1) * 128], S_sb[:, :])
                            return v.tensor_tensor(S_sb[:, :], S_sb[:, :],
                                                   ps[:, pao:pao + 128], ALU.add)
                        return f
                    sadeps = [smm[c2], init if c2 == 0 else None, prev_inter15 if c2 <= 1 else None]
                    sacc[c2] = a3p.add_n("dve", mk_sacc(), sadeps)
                for c2 in range(NCH):
                    pao = (c2 % 2) * G
                    def mk_inter(h=h, c2=c2, pao=pao):
                        def f(t):
                            return t.matmul(po[:, pao:pao + 128],
                                            S_bf[:, c2 * 128:(c2 + 1) * 128],
                                            q_bf[:, c2 * 128:(c2 + 1) * 128],
                                            start=True, stop=True)
                        return f
                    rdeps = [sacc[c2], ocopy[NCH - 2 + (c2 % 2)] if c2 < 2 else oadd[c2 - 2]]
                    inter[c2] = a3p.add_n("pe", mk_inter(), rdeps)
                    def mk_oadd(h=h, c2=c2, pao=pao):
                        def f(v):
                            return v.tensor_tensor(o_f32[:, c2 * 128:(c2 + 1) * 128],
                                                   o_f32[:, c2 * 128:(c2 + 1) * 128],
                                                   po[:, pao:pao + 128], ALU.add)
                        return f
                    oadd[c2] = a3p.add_n("dve", mk_oadd(), [inter[c2], ocopy[c2]])
                prev_mask15 = mask[NCH - 1]
                prev_omm15 = inter[NCH - 1]
                prev_sacc15 = sacc[NCH - 1]
                prev_oadd15 = oadd[NCH - 1]
                prev_inter15 = inter[NCH - 1]
                prev_ocopy15 = oadd[NCH - 1]
                sqs_ = [None] * NG
                summ_ = [None] * NG
                rowm_ = [None] * NG
                for g in range(NG):
                    def mk_sq(h=h, g=g):
                        def f(s):
                            return s.activation(sq_scr[:, (g % 2) * G:(g % 2) * G + G],
                                                o_f32[:, g * G:(g + 1) * G], AF.Square)
                        return f
                    sdeps = [oadd[NCH - 1]]
                    if g >= 2:
                        sdeps.append(summ_[g - 2])
                    sqs_[g] = a3p.add_n("act", mk_sq(), sdeps)
                    def mk_summ(h=h, g=g):
                        def f(t):
                            return t.matmul(pnq[0:1, 0:G], ones_col[:, :],
                                            sq_scr[:, (g % 2) * G:(g % 2) * G + G],
                                            start=True, stop=True)
                        return f
                    smdeps = [sqs_[g]]
                    if g >= 1:
                        smdeps.append(rowm_[g - 1])
                    summ_[g] = a3p.add_n("pe", mk_summ(), smdeps)
                    def mk_rowm(h=h, g=g):
                        def f(v):
                            return v.tensor_scalar(nr_m[0:1, g * G:(g + 1) * G], pnq[0:1, 0:G],
                                                   1.0 / 128.0, None, ALU.mult)
                        return f
                    rowm_[g] = a3p.add_n("dve", mk_rowm(), [summ_[g]])
                def mk_sqr(h=h):
                    def f(s):
                        return s.activation(nr_s[0:1, :], nr_m[0:1, :], AF.Sqrt,
                                            bias=eps_cols[0:1, 0:1])
                    return f
                sqr = a3p.add_n("act", mk_sqr(), [rowm_[NG - 1]])
                def mk_rcp(h=h):
                    def f(v):
                        return v.reciprocal(nr_s[0:1, :], nr_s[0:1, :])
                    return f
                rcp = a3p.add_n("dve", mk_rcp(), [sqr])
                rbc = None
                onl = None
                for g in range(NG):
                    def mk_bcast(h=h, g=g):
                        def f(t):
                            return t.matmul(pa[:, 0:G], ones_row[:, :],
                                            nr_s[0:1, g * G:(g + 1) * G], start=True, stop=True)
                        return f
                    bdeps = [rcp, mask[NCH - 1]]
                    if rbc is not None:
                        bdeps.append(rbc)
                    bc = a3p.add_n("pe", mk_bcast(), bdeps)
                    def mk_rbc(h=h, g=g):
                        def f(s):
                            return s.activation(rb_scr[:, :], pa[:, 0:G], AF.Copy)
                        return f
                    rbc = a3p.add_n("act", mk_rbc(), [bc])
                    def mk_on(h=h, g=g):
                        def f(v):
                            v.tensor_tensor(sq_scr[:, 0:G], o_f32[:, g * G:(g + 1) * G],
                                            rb_scr[:, :], ALU.mult)
                            return v.tensor_scalar(o_nb[:, g * G:(g + 1) * G], sq_scr[:, 0:G],
                                                   w128[:, 0:1], None, ALU.mult)
                        return f
                    onl = a3p.add_n("dve", mk_on(), [rbc, prev_spill])
                def mk_spill(h=h):
                    def f(sy, sem):
                        sy.dma_start(out=io["o_dram"][h * 128:(h + 1) * 128, :],
                                     in_=o_nb[:, :]).then_inc(sem, 16)
                    return f
                prev_spill = a3p.add_n("sy", mk_spill(), [onl], inc_total=16)
                prev_rbc3 = rbc
            a3p.emit(nc, c, "a3p")

    u_sb = sb.enter_context(nc.sbuf_tensor("a_u", [128, NIC * R], F32))
    a4f = c.enter_context(nc.semaphore("a_a4f"))
    a4e = c.enter_context(nc.semaphore("a_a4e"))
    with ExitStack() as a4c:
        wo_sb = a4c.enter_context(nc.sbuf_tensor("a_wo", [128, NIC * 1024], BF16))
        og = a4c.enter_context(nc.sbuf_tensor("a_og", [128, 8 * G], BF16))
        xg = a4c.enter_context(nc.sbuf_tensor("a_xg", [128, 8 * G], F32))
        scr4 = a4c.enter_context(nc.sbuf_tensor("a_scr4", [128, G], F32))
        pR = a4c.enter_context(nc.psum_tensor("a_pR", [128, G], F32))
        pC = a4c.enter_context(nc.psum_tensor("a_pC", [128, G], F32))
        pwo = a4c.enter_context(nc.psum_tensor("a_pwo", [128, 2 * G], F32))
        with nc.Block() as blk:
            @blk.gpsimd
            def _(gg):
                GLD(gg, gg.dma_start(out=wo_sb[:, :].rearrange("p (ic f) -> p ic f", ic=NIC),
                                     in_=io["wo"].rearrange("(ic p) f -> p ic f", p=128)))
        a4p = Plan()
        dve_last = None
        bank_user = {0: None, 1: None}
        for g in range(NG):
            def mk_ldog(g=g):
                def f(sy, sem):
                    sy.dma_start(out=og[:, :].rearrange("p (kc t) -> p kc t", kc=8),
                                 in_=io["o_dram"][:, g * G:(g + 1) * G].rearrange(
                                     "(kc p) t -> p kc t", p=128)).then_inc(sem, 16)
                    sy.dma_start(out=xg[:, :].rearrange("p (ic t) -> p ic t", ic=8),
                                 in_=io["xin"][:, g * G:(g + 1) * G].rearrange(
                                     "(ic p) t -> p ic t", p=128)).then_inc(sem, 16)
                return f
            ldog = a4p.add_n("sy", mk_ldog(), [dve_last], inc_total=32)
            def mk_bc(g=g):
                def f(t):
                    t.matmul(pR[:, :], ones_row[:, :], rows_s1[0:1, g * G:(g + 1) * G],
                             start=True, stop=True)
                    return t.matmul(pC[:, :], ones_row[:, :], rows_c1[0:1, g * G:(g + 1) * G],
                                    start=True, stop=True)
                return f
            bc = a4p.add_n("pe", mk_bc(), [dve_last])
            for oc in range(NIC):
                def mk_wmm(g=g, oc=oc):
                    def f(t):
                        for kc in range(8):
                            ins = t.matmul(pwo[:, (oc % 2) * G:(oc % 2 + 1) * G],
                                           wo_sb[:, kc * 1024 + oc * 128: kc * 1024 + (oc + 1) * 128],
                                           og[:, kc * G:(kc + 1) * G],
                                           start=(kc == 0), stop=(kc == 7))
                        return ins
                    return f
                wdeps = [ldog, bank_user[oc % 2]]
                wmm = a4p.add_n("pe", mk_wmm(), wdeps)
                def mk_udve(g=g, oc=oc):
                    def f(v):
                        v.tensor_tensor(scr4[:, :], xg[:, oc * G:(oc + 1) * G], pR[:, :], ALU.mult)
                        v.tensor_tensor(scr4[:, :], scr4[:, :], pC[:, :], ALU.add)
                        return v.tensor_tensor(u_sb[:, oc * R + g * G: oc * R + (g + 1) * G],
                                               scr4[:, :], pwo[:, (oc % 2) * G:(oc % 2 + 1) * G],
                                               ALU.add)
                    return f
                bank_user[oc % 2] = a4p.add_n("dve", mk_udve(), [wmm, bc])
            dve_last = bank_user[(NIC - 1) % 2]
        a4p.emit(nc, c, "a4p")
    with ExitStack() as a5c:
        h2 = sb.enter_context(nc.sbuf_tensor("a_h2", [128, NIC * R], BF16))
        emit_ln_zn(nc, c, vsn, u_sb, h2, ones_col, ones_row, 'rms',
                   eps1_col=eps_cols[:, 0:1], nic=NIC, xcols=R, width=1024, name="AU",
                   rms_gain=True, gain_coltile=wcolM, ones_col_bf=consts["ones_col_bf"])
        xg2 = a5c.enter_context(nc.sbuf_tensor("a_xg2", [128, 2 * R], F32))
        a5p = Plan()
        addr = {0: None, 1: None}
        for oc in range(NIC):
            def mk_ldx(oc=oc):
                def f(sy, sem):
                    sy.dma_start(out=xg2[:, (oc % 2) * R:(oc % 2 + 1) * R],
                                 in_=io["xin"][oc * 128:(oc + 1) * 128, :]).then_inc(sem, 16)
                return f
            ldx = a5p.add_n("sy", mk_ldx(), [addr[oc % 2]], inc_total=16)
            def mk_addx(oc=oc):
                def f(v):
                    for g in range(NG):
                        sl = slice(oc * R + g * G, oc * R + (g + 1) * G)
                        ins = v.tensor_tensor(u_sb[:, sl], u_sb[:, sl],
                                              xg2[:, (oc % 2) * R + g * G:(oc % 2) * R + (g + 1) * G],
                                              ALU.add)
                    return ins
                return f
            addr[oc % 2] = a5p.add_n("dve", mk_addx(), [ldx])
        a5p.emit(nc, c, "a5p")

    sems = None
    for half in range(2):
        toff = half * (R // 2)
        goffs = [toff, toff + G]
        hgoffs = [0, G]
        with ExitStack() as hc:
            m_sb = hc.enter_context(nc.sbuf_tensor(f"a_m{half}", [128, 32 * (R // 2)], BF16))
            aplan = Plan()
            pgS = hc.enter_context(nc.psum_tensor(f"a_pgS{half}", [128, 4 * G], F32))
            puS = hc.enter_context(nc.psum_tensor(f"a_puS{half}", [128, 4 * G], F32))
            spsum = (pgS, puS, {0: None, 1: None})
            r1 = emit_ffn(nc, c, sems, f"am12h{half}", 'silu2', (io["wgate"], io["wup"]),
                          h2, R, goffs, 32, NIC, dst_sb=m_sb, dst_cols=R // 2,
                          dst_goffs=hgoffs, plan=aplan, bufc=hc, shared_psum=spsum)
            emit_ffn(nc, c, sems, f"amdnh{half}", 'addx', (io["wdown"],),
                     m_sb, R // 2, hgoffs, NIC, 32, x_sb=u_sb, dst_goffs=goffs,
                     plan=aplan, bufc=hc, shared_psum=spsum, src_ready=r1)
            aplan.emit(nc, c, f"aph{half}")

    with nc.Block() as blk:
        @blk.sync
        def _(sy):
            LD(sy, sy.dma_start(out=io["xout"].rearrange("(ic p) t -> p ic t", p=128),
                                in_=u_sb[:, :].rearrange("p (ic t) -> p ic t", ic=NIC)))
    sb.close()


def build_A(maskin):
    nc = bass.Bass("TRN2", target_bir_lowering=False, detect_race_conditions=False)
    io = declare_consts(nc)
    io["xin"] = nc.dram_tensor("xin", [D, R], F32, kind="ExternalInput").ap()
    io["wq"] = nc.dram_tensor("wq", [D, D], F32, kind="ExternalInput").ap()
    io["wk"] = nc.dram_tensor("wk", [D, D], F32, kind="ExternalInput").ap()
    io["wv"] = nc.dram_tensor("wv", [D, D], F32, kind="ExternalInput").ap()
    io["wo"] = nc.dram_tensor("wo", [D, D], F32, kind="ExternalInput").ap()
    io["wgate"] = nc.dram_tensor("wgate", [D, 4096], F32, kind="ExternalInput").ap()
    io["wup"] = nc.dram_tensor("wup", [D, 4096], F32, kind="ExternalInput").ap()
    io["wdown"] = nc.dram_tensor("wdown", [4096, D], F32, kind="ExternalInput").ap()
    io["gwA"] = nc.dram_tensor("gwA", [128, 17], F32, kind="ExternalInput").ap()
    io["wrowA"] = nc.dram_tensor("wrowA", [1, D], F32, kind="ExternalInput").ap()
    io["maskin"] = nc.dram_tensor("maskin", [128, 8], F32, kind="ExternalInput").ap()
    io["xout"] = nc.dram_tensor("xout", [D, R], F32, kind="ExternalOutput").ap()
    io["cc_in"] = nc.dram_tensor("cc_in", [1024, 128], F32).ap()
    io["cc_out"] = nc.dram_tensor("cc_out", [8192, 128], F32).ap()
    io["o_dram"] = nc.dram_tensor("o_dram", [1024, R], BF16).ap()
    with ExitStack() as c:
        consts = load_consts(nc, c, io)
        consts["vsn"] = VSync(c, nc, "a_vs")
        emit_blockA(nc, c, io, consts)
    return nc


def dA_host_inputs(inputs):
    m = {}
    m["wq"] = f32c(inputs["L0_attn_Wq"])
    m["wk"] = f32c(inputs["L0_attn_Wk"])
    m["wv"] = f32c(inputs["L0_attn_Wv"])
    m["wo"] = f32c(inputs["L0_attn_Wo"])
    m["wgate"] = f32c(inputs["L0_attn_Wgate"])
    m["wup"] = f32c(inputs["L0_attn_Wup"])
    m["wdown"] = f32c(inputs["L0_attn_Wdown"])
    gw = np.zeros((128, 17), np.float32)
    gw[:, 0:8] = f32c(inputs["L0_attn_attn_norm_w"]).reshape(8, 128).T
    gw[:, 8:16] = f32c(inputs["L0_attn_mlp_norm_w"]).reshape(8, 128).T
    gw[:, 16] = f32c(inputs["L0_attn_attn_out_norm_w"])
    m["gwA"] = gw
    m["wrowA"] = f32c(inputs["L0_attn_attn_norm_w"]).reshape(1, D)
    return m


def core_masks():
    outs = []
    for i in range(NCORES):
        mk = np.zeros((128, 8), np.float32)
        mk[:, :i] = 1.0
        outs.append(mk)
    return outs


def run_A(x0_fm_shards, inputs):
    nc = build_A(None)
    ca = const_arrays()
    da = dA_host_inputs(inputs)
    mks = core_masks()
    in_maps = []
    for i in range(NCORES):
        m = dict(ca)
        m.update(da)
        m["xin"] = f32c(x0_fm_shards[i])
        m["maskin"] = mks[i]
        in_maps.append(m)
    res = run_bass_kernel_spmd(nc, in_maps, core_ids=list(range(NCORES)))
    return [np.asarray(r["xout"], dtype=np.float32) for r in res.results]



def make_ffn_sems(nc, c, tag):
    sems = {}
    for nm in ("w", "f", "a", "e", "x", "s"):
        sems[nm] = c.enter_context(nc.semaphore(f"{tag}_{nm}"))
    return sems


def emit_blockD(nc, c, io, x_sb_ctx, consts):
    sb = ExitStack()
    ones_col, ones_row, eps_cols = consts["ones_col"], consts["ones_row"], consts["eps_cols"]
    vsn = consts["vsn"]
    NHID = 4096
    NOC = NHID // 128
    sems = None
    scol = sb.enter_context(nc.sbuf_tensor("d_scol", [128, 4 * NOC + 8], F32))
    zn = sb.enter_context(nc.sbuf_tensor("d_zn", [128, NIC * R], BF16))
    ld = c.enter_context(nc.semaphore("d_ld"))
    with nc.Block() as blk:
        @blk.sync
        def _(sy):
            sy.dma_start(out=scol[:, :], in_=io["scolD"]).then_inc(ld, 16)
            sy.wait_ge(ld, 16)
    s1c, s2c = scol[:, 0:NOC], scol[:, NOC:2 * NOC]
    hs1c, hs2c = scol[:, 2 * NOC:3 * NOC], scol[:, 3 * NOC:4 * NOC]
    s3c = scol[:, 4 * NOC:4 * NOC + 8]

    with ExitStack() as lnc:
        if x_sb_ctx is None:
            x_sb = lnc.enter_context(nc.sbuf_tensor("d_x", [128, NIC * R], F32))
            with nc.Block() as blk:
                @blk.sync
                def _(sy):
                    sy.dma_start(out=x_sb[:, :].rearrange("p (ic t) -> p ic t", ic=NIC),
                                 in_=io["xin"].rearrange("(ic p) t -> p ic t", p=128)).then_inc(ld, 16)
                    sy.wait_ge(ld, 32)
        else:
            x_sb = x_sb_ctx
        emit_ln_zn(nc, c, vsn, x_sb, zn, ones_col, ones_row, 'ln',
                   eps1_col=eps_cols[:, 0:1], name="D", ones_col_bf=consts["ones_col_bf"])

    for half in range(2):
        toff = half * (R // 2)
        goffs = [toff, toff + G]
        hgoffs = [0, G]
        with ExitStack() as hc:
            g1_sb = hc.enter_context(nc.sbuf_tensor(f"d_g1{half}", [128, NOC * (R // 2)], BF16))
            hplan = Plan()
            pgS = hc.enter_context(nc.psum_tensor(f"d_pgS{half}", [128, 4 * G], F32))
            puS = hc.enter_context(nc.psum_tensor(f"d_puS{half}", [128, 4 * G], F32))
            spsum = (pgS, puS, {0: None, 1: None})
            with ExitStack() as hi:
                h_sb = hi.enter_context(nc.sbuf_tensor(f"d_h{half}", [128, NOC * (R // 2)], BF16))
                r1 = emit_ffn(nc, c, sems, f"d12h{half}", 'silu2', (io["w1T"], io["w2T"]),
                              zn, R, goffs, NOC, NIC, dst_sb=h_sb, dst_cols=R // 2,
                              dst_goffs=hgoffs, scol1=s1c, scol2=s2c,
                              plan=hplan, bufc=hi, shared_psum=spsum)
                r2 = emit_ffn(nc, c, sems, f"d3h{half}", 'gelu', (io["h1T"],),
                              h_sb, R // 2, hgoffs, NOC, NOC, dst_sb=g1_sb, dst_cols=R // 2,
                              dst_goffs=hgoffs, scol1=hs1c,
                              plan=hplan, bufc=hi, shared_psum=spsum, src_ready=r1)
                hplan.emit(nc, c, f"dph{half}")
            g2_sb = hc.enter_context(nc.sbuf_tensor(f"d_g2{half}", [128, NOC * (R // 2)], BF16))
            hplan2 = Plan()
            spsum2 = (pgS, puS, {0: None, 1: None})
            r3 = emit_ffn(nc, c, sems, f"d4h{half}", 'gelu', (io["h2T"],),
                          g1_sb, R // 2, hgoffs, NOC, NOC, dst_sb=g2_sb, dst_cols=R // 2,
                          dst_goffs=hgoffs, scol1=hs2c,
                          plan=hplan2, bufc=hc, shared_psum=spsum2)
            r4 = emit_ffn(nc, c, sems, f"d5h{half}", 'w3add', (io["w3T"],),
                          g2_sb, R // 2, hgoffs, NIC, NOC, scol1=s3c,
                          xin_ap=io["xin"], out_ap=io["xout"], tok_offs=toff,
                          plan=hplan2, bufc=hc, shared_psum=spsum2, src_ready=r3)
            hplan2.emit(nc, c, f"dqh{half}")
    sb.close()


def build_D():
    nc = bass.Bass("TRN2", target_bir_lowering=False, detect_race_conditions=False)
    io = declare_consts(nc)
    io["xin"] = nc.dram_tensor("xin", [D, R], F32, kind="ExternalInput").ap()
    io["w1T"] = nc.dram_tensor("w1T", [D, 4096], F32, kind="ExternalInput").ap()
    io["w2T"] = nc.dram_tensor("w2T", [D, 4096], F32, kind="ExternalInput").ap()
    io["h1T"] = nc.dram_tensor("h1T", [4096, 4096], F32, kind="ExternalInput").ap()
    io["h2T"] = nc.dram_tensor("h2T", [4096, 4096], F32, kind="ExternalInput").ap()
    io["w3T"] = nc.dram_tensor("w3T", [4096, D], F32, kind="ExternalInput").ap()
    io["scolD"] = nc.dram_tensor("scolD", [128, 4 * 32 + 8], F32, kind="ExternalInput").ap()
    io["xout"] = nc.dram_tensor("xout", [D, R], F32, kind="ExternalOutput").ap()
    with ExitStack() as c:
        consts = load_consts(nc, c, io)
        consts["vsn"] = VSync(c, nc, "d_vs")
        emit_blockD(nc, c, io, None, consts)
    return nc


def dD_host_inputs(inputs):
    m = {}
    m["w1T"] = f32c(np.asarray(inputs["L1_ffn_w1_w"]).T)
    m["w2T"] = f32c(np.asarray(inputs["L1_ffn_w2_w"]).T)
    m["h1T"] = f32c(np.asarray(inputs["L1_ffn_h1_w"]).T)
    m["h2T"] = f32c(np.asarray(inputs["L1_ffn_h2_w"]).T)
    m["w3T"] = f32c(np.asarray(inputs["L1_ffn_w3_w"]).T)
    sc = np.zeros((128, 4 * 32 + 8), np.float32)
    sc[:, 0:32] = np.asarray(inputs["L1_ffn_w1_s"]).reshape(32, 128).T
    sc[:, 32:64] = np.asarray(inputs["L1_ffn_w2_s"]).reshape(32, 128).T
    sc[:, 64:96] = np.asarray(inputs["L1_ffn_h1_s"]).reshape(32, 128).T
    sc[:, 96:128] = np.asarray(inputs["L1_ffn_h2_s"]).reshape(32, 128).T
    sc[:, 128:136] = np.asarray(inputs["L1_ffn_w3_s"]).reshape(8, 128).T
    m["scolD"] = sc
    return m


def run_D(x3_fm_shards, inputs):
    nc = build_D()
    ca = const_arrays()
    dm = dD_host_inputs(inputs)
    in_maps = []
    for i in range(NCORES):
        m = dict(ca)
        m.update(dm)
        m["xin"] = f32c(x3_fm_shards[i])
        in_maps.append(m)
    res = run_bass_kernel_spmd(nc, in_maps, core_ids=list(range(NCORES)))
    return [np.asarray(r["xout"], dtype=np.float32) for r in res.results]



def build_B(stage="full"):
    nc = bass.Bass("TRN2", target_bir_lowering=False, detect_race_conditions=False)
    io = declare_consts(nc)
    io["xin"] = nc.dram_tensor("xin", [D, R], F32, kind="ExternalInput").ap()
    io["wr"] = nc.dram_tensor("wr", [D, 8], F32, kind="ExternalInput").ap()
    io["wg"] = nc.dram_tensor("wg", [8, D, 2048], F32, kind="ExternalInput").ap()
    io["wu"] = nc.dram_tensor("wu", [8, D, 2048], F32, kind="ExternalInput").ap()
    io["wd"] = nc.dram_tensor("wd", [8, 2048, D], F32, kind="ExternalInput").ap()
    io["xout"] = nc.dram_tensor("xout", [D, R], F32, kind="ExternalOutput").ap()
    with ExitStack() as c:
        x_sb = c.enter_context(nc.sbuf_tensor("x_sb", [128, NIC * R], F32))
        zn_sb = c.enter_context(nc.sbuf_tensor("zn_sb", [128, NIC * R], BF16))
        xld = c.enter_context(nc.semaphore("xld"))
        with nc.Block() as blk:
            @blk.sync
            def _(sy):
                sy.dma_start(out=x_sb[:, :].rearrange("p (ic t) -> p ic t", ic=NIC),
                             in_=io["xin"].rearrange("(ic p) t -> p ic t", p=128)).then_inc(xld, 16)
                sy.wait_ge(xld, 16)
        cst = load_consts(nc, c, io)
        cst["vsn"] = VSync(c, nc, "b_vs0")
        emit_blockB(nc, c, io, x_sb, zn_sb, cst["ones_col"], cst["ones_row"],
                    cst["ident"], cst["eps_cols"], stage=stage,
                    ones_col_bf=cst["ones_col_bf"])
        xst = c.enter_context(nc.semaphore("xst"))
        with nc.Block() as blk:
            @blk.sync
            def _(sy):
                sy.dma_start(out=io["xout"].rearrange("(ic p) t -> p ic t", p=128),
                             in_=x_sb[:, :].rearrange("p (ic t) -> p ic t", ic=NIC)).then_inc(xst, 16)
                sy.wait_ge(xst, 16)
    return nc


def run_B(x1_fm_shards, inputs, stage="full"):
    nc = build_B(stage)
    ca = const_arrays()
    in_maps = []
    for i in range(NCORES):
        m = dict(ca)
        m["xin"] = f32c(x1_fm_shards[i])
        m["wr"] = f32c(inputs["L0_ffn_router_kernel_DE"])
        m["wg"] = f32c(inputs["L0_ffn_kernel_gating_EDF"])
        m["wu"] = f32c(inputs["L0_ffn_kernel_up_EDF"])
        m["wd"] = f32c(inputs["L0_ffn_kernel_down_EFD"])
        in_maps.append(m)
    res = run_bass_kernel_spmd(nc, in_maps, core_ids=list(range(NCORES)))
    return [np.asarray(r["xout"], dtype=np.float32) for r in res.results]


def build_fused():
    nc = bass.Bass("TRN2", target_bir_lowering=False, detect_race_conditions=False)
    io = declare_consts(nc)
    io["xin"] = nc.dram_tensor("xin", [D, R], F32, kind="ExternalInput").ap()
    io["wq"] = nc.dram_tensor("wq", [D, D], F32, kind="ExternalInput").ap()
    io["wk"] = nc.dram_tensor("wk", [D, D], F32, kind="ExternalInput").ap()
    io["wv"] = nc.dram_tensor("wv", [D, D], F32, kind="ExternalInput").ap()
    io["wo"] = nc.dram_tensor("wo", [D, D], F32, kind="ExternalInput").ap()
    io["wgate"] = nc.dram_tensor("wgate", [D, 4096], F32, kind="ExternalInput").ap()
    io["wup"] = nc.dram_tensor("wup", [D, 4096], F32, kind="ExternalInput").ap()
    io["wdown"] = nc.dram_tensor("wdown", [4096, D], F32, kind="ExternalInput").ap()
    io["gwA"] = nc.dram_tensor("gwA", [128, 17], F32, kind="ExternalInput").ap()
    io["wrowA"] = nc.dram_tensor("wrowA", [1, D], F32, kind="ExternalInput").ap()
    io["maskin"] = nc.dram_tensor("maskin", [128, 8], F32, kind="ExternalInput").ap()
    io["wr"] = nc.dram_tensor("wr", [D, 8], F32, kind="ExternalInput").ap()
    io["wg"] = nc.dram_tensor("wg", [8, D, 2048], F32, kind="ExternalInput").ap()
    io["wu"] = nc.dram_tensor("wu", [8, D, 2048], F32, kind="ExternalInput").ap()
    io["wd"] = nc.dram_tensor("wd", [8, 2048, D], F32, kind="ExternalInput").ap()
    io["wqkv"] = nc.dram_tensor("wqkv", [D, 3072], F32, kind="ExternalInput").ap()
    io["wqkvT"] = nc.dram_tensor("wqkvT", [3072, D], F32, kind="ExternalInput").ap()
    io["proj"] = nc.dram_tensor("proj", [16, 64, 64], F32, kind="ExternalInput").ap()
    io["wout"] = nc.dram_tensor("wout", [D, D], F32, kind="ExternalInput").ap()
    io["bqkvC"] = nc.dram_tensor("bqkvC", [64, 48], F32, kind="ExternalInput").ap()
    io["boutC"] = nc.dram_tensor("boutC", [128, 8], F32, kind="ExternalInput").ap()
    io["w1T"] = nc.dram_tensor("w1T", [D, 4096], F32, kind="ExternalInput").ap()
    io["w2T"] = nc.dram_tensor("w2T", [D, 4096], F32, kind="ExternalInput").ap()
    io["h1T"] = nc.dram_tensor("h1T", [4096, 4096], F32, kind="ExternalInput").ap()
    io["h2T"] = nc.dram_tensor("h2T", [4096, 4096], F32, kind="ExternalInput").ap()
    io["w3T"] = nc.dram_tensor("w3T", [4096, D], F32, kind="ExternalInput").ap()
    io["scolD"] = nc.dram_tensor("scolD", [128, 4 * 32 + 8], F32, kind="ExternalInput").ap()
    io["xout"] = nc.dram_tensor("xout", [D, R], F32, kind="ExternalOutput").ap()
    io["cc_in"] = nc.dram_tensor("cc_in", [1024, 128], F32).ap()
    io["cc_out"] = nc.dram_tensor("cc_out", [8192, 128], F32).ap()
    io["o_dram"] = nc.dram_tensor("o_dram", [1024, R], BF16).ap()
    io["cc_inC"] = nc.dram_tensor("cc_inC", [1024, 65], F32).ap()
    io["cc_outC"] = nc.dram_tensor("cc_outC", [8192, 65], F32).ap()
    x1s = nc.dram_tensor("x1s", [D, R], F32).ap()
    x2s = nc.dram_tensor("x2s", [D, R], F32).ap()
    x3s = nc.dram_tensor("x3s", [D, R], F32).ap()

    with ExitStack() as c:
        consts = load_consts(nc, c, io)
        consts["vsn"] = VSync(c, nc, "f_vs")
        ioA = dict(io); ioA["xout"] = x1s
        emit_blockA(nc, c, ioA, consts)
        ioB = dict(io); ioB["xin"] = x1s; ioB["xout"] = x2s
        with ExitStack() as bscope:
            x_sb = bscope.enter_context(nc.sbuf_tensor("f_bx", [128, NIC * R], F32))
            zn_sb = bscope.enter_context(nc.sbuf_tensor("f_bzn", [128, NIC * R], BF16))
            bld = c.enter_context(nc.semaphore("f_bld"))
            with nc.Block() as blk:
                @blk.sync
                def _(sy):
                    sy.dma_start(out=x_sb[:, :].rearrange("p (ic t) -> p ic t", ic=NIC),
                                 in_=x1s.rearrange("(ic p) t -> p ic t", p=128)).then_inc(bld, 16)
                    sy.wait_ge(bld, 16)
            emit_blockB(nc, c, ioB, x_sb, zn_sb, consts["ones_col"], consts["ones_row"],
                        consts["ident"], consts["eps_cols"],
                        ones_col_bf=consts["ones_col_bf"])
            with nc.Block() as blk:
                @blk.sync
                def _(sy):
                    sy.dma_start(out=x2s.rearrange("(ic p) t -> p ic t", p=128),
                                 in_=x_sb[:, :].rearrange("p (ic t) -> p ic t", ic=NIC)).then_inc(bld, 16)
                    sy.wait_ge(bld, 32)
        ioC = dict(io); ioC["xin"] = x2s; ioC["xout"] = x3s
        emit_blockC(nc, c, ioC, consts)
        ioD = dict(io); ioD["xin"] = x3s
        emit_blockD(nc, c, ioD, None, consts)
    return nc


_FUSED_NC = None


def kernel_fused(**inputs):
    global _FUSED_NC
    if _FUSED_NC is None:
        _FUSED_NC = build_fused()
    nc = _FUSED_NC
    x0 = np.asarray(inputs["L0_attn_hidden_states"], dtype=np.float32)[0]
    ca = const_arrays()
    da = dA_host_inputs(inputs)
    dc = dC_host_inputs(inputs)
    dd = dD_host_inputs(inputs)
    mks = core_masks()
    base = dict(ca)
    base.update(da); base.update(dc); base.update(dd)
    base["wr"] = f32c(inputs["L0_ffn_router_kernel_DE"])
    base["wg"] = f32c(inputs["L0_ffn_kernel_gating_EDF"])
    base["wu"] = f32c(inputs["L0_ffn_kernel_up_EDF"])
    base["wd"] = f32c(inputs["L0_ffn_kernel_down_EFD"])
    in_maps = []
    for i in range(NCORES):
        m = dict(base)
        m["xin"] = np.ascontiguousarray(x0[i * R:(i + 1) * R, :].T)
        m["maskin"] = mks[i]
        in_maps.append(m)
    res = run_bass_kernel_spmd(nc, in_maps, core_ids=list(range(NCORES)))
    out = np.concatenate([np.asarray(r["xout"], np.float32).T for r in res.results], axis=0)
    return np.ascontiguousarray(out[None, :, :], dtype=np.float32)


def kernel_unfused(**inputs):
    x0 = np.asarray(inputs["L0_attn_hidden_states"], dtype=np.float32)[0]
    shards = [np.ascontiguousarray(x0[i * R:(i + 1) * R, :].T) for i in range(NCORES)]
    shards = run_A(shards, inputs)
    shards = run_B(shards, inputs)
    shards = run_C(shards, inputs)
    shards = run_D(shards, inputs)
    out = np.concatenate([s.T for s in shards], axis=0)[None, :, :]
    return np.ascontiguousarray(out, dtype=np.float32)


def kernel(**inputs):
    return kernel_fused(**inputs)
